# version 08
# baseline (speedup 1.0000x reference)
import numpy as np
import ml_dtypes
from contextlib import ExitStack

import concourse.bass as bass
import concourse.mybir as mybir
from concourse.bass_utils import run_bass_kernel_spmd

F32 = mybir.dt.float32
BF16 = mybir.dt.bfloat16
I32 = mybir.dt.int32
AF = mybir.ActivationFunctionType
ALU = mybir.AluOpType
AX = mybir.AxisListType

B, T, C = 8, 2048, 1024
H, HD, R = 16, 64, 32
FH = 3072
NT = T // 128
ND = C // 128
NH = FH // 128
S8 = 1.0 / 8.0
FSC = 1.0 / 127.0
QA = 32767.0
EPS = 1e-5
DEV = {}


class Rec:

    class Tok:
        __slots__ = ("incs",)

        def __init__(self):
            self.incs = []

        def then_inc(self, sem, n=1):
            self.incs.append((sem, n))
            return self

    def __init__(self):
        self.q = {k: [] for k in ("sync", "gpsimd", "vector", "scalar", "tensor")}

    def eng(self, name):
        rec = self

        class E:
            def __getattr__(self, op):
                def call(*a, **k):
                    tok = Rec.Tok()
                    rec.q[name].append((op, a, k, tok))
                    return tok
                return call
        return E()

    def replay(self, block):
        def mk(name):
            def run(e):
                for op, a, k, tok in self.q[name]:
                    inst = getattr(e, op)(*a, **k)
                    for sem, n in tok.incs:
                        inst.then_inc(sem, n)
            return run
        block.sync(mk("sync"))
        block.gpsimd(mk("gpsimd"))
        block.vector(mk("vector"))
        block.scalar(mk("scalar"))
        block.tensor(mk("tensor"))


class Sem:
    def __init__(self, E, nc, name):
        self.h = E(nc.semaphore(name))
        self.v = 0

    def bump(self, n=1):
        self.v += n
        return self.v


def tiled(x):
    return x.rearrange("(tt p) d -> p tt d", p=128)


def chunked(w):
    return w.rearrange("(c p) e -> p c e", p=128)


def host_constants():
    c = {}
    c["ones_row"] = np.ones((1, 128), dtype=ml_dtypes.bfloat16)
    c["ones_f32"] = np.ones((1, 128), dtype=np.float32)
    kk = np.arange(128)[:, None]
    qq = np.arange(128)[None, :]
    c["diagmask"] = (kk <= qq).astype(ml_dtypes.bfloat16)
    c["idn"] = np.eye(128, dtype=ml_dtypes.bfloat16)
    c["madd"] = np.where(kk > qq, -1e30, 0.0).astype(ml_dtypes.bfloat16)
    half = HD // 2
    theta = 1.0 / (10000.0 ** (np.arange(0, HD, 2, dtype=np.float64) / HD))
    pos = np.arange(1, T + 1, dtype=np.float64)
    ang = pos[:, None] * theta[None, :]
    ang = np.concatenate([ang, ang], axis=1)
    sgn = np.where(np.arange(HD) < half, -1.0, 1.0)
    c["ropecos"] = np.tile(np.cos(ang), (1, H)).astype(ml_dtypes.bfloat16)
    c["ropesin"] = np.tile(np.sin(ang) * sgn[None, :], (1, H)).astype(
        ml_dtypes.bfloat16)
    return c


class _VProxy:

    def __init__(self, cx, raw):
        self._cx, self._raw = cx, raw
        self._written = set()
        self._read = set()

    @staticmethod
    def _names(objs):
        out = set()
        for o in objs:
            t = getattr(o, "tensor", None)
            n = getattr(t, "name", None)
            if n is not None:
                out.add(n)
        return out

    def __getattr__(self, op):
        raw = self._raw

        def call(*a, **k):
            if op not in ("wait_ge", "engine_nop"):
                names = self._names(list(a) + list(k.values()))
                outs = []
                if a and getattr(a[0], "tensor", None) is not None:
                    outs.append(a[0])
                if "accum_out" in k and k["accum_out"] is not None:
                    outs.append(k["accum_out"])
                onames = self._names(outs)
                if (names & self._written) or (onames & self._read):
                    cx = self._cx
                    raw.engine_nop().then_inc(cx.sVE.h, 1)
                    cx.sVE.bump()
                    raw.wait_ge(cx.sVE.h, cx.sVE.v)
                    self._written.clear()
                    self._read.clear()
                self._written |= onames
                self._read |= names - onames
            return getattr(raw, op)(*a, **k)
        return call


class Cx:
    def __init__(self, nc, E):
        self.nc = nc
        self.rec = Rec()
        r = self.rec
        self.SY, self.GP, self.SC, self.TE = (
            r.eng("sync"), r.eng("gpsimd"), r.eng("scalar"), r.eng("tensor"))
        for nm in ("sLD", "sGD", "sXP", "sTE", "sVE", "sSC", "sST", "sRS"):
            setattr(self, nm, Sem(E, nc, nm))
        self.VE = _VProxy(self, r.eng("vector"))

    def w(self, eng, sem, val=None):
        eng.wait_ge(sem.h, sem.v if val is None else val)

    def gTE(self):
        self.w(self.TE, self.sSC)
        self.w(self.TE, self.sVE)

    def vfence(self):
        self.VE.engine_nop().then_inc(self.sVE.h, 1)
        self.sVE.bump()
        self.VE.wait_ge(self.sVE.h, self.sVE.v)


def emit_ln_tile(cx, P, pin, resid_ap, out_ap, resid_scale=None,
                 out_wait=None):
    VE, SC = cx.VE, cx.SC
    col, lnA, lnsq = P["col"], P["lnA"], P["lnsq"]
    cx.w(VE, cx.sTE)
    if resid_scale is None:
        VE.tensor_add(lnA[:, 0:C], pin, resid_ap)
    else:
        VE.scalar_tensor_tensor(lnA[:, 0:C], pin, resid_scale, resid_ap,
                                ALU.mult, ALU.add)
    VE.reduce_sum(col[:, 0:1], lnA[:, 0:C], axis=AX.X)
    VE.tensor_scalar_mul(col[:, 0:1], col[:, 0:1], 1.0 / C)
    VE.engine_nop().then_inc(cx.sVE.h, 1); cx.sVE.bump()
    cx.w(SC, cx.sVE)
    SC.activation(lnsq[:, 0:C], lnA[:, 0:C], AF.Square,
                  accum_out=col[:, 1:2]).then_inc(cx.sSC.h, 1); cx.sSC.bump()
    cx.w(VE, cx.sSC)
    VE.tensor_scalar(col[:, 2:3], col[:, 1:2], 1.0 / C, None, ALU.mult)
    VE.tensor_tensor(col[:, 3:4], col[:, 0:1], col[:, 0:1], ALU.mult)
    VE.tensor_sub(col[:, 2:3], col[:, 2:3], col[:, 3:4])
    VE.tensor_scalar_add(col[:, 2:3], col[:, 2:3], EPS)
    VE.reciprocal(col[:, 4:5], col[:, 2:3]).then_inc(cx.sVE.h, 1); cx.sVE.bump()
    cx.w(SC, cx.sVE)
    SC.activation(col[:, 5:6], col[:, 4:5], AF.Sqrt).then_inc(cx.sSC.h, 1)
    cx.sSC.bump()
    cx.w(VE, cx.sSC)
    VE.tensor_tensor(col[:, 6:7], col[:, 5:6], col[:, 5:6], ALU.mult)
    VE.tensor_tensor(col[:, 6:7], col[:, 6:7], col[:, 2:3], ALU.mult)
    VE.tensor_scalar(col[:, 6:7], col[:, 6:7], -0.5, 1.5, ALU.mult, ALU.add)
    VE.tensor_tensor(col[:, 5:6], col[:, 5:6], col[:, 6:7], ALU.mult)
    VE.tensor_scalar(col[:, 7:8], col[:, 0:1], -1.0, None, ALU.mult)
    VE.tensor_tensor(col[:, 7:8], col[:, 7:8], col[:, 5:6], ALU.mult)
    VE.engine_nop().then_inc(cx.sVE.h, 1); cx.sVE.bump()
    cx.w(SC, cx.sVE)
    for (sm, vv) in (out_wait or ()):
        cx.w(SC, sm, vv)
    SC.activation(out_ap, lnA[:, 0:C], AF.Identity,
                  bias=col[:, 7:8], scale=col[:, 5:6]).then_inc(cx.sSC.h, 1)
    cx.sSC.bump()


def emit_xT_tile(cx, P, src_ap, tt):
    buf = P["xbfr"][:, 0, :]
    cx.w(cx.VE, cx.sXP)
    cx.VE.tensor_copy(buf, src_ap).then_inc(cx.sVE.h, 1); cx.sVE.bump()
    cx.w(cx.SY, cx.sVE)
    cx.w(cx.SY, cx.sGD)
    for d in range(ND):
        cx.SY.dma_start_transpose(
            P["xT"][:, d, 128 * tt:128 * (tt + 1)],
            buf[:, 128 * d:128 * (d + 1)]).then_inc(cx.sXP.h, 16)
        cx.sXP.bump(16)


def emit_colmax_scalar(cx, P, bhbm, col_ap, out11):
    SY, VE = cx.SY, cx.VE
    cx.w(SY, cx.sVE)
    SY.dma_start(out=bhbm, in_=col_ap).then_inc(cx.sXP.h, 16)
    cx.sXP.bump(16)
    cx.w(SY, cx.sXP)
    SY.dma_start(out=P["grow"][0:1, :],
                 in_=bhbm.rearrange("a b -> b a")).then_inc(cx.sXP.h, 16)
    cx.sXP.bump(16)
    cx.w(VE, cx.sXP)
    VE.reduce_max(out11, P["grow"][0:1, :], axis=AX.X).then_inc(cx.sVE.h, 1)
    cx.sVE.bump()


def emit_bcast_cols(cx, P, src_row_ap, dst_cols_ap, n):
    cx.gTE()
    t = cx.TE.matmul(P["pA"][:, 0:n], P["onesf"][0:1, :], src_row_ap,
                     start=True, stop=True)
    t.then_inc(cx.sTE.h, 1); cx.sTE.bump()
    cx.w(cx.VE, cx.sTE)
    cx.VE.tensor_copy(dst_cols_ap, P["pA"][:, 0:n]).then_inc(cx.sVE.h, 1)
    cx.sVE.bump()


def alloc_perm(cx, E, with_xT=True):
    st, pt = cx.nc.sbuf_tensor, cx.nc.psum_tensor
    P = {}
    if with_xT:
        P["xT"] = E(st("xT", [128, ND, T], BF16))
    P["xbfr"] = E(st("xbfr", [128, 1, C], BF16))
    P["xres"] = E(st("xres", [128, 2, C], F32))
    P["xln"] = E(st("xln", [128, 2, C], F32))
    P["onesr"] = E(st("onesr", [1, 128], BF16))
    P["onesf"] = E(st("onesf", [1, 128], F32))
    P["onescol"] = E(st("onescol", [128, 1], BF16))
    P["col"] = E(st("col", [128, 16], F32))
    P["lnA"] = E(st("lnA", [128, C], F32))
    P["lnsq"] = E(st("lnsq", [128, C], BF16))
    P["runmax"] = E(st("runmax", [128, 4], F32))
    P["grow"] = E(st("grow", [1, 128], F32))
    P["srow"] = E(st("srow", [1, 64], F32))
    P["pA"] = E(pt("pA", [128, T], F32))
    P["pB"] = E(pt("pB", [128, T], F32))
    return P


def load_base_consts(cx, P, inp):
    SY, VE = cx.SY, cx.VE
    SY.dma_start(out=P["onesr"][:], in_=inp["ones_row"]).then_inc(cx.sLD.h, 16)
    cx.sLD.bump(16)
    SY.dma_start(out=P["onesf"][:], in_=inp["ones_f32"]).then_inc(cx.sLD.h, 16)
    cx.sLD.bump(16)
    cx.w(VE, cx.sLD)
    VE.memset(P["onescol"][:], 1.0)
    VE.memset(P["runmax"][:], 0.0)
    VE.engine_nop().then_inc(cx.sVE.h, 1); cx.sVE.bump()


def build_p1(stop_after=None):
    nc = bass.Bass("TRN2", target_bir_lowering=False)
    dt = nc.dram_tensor
    inp = {}

    def I(name, shape, dtyp=F32):
        inp[name] = dt(name, shape, dtyp, kind="ExternalInput").ap()
        return inp[name]

    x0 = I("x0", [T, C])
    cattn_w = I("cattn_w", [C, 3 * C])
    cattn_bp = I("cattn_bp", [128, 24])
    cproj_w = I("cproj_w", [C, C])
    cproj_b = I("cproj_b", [1, C])
    wk_c = I("wk_c", [HD, R]); wk_eT = I("wk_eT", [HD, R])
    wv_c = I("wv_c", [HD, R]); wv_eT = I("wv_eT", [HD, R])
    wg0T = I("wg0T", [C, FH], I32)
    wu0T = I("wu0T", [C, FH], I32)
    wd0T = I("wd0T", [FH, C], I32)
    wqT = I("wqT", [C, C]); wkT = I("wkT", [C, C])
    wvT = I("wvT", [C, C]); woT = I("woT", [C, C])
    bqr = I("bq", [1, C]); bkr = I("bk", [1, C])
    bvr = I("bv", [1, C]); bor = I("bo", [1, C])
    I("ones_row", [1, 128], BF16)
    I("ones_f32", [1, 128])
    diagmask = I("diagmask", [128, 128], BF16)
    idn_h = I("idn", [128, 128], BF16)
    madd_h = I("madd", [128, 128], BF16)
    ropecos = I("ropecos", [T, C], BF16)
    ropesin = I("ropesin", [T, C], BF16)

    x1_out = dt("x1_out", [T, C], F32, kind="ExternalOutput").ap()
    m1_out = dt("m1_out", [1, 1], F32, kind="ExternalOutput").ap()
    xspillA = dt("xspillA", [T, C], F32).ap()
    xspillB = dt("xspillB", [T, C], F32).ap()
    bhbm = dt("bhbm", [128, 1], F32).ap()

    with ExitStack() as perm:
        E = perm.enter_context
        st = nc.sbuf_tensor
        cx = Cx(nc, E)
        SY, GP, VE, SC, TE = cx.SY, cx.GP, cx.VE, cx.SC, cx.TE
        P = alloc_perm(cx, E)
        pA, pB = P["pA"], P["pB"]
        onesr, onesf, onescol = P["onesr"], P["onesf"], P["onescol"]
        srow = P["srow"]
        dmask = E(st("dmask", [128, 128], BF16))
        idn = E(st("idn_s", [128, 128], BF16))
        madd = E(st("madd_s", [128, 128], BF16))
        negsB = E(st("negsB", [128, H], F32))

        SY.dma_start(out=dmask[:], in_=diagmask).then_inc(cx.sLD.h, 16)
        cx.sLD.bump(16)
        SY.dma_start(out=idn[:], in_=idn_h).then_inc(cx.sLD.h, 16)
        cx.sLD.bump(16)
        SY.dma_start(out=madd[:], in_=madd_h).then_inc(cx.sLD.h, 16)
        cx.sLD.bump(16)
        load_base_consts(cx, P, inp)

        for tt in range(NT):
            bb = tt % 2
            cx.w(SY, cx.sVE)
            SY.dma_start(out=P["xres"][:, bb, :],
                         in_=tiled(x0)[:, tt, :]).then_inc(cx.sLD.h, 16)
            cx.sLD.bump(16)
            cx.w(VE, cx.sLD)
            emit_xT_tile(cx, P, P["xres"][:, bb, :], tt)
        xT0_xp = cx.sXP.v

        with ExitStack() as ph:
            Ep = ph.enter_context
            WAr = Ep(st("WAr", [128, 2, 3, ND, 128], BF16))
            WP = Ep(st("WP", [128, ND, C], BF16))
            bias24 = Ep(st("bias24", [128, 24], F32))
            w64 = Ep(st("w64", [128, 4 * R], BF16))
            browp = Ep(st("browp", [1, C], BF16))
            qT2 = Ep(st("qT2", [128, T], BF16))
            kT2 = Ep(st("kT2", [128, T], BF16))
            vT2 = Ep(st("vT2", [128, T], BF16))
            qpT = Ep(st("qpT", [128, 4, T], BF16))
            klT = Ep(st("klT", [128, 4, T], BF16))
            atT = Ep(st("atT", [128, 4, T], BF16))
            atst = Ep(st("atst", [32, T], BF16))
            vones = Ep(st("vones", [128, H, NT, R + 1], BF16))
            sqs = Ep(st("sqs", [128, T], BF16))
            probsT = Ep(st("probsT", [128, T], BF16))
            Weff = Ep(st("Weff", [128, 4, C], BF16))
            reciprow = Ep(st("reciprow", [1, T], F32))
            recipbf = Ep(st("recipbf", [1, T], BF16))

            GP.dma_start(out=WP[:], in_=chunked(cproj_w)).then_inc(cx.sGD.h, 16)
            cx.sGD.bump(16)
            for half64 in range(2):
                for i, wsrc in enumerate((wk_c, wk_eT, wv_c, wv_eT)):
                    GP.dma_start(out=w64[64 * half64:64 * (half64 + 1),
                                         R * i:R * (i + 1)],
                                 in_=wsrc).then_inc(cx.sGD.h, 16)
                    cx.sGD.bump(16)
            GP.dma_start(out=browp[:], in_=cproj_b).then_inc(cx.sGD.h, 16)
            cx.sGD.bump(16)
            SY.dma_start(out=bias24[:], in_=cattn_bp).then_inc(cx.sLD.h, 16)
            cx.sLD.bump(16)
            a_ld, a_gd = cx.sLD.v, cx.sGD.v
            VE.memset(vones[:, :, :, R:R + 1], 1.0)
            VE.engine_nop().then_inc(cx.sVE.h, 1); cx.sVE.bump()

            te_qkv = {}
            for hp in range(DEV.get("n_hp", 8)):
                heads = (2 * hp, 2 * hp + 1)
                bbw = hp % 2
                if hp >= 2:
                    GP.wait_ge(cx.sTE.h, te_qkv[hp - 2])
                for which, ech in enumerate((hp, 8 + hp, 16 + hp)):
                    GP.dma_start(
                        out=WAr[:, bbw, which, :, :],
                        in_=cattn_w[:, 128 * ech:128 * (ech + 1)].rearrange(
                            "(c p) e -> p c e", p=128)).then_inc(cx.sGD.h, 16)
                    cx.sGD.bump(16)
                war_gd = cx.sGD.v
                for which, (dst, ech) in enumerate(
                        [(qT2, hp), (kT2, 8 + hp), (vT2, 16 + hp)]):
                    p = pA if which % 2 == 0 else pB
                    cx.gTE()
                    cx.w(TE, cx.sGD, war_gd)
                    if hp == 0 and which == 0:
                        cx.w(TE, cx.sXP, xT0_xp)
                        cx.w(TE, cx.sLD, a_ld)
                    for b in range(4):
                        for d in range(ND):
                            t = TE.matmul(
                                p[:, 512 * b:512 * (b + 1)],
                                WAr[:, bbw, which, d, :],
                                P["xT"][:, d, 512 * b:512 * (b + 1)],
                                start=(d == 0), stop=(d == ND - 1))
                    t.then_inc(cx.sTE.h, 1); cx.sTE.bump()
                    if which == 2:
                        te_qkv[hp] = cx.sTE.v
                    cx.w(SC, cx.sTE)
                    SC.activation(dst[:], p[:], AF.Identity,
                                  bias=bias24[:, ech:ech + 1]).then_inc(
                        cx.sSC.h, 1); cx.sSC.bump()

                for hi, h in enumerate(heads):
                    if "qk" in DEV.get("skip", ()):
                        break
                    rows = 64 * (h % 2)
                    g, cc = h % 4, h // 4
                    gb = 32 * g
                    for which, (srcT, wof, dstT) in enumerate(
                            [(qT2, R, qpT), (kT2, 0, klT)]):
                        p = pA if which == 0 else pB
                        cx.gTE()
                        for b in range(4):
                            t = TE.matmul(
                                p[gb:gb + 32, 512 * b:512 * (b + 1)],
                                w64[rows:rows + 64, wof:wof + R],
                                srcT[rows:rows + 64, 512 * b:512 * (b + 1)],
                                start=True, stop=True,
                                tile_position=(rows, gb))
                        t.then_inc(cx.sTE.h, 1); cx.sTE.bump()
                        cx.w(SC, cx.sTE)
                        SC.activation(dstT[gb:gb + 32, cc, :],
                                      p[gb:gb + 32, :],
                                      AF.Identity).then_inc(cx.sSC.h, 1)
                        cx.sSC.bump()
                        SC.activation(sqs[gb:gb + 32, :], p[gb:gb + 32, :],
                                      AF.Square).then_inc(cx.sSC.h, 1)
                        cx.sSC.bump()
                        cx.gTE()
                        for b in range(4):
                            t = TE.matmul(
                                p[0:1, 512 * b:512 * (b + 1)],
                                onescol[gb:gb + 32, 0:1],
                                sqs[gb:gb + 32, 512 * b:512 * (b + 1)],
                                start=True, stop=True,
                                tile_position=(gb, 0))
                        t.then_inc(cx.sTE.h, 1); cx.sTE.bump()
                        cx.w(VE, cx.sTE)
                        idx = 2 * hi + which
                        VE.reduce_max(srow[0:1, idx:idx + 1], p[0:1, 0:T],
                                      axis=AX.X).then_inc(cx.sVE.h, 1)
                        cx.sVE.bump()

                cx.w(VE, cx.sVE)
                VE.tensor_mul(
                    srow[0:1, 4:6],
                    srow[0:1, 0:4].rearrange("a (h k) -> a h k", k=2)[:, :, 0],
                    srow[0:1, 0:4].rearrange("a (h k) -> a h k", k=2)[:, :, 1])
                VE.engine_nop().then_inc(cx.sVE.h, 1); cx.sVE.bump()
                cx.w(SC, cx.sVE)
                SC.activation(srow[0:1, 6:8], srow[0:1, 4:6],
                              AF.Sqrt).then_inc(cx.sSC.h, 1); cx.sSC.bump()
                cx.w(VE, cx.sSC)
                VE.tensor_scalar_mul(srow[0:1, 8:10], srow[0:1, 6:8], -S8)
                VE.engine_nop().then_inc(cx.sVE.h, 1); cx.sVE.bump()
                emit_bcast_cols(cx, P, srow[0:1, 8:10],
                                negsB[:, 2 * hp:2 * hp + 2], 2)

                for hi, h in enumerate(heads):
                    if "vlat" in DEV.get("skip", ()):
                        break
                    rows = 64 * (h % 2)
                    cx.gTE()
                    for tt in range(NT):
                        t = TE.matmul(
                            pA[:, 32 * tt:32 * (tt + 1)],
                            vT2[rows:rows + 64, 128 * tt:128 * (tt + 1)],
                            w64[rows:rows + 64, 2 * R:3 * R],
                            start=True, stop=True,
                            tile_position=(rows, 0))
                    t.then_inc(cx.sTE.h, 1); cx.sTE.bump()
                    cx.w(VE, cx.sTE)
                    VE.tensor_copy(
                        vones[:, h, :, 0:R],
                        pA[:, 0:512].rearrange("p (tt r) -> p tt r", r=R),
                    ).then_inc(cx.sVE.h, 1); cx.sVE.bump()

                for hi, h in enumerate(heads):
                    if "scores" in DEV.get("skip", ()):
                        break
                    g, cc = h % 4, h // 4
                    gb = 32 * g
                    for kt in range(NT):
                        q0 = 128 * kt
                        b0 = q0 // 512
                        cx.gTE()
                        segv = {}
                        for b in range(b0, 4):
                            lo, hi2 = max(512 * b, q0), 512 * (b + 1)
                            t = TE.matmul(pA[:, lo:hi2],
                                          klT[gb:gb + 32, cc, q0:q0 + 128],
                                          qpT[gb:gb + 32, cc, lo:hi2],
                                          start=True, stop=True,
                                          tile_position=(gb, 0))
                            if b == b0:
                                t = TE.matmul(pA[:, q0:q0 + 128], idn[:, :],
                                              madd[:, :], start=False,
                                              stop=True,
                                              skip_group_check=True)
                            t.then_inc(cx.sTE.h, 1); cx.sTE.bump()
                            segv[b] = cx.sTE.v
                        expv = {}
                        for b in range(b0, 4):
                            lo, hi2 = max(512 * b, q0), 512 * (b + 1)
                            SC.wait_ge(cx.sTE.h, segv[b])
                            SC.activation(probsT[:, lo:hi2], pA[:, lo:hi2],
                                          AF.Exp, bias=negsB[:, h:h + 1],
                                          scale=S8).then_inc(cx.sSC.h, 1)
                            cx.sSC.bump()
                            expv[b] = cx.sSC.v
                        for b in range(b0, 4):
                            lo, hi2 = max(512 * b, q0), 512 * (b + 1)
                            TE.wait_ge(cx.sSC.h, expv[b])
                            t = TE.matmul(
                                pB[0:R + 1, lo:hi2], vones[:, h, kt, :],
                                probsT[:, lo:hi2],
                                start=(kt == 0),
                                stop=(kt == min(4 * b + 3, NT - 1)),
                                skip_group_check=True)
                            t.then_inc(cx.sTE.h, 1); cx.sTE.bump()
                    cx.w(VE, cx.sTE)
                    VE.reciprocal(reciprow[0:1, :], pB[R:R + 1, :])
                    cx.vfence()
                    VE.tensor_copy(recipbf[0:1, :],
                                   reciprow[0:1, :]).then_inc(cx.sVE.h, 1)
                    cx.sVE.bump()
                    cx.gTE()
                    for b in range(4):
                        t = TE.matmul(pA[0:R, 512 * b:512 * (b + 1)],
                                      onesr[:, 0:R],
                                      recipbf[0:1, 512 * b:512 * (b + 1)],
                                      start=True, stop=True)
                    t.then_inc(cx.sTE.h, 1); cx.sTE.bump()
                    cx.w(SC, cx.sTE)
                    cx.w(SC, cx.sGD)
                    cx.w(SC, cx.sVE)
                    SC.activation(atst[:, :], pB[0:R, :],
                                  AF.Identity).then_inc(cx.sSC.h, 1)
                    cx.sSC.bump()
                    cx.w(VE, cx.sSC)
                    VE.tensor_mul(atst[:, :], atst[:, :],
                                  pA[0:R, :]).then_inc(cx.sVE.h, 1)
                    cx.sVE.bump()
                    cx.w(GP, cx.sVE)
                    GP.dma_start(out=atT[gb:gb + 32, cc, :],
                                 in_=atst[:, :]).then_inc(cx.sGD.h, 16)
                    cx.sGD.bump(16)

            for h in range(0 if "weff" in DEV.get("skip", ()) else H):
                g, cc = h % 4, h // 4
                gb = 32 * g
                rc, ro = (h * HD) // 128, (h * HD) % 128
                cx.gTE()
                for b in range(2):
                    t = TE.matmul(pA[gb:gb + 32, 512 * b:512 * (b + 1)],
                                  w64[ro:ro + 64, 3 * R:4 * R],
                                  WP[ro:ro + 64, rc, 512 * b:512 * (b + 1)],
                                  start=True, stop=True,
                                  tile_position=(ro, gb))
                t.then_inc(cx.sTE.h, 1); cx.sTE.bump()
                cx.w(VE, cx.sTE)
                VE.tensor_copy(Weff[gb:gb + 32, cc, :],
                               pA[gb:gb + 32, 0:C]).then_inc(cx.sVE.h, 1)
                cx.sVE.bump()

            for tt in range(NT):
                bb = tt % 2
                cx.w(SY, cx.sVE)
                SY.dma_start(out=P["xres"][:, bb, :],
                             in_=tiled(x0)[:, tt, :]).then_inc(cx.sRS.h, 16)
                cx.sRS.bump(16)
                cx.gTE()
                cx.w(TE, cx.sGD)
                first = True
                if "proj" not in DEV.get("skip", ()):
                    for cc in range(4):
                        for b in range(2):
                            TE.matmul(pB[:, 512 * b:512 * (b + 1)],
                                      atT[:, cc, 128 * tt:128 * (tt + 1)],
                                      Weff[:, cc, 512 * b:512 * (b + 1)],
                                      start=first, stop=False,
                                      skip_group_check=True)
                        first = False
                cx.w(TE, cx.sRS)
                for b in range(2):
                    t = TE.matmul(pB[:, 512 * b:512 * (b + 1)], onesr[:, :],
                                  browp[0:1, 512 * b:512 * (b + 1)],
                                  start=(first and b == 0), stop=True,
                                  skip_group_check=True)
                t.then_inc(cx.sTE.h, 1); cx.sTE.bump()
                emit_ln_tile(cx, P, pB[:, 0:C], P["xres"][:, bb, :],
                             P["xln"][:, bb, :],
                             out_wait=[(cx.sXP, cx.sXP.v)])
                cx.w(SY, cx.sSC)
                SY.dma_start(out=tiled(xspillA)[:, tt, :],
                             in_=P["xln"][:, bb, :]).then_inc(cx.sXP.h, 16)
                cx.sXP.bump(16)
                if stop_after != "l0attn":
                    emit_xT_tile(cx, P, P["xln"][:, bb, :], tt)

        if stop_after == "l0attn":
            _emit_p1_tail(cx, P, xspillA, x1_out, m1_out, bhbm)
            blk = perm.enter_context(nc.Block())
            cx.rec.replay(blk)
            return nc, inp

        with ExitStack() as ph:
            Ep = ph.enter_context
            g2 = Ep(st("g2", [128, NH, T], BF16))
            with ExitStack() as ph2:
                Ep2 = ph2.enter_context
                wring = Ep2(st("wring", [128, 2, 2, ND, 128], BF16))
                silu = Ep2(st("silu", [128, T], F32))
                sW = [Sem(Ep2, nc, "sW0"), Sem(Ep2, nc, "sW1")]
                te_after = {}
                for hh in range(NH):
                    bb = hh % 2
                    if hh >= 2:
                        GP.wait_ge(cx.sTE.h, te_after[hh - 2])
                    GP.dma_start(
                        out=wring[:, bb, 0, :, :],
                        in_=inp["wg0T"][:, 128 * hh:128 * (hh + 1)].rearrange(
                            "(c p) h -> p c h", p=128)).then_inc(
                        sW[bb].h, 16); sW[bb].bump(16)
                    GP.dma_start(
                        out=wring[:, bb, 1, :, :],
                        in_=inp["wu0T"][:, 128 * hh:128 * (hh + 1)].rearrange(
                            "(c p) h -> p c h", p=128)).then_inc(
                        sW[bb].h, 16); sW[bb].bump(16)
                    cx.gTE()
                    cx.w(TE, sW[bb])
                    if hh == 0:
                        cx.w(TE, cx.sXP, cx.sXP.v)
                    for w_i, p in ((0, pA), (1, pB)):
                        for b in range(4):
                            for d in range(ND):
                                t = TE.matmul(
                                    p[:, 512 * b:512 * (b + 1)],
                                    wring[:, bb, w_i, d, :],
                                    P["xT"][:, d, 512 * b:512 * (b + 1)],
                                    start=(d == 0), stop=(d == ND - 1))
                        t.then_inc(cx.sTE.h, 1); cx.sTE.bump()
                    te_after[hh] = cx.sTE.v
                    cx.w(SC, cx.sTE)
                    SC.activation(silu[:, :], pA[:, :], AF.Silu,
                                  scale=FSC).then_inc(cx.sSC.h, 1)
                    cx.sSC.bump()
                    cx.w(VE, cx.sSC)
                    VE.tensor_mul(g2[:, hh, :], silu[:, :],
                                  pB[:, :]).then_inc(cx.sVE.h, 1); cx.sVE.bump()

            wdsb = Ep(st("wdsb", [128, NH, C], BF16))
            cx.w(GP, cx.sTE); cx.w(GP, cx.sSC); cx.w(GP, cx.sVE)
            GP.dma_start(out=wdsb[:], in_=chunked(inp["wd0T"])).then_inc(
                cx.sGD.h, 16); cx.sGD.bump(16)
            wd_gd = cx.sGD.v
            for tt in range(NT):
                bb = tt % 2
                cx.w(SY, cx.sVE)
                cx.w(SY, cx.sXP)
                SY.dma_start(out=P["xres"][:, bb, :],
                             in_=tiled(xspillA)[:, tt, :]).then_inc(
                    cx.sRS.h, 16); cx.sRS.bump(16)
                cx.gTE()
                cx.w(TE, cx.sRS)
                if tt == 0:
                    cx.w(TE, cx.sGD, wd_gd)
                for hh in range(NH):
                    for b in range(2):
                        t = TE.matmul(pB[:, 512 * b:512 * (b + 1)],
                                      g2[:, hh, 128 * tt:128 * (tt + 1)],
                                      wdsb[:, hh, 512 * b:512 * (b + 1)],
                                      start=(hh == 0), stop=(hh == NH - 1),
                                      skip_group_check=True)
                t.then_inc(cx.sTE.h, 1); cx.sTE.bump()
                emit_ln_tile(cx, P, pB[:, 0:C], P["xres"][:, bb, :],
                             P["xln"][:, bb, :], resid_scale=FSC * FSC,
                             out_wait=[(cx.sXP, cx.sXP.v)])
                cx.w(SY, cx.sSC)
                SY.dma_start(out=tiled(xspillB)[:, tt, :],
                             in_=P["xln"][:, bb, :]).then_inc(cx.sXP.h, 16)
                cx.sXP.bump(16)
                if stop_after != "l0ffn":
                    emit_xT_tile(cx, P, P["xln"][:, bb, :], tt)

        if stop_after == "l0ffn":
            _emit_p1_tail(cx, P, xspillB, x1_out, m1_out, bhbm)
            blk = perm.enter_context(nc.Block())
            cx.rec.replay(blk)
            return nc, inp

        with ExitStack() as ph:
            Ep = ph.enter_context
            Wbuf = Ep(st("Wbuf", [128, ND, C], BF16))
            qroT = Ep(st("qroT", [128, 4, T], BF16))
            kroT = Ep(st("kroT", [128, 4, T], BF16))
            von1 = Ep(st("von1", [128, NT, 8, HD + 1], BF16))
            avT = Ep(st("avT", [128, ND, T], BF16))
            avst = Ep(st("avst", [64, T], BF16))
            probsT = Ep(st("probsT1", [128, T], BF16))
            qtile = Ep(st("qtile", [128, 512], BF16))
            rtmp = Ep(st("rtmp", [128, 512], BF16))
            rcos = Ep(st("rcos", [128, 1, C], BF16))
            rsin = Ep(st("rsin", [128, 1, C], BF16))
            nsq = Ep(st("nsq", [128, 512], F32))
            nstat = Ep(st("nstat", [128, 16], F32))
            brow4 = Ep(st("brow4", [1, 4, C], BF16))
            growL = Ep(st("growL", [1, 2048], F32))
            reciprow = Ep(st("reciprow1", [1, T], F32))
            recipbf = Ep(st("recipbf1", [1, T], BF16))
            ghbm = nc.dram_tensor("ghbm16", [128, 16], F32).ap()

            cx.w(GP, cx.sTE); cx.w(GP, cx.sSC); cx.w(GP, cx.sVE)
            for i, bsrc in enumerate((bqr, bkr, bvr, bor)):
                GP.dma_start(out=brow4[:, i, :], in_=bsrc).then_inc(
                    cx.sGD.h, 16); cx.sGD.bump(16)
            brow_gd = cx.sGD.v

            for half in range(2):
                hh0 = 8 * half
                cx.w(VE, cx.sSC)
                VE.memset(nstat[:, :], 0.0)
                VE.memset(von1[:, :, :, HD:HD + 1], 1.0)
                VE.engine_nop().then_inc(cx.sVE.h, 1); cx.sVE.bump()
                for wi, (wsrc, brix) in enumerate(
                        [(wqT, 0), (wkT, 1), (wvT, 2)]):
                    cx.w(GP, cx.sTE)
                    GP.dma_start(out=Wbuf[:, :, :],
                                 in_=chunked(wsrc)).then_inc(cx.sGD.h, 16)
                    cx.sGD.bump(16)
                    w_gd = cx.sGD.v
                    for tt in range(NT):
                        rb = 0
                        if wi != 2:
                            cx.w(SY, cx.sVE)
                            SY.dma_start(
                                out=rcos[:, rb, :],
                                in_=tiled(ropecos)[:, tt, :]).then_inc(
                                cx.sRS.h, 16); cx.sRS.bump(16)
                            SY.dma_start(
                                out=rsin[:, rb, :],
                                in_=tiled(ropesin)[:, tt, :]).then_inc(
                                cx.sRS.h, 16); cx.sRS.bump(16)
                        rs_here = cx.sRS.v
                        cx.gTE()
                        if tt == 0:
                            cx.w(TE, cx.sGD, w_gd)
                            if wi == 0 and half == 0:
                                cx.w(TE, cx.sXP, cx.sXP.v)
                        for d in range(ND):
                            TE.matmul(
                                pA[:, 0:512],
                                P["xT"][:, d, 128 * tt:128 * (tt + 1)],
                                Wbuf[:, d, 512 * half:512 * (half + 1)],
                                start=(d == 0), stop=False,
                                skip_group_check=True)
                        t = TE.matmul(pA[:, 0:512], onesr[:, :],
                                      brow4[0:1, brix,
                                            512 * half:512 * (half + 1)],
                                      start=False, stop=True,
                                      skip_group_check=True)
                        t.then_inc(cx.sTE.h, 1); cx.sTE.bump()
                        if wi == 2:
                            cx.w(SC, cx.sTE)
                            SC.activation(
                                von1[:, tt, :, 0:HD],
                                pA[:, 0:512].rearrange(
                                    "p (h d) -> p h d", d=HD),
                                AF.Identity).then_inc(
                                cx.sSC.h, 1); cx.sSC.bump()
                            continue
                        cx.w(SC, cx.sTE)
                        cx.w(SC, cx.sXP)
                        SC.activation(qtile[:, :], pA[:, 0:512],
                                      AF.Identity).then_inc(cx.sSC.h, 1)
                        cx.sSC.bump()
                        qh = qtile[:, :].rearrange("p (h d) -> p h d", d=HD)
                        th = rtmp[:, :].rearrange("p (h d) -> p h d", d=HD)
                        cs = rcos[:, rb, 512 * half:512 * (half + 1)]
                        sn = rsin[:, rb, 512 * half:512 * (half + 1)].rearrange(
                            "p (h d) -> p h d", d=HD)
                        cx.w(VE, cx.sSC)
                        cx.w(VE, cx.sRS, rs_here)
                        VE.tensor_mul(th[:, :, 0:32], qh[:, :, 32:64],
                                      sn[:, :, 0:32])
                        VE.tensor_mul(th[:, :, 32:64], qh[:, :, 0:32],
                                      sn[:, :, 32:64])
                        VE.tensor_mul(qtile[:, :], qtile[:, :], cs)
                        VE.tensor_add(qtile[:, :], qtile[:, :], rtmp[:, :])
                        VE.tensor_mul(nsq[:, :], qtile[:, :], qtile[:, :])
                        VE.reduce_sum(
                            P["col"][:, 8:16],
                            nsq[:, :].rearrange("p (h d) -> p h d", d=HD),
                            axis=AX.X)
                        VE.tensor_max(nstat[:, 8 * wi:8 * (wi + 1)],
                                      nstat[:, 8 * wi:8 * (wi + 1)],
                                      P["col"][:, 8:16])
                        VE.engine_nop().then_inc(cx.sVE.h, 1); cx.sVE.bump()
                        dstT = qroT if wi == 0 else kroT
                        cx.w(SY, cx.sVE)
                        cx.w(SY, cx.sGD)
                        for d4 in range(4):
                            SY.dma_start_transpose(
                                dstT[:, d4, 128 * tt:128 * (tt + 1)],
                                qtile[:, 128 * d4:128 * (d4 + 1)]).then_inc(
                                cx.sXP.h, 16)
                            cx.sXP.bump(16)

                cx.w(SY, cx.sVE)
                SY.dma_start(out=ghbm, in_=nstat[:, :]).then_inc(cx.sXP.h, 16)
                cx.sXP.bump(16)
                cx.w(SY, cx.sXP)
                SY.dma_start(out=growL[0:1, :],
                             in_=ghbm.rearrange("p s -> (p s)").unsqueeze(0)).then_inc(
                    cx.sXP.h, 16); cx.sXP.bump(16)
                cx.w(VE, cx.sXP)
                VE.reduce_max(srow[0:1, 0:16],
                              growL[0:1, :].rearrange("a (p s) -> a s p", s=16),
                              axis=AX.X)
                VE.engine_nop().then_inc(cx.sVE.h, 1); cx.sVE.bump()
                cx.w(VE, cx.sVE)
                VE.tensor_mul(srow[0:1, 16:24], srow[0:1, 0:8],
                              srow[0:1, 8:16])
                VE.engine_nop().then_inc(cx.sVE.h, 1); cx.sVE.bump()
                cx.w(SC, cx.sVE)
                SC.activation(srow[0:1, 24:32], srow[0:1, 16:24],
                              AF.Sqrt).then_inc(cx.sSC.h, 1); cx.sSC.bump()
                cx.w(VE, cx.sSC)
                VE.tensor_scalar_mul(srow[0:1, 32:40], srow[0:1, 24:32], -S8)
                VE.engine_nop().then_inc(cx.sVE.h, 1); cx.sVE.bump()
                emit_bcast_cols(cx, P, srow[0:1, 32:40],
                                negsB[:, hh0:hh0 + 8], 8)

                for hl in range(8):
                    h = hh0 + hl
                    ch, rows = hl // 2, 64 * (hl % 2)
                    for kt in range(NT):
                        q0 = 128 * kt
                        b0 = q0 // 512
                        cx.gTE()
                        segv = {}
                        for b in range(b0, 4):
                            lo, hi2 = max(512 * b, q0), 512 * (b + 1)
                            t = TE.matmul(pA[:, lo:hi2],
                                          kroT[rows:rows + 64, ch, q0:q0 + 128],
                                          qroT[rows:rows + 64, ch, lo:hi2],
                                          start=True, stop=True,
                                          tile_position=(rows, 0))
                            if b == b0:
                                t = TE.matmul(pA[:, q0:q0 + 128], idn[:, :],
                                              madd[:, :], start=False,
                                              stop=True,
                                              skip_group_check=True)
                            t.then_inc(cx.sTE.h, 1); cx.sTE.bump()
                            segv[b] = cx.sTE.v
                        expv = {}
                        for b in range(b0, 4):
                            lo, hi2 = max(512 * b, q0), 512 * (b + 1)
                            SC.wait_ge(cx.sTE.h, segv[b])
                            SC.activation(probsT[:, lo:hi2], pA[:, lo:hi2],
                                          AF.Exp, bias=negsB[:, h:h + 1],
                                          scale=S8).then_inc(cx.sSC.h, 1)
                            cx.sSC.bump()
                            expv[b] = cx.sSC.v
                        for b in range(b0, 4):
                            lo, hi2 = max(512 * b, q0), 512 * (b + 1)
                            TE.wait_ge(cx.sSC.h, expv[b])
                            t = TE.matmul(
                                pB[0:HD + 1, lo:hi2],
                                von1[:, kt, hl, :],
                                probsT[:, lo:hi2],
                                start=(kt == 0),
                                stop=(kt == min(4 * b + 3, NT - 1)),
                                skip_group_check=True)
                            t.then_inc(cx.sTE.h, 1); cx.sTE.bump()
                    cx.w(VE, cx.sTE)
                    VE.reciprocal(reciprow[0:1, :], pB[HD:HD + 1, :])
                    cx.vfence()
                    VE.tensor_copy(recipbf[0:1, :],
                                   reciprow[0:1, :]).then_inc(cx.sVE.h, 1)
                    cx.sVE.bump()
                    cx.gTE()
                    for b in range(4):
                        t = TE.matmul(pA[0:HD, 512 * b:512 * (b + 1)],
                                      onesr[:, 0:HD],
                                      recipbf[0:1, 512 * b:512 * (b + 1)],
                                      start=True, stop=True)
                    t.then_inc(cx.sTE.h, 1); cx.sTE.bump()
                    cx.w(SC, cx.sTE)
                    cx.w(SC, cx.sGD)
                    cx.w(SC, cx.sVE)
                    SC.activation(avst[:, :], pB[0:HD, :],
                                  AF.Identity).then_inc(cx.sSC.h, 1)
                    cx.sSC.bump()
                    cx.w(VE, cx.sSC)
                    VE.tensor_mul(avst[:, :], avst[:, :],
                                  pA[0:HD, :]).then_inc(cx.sVE.h, 1)
                    cx.sVE.bump()
                    cx.w(GP, cx.sVE)
                    GP.dma_start(out=avT[rows:rows + 64, ch + 4 * half, :],
                                 in_=avst[:, :]).then_inc(cx.sGD.h, 16)
                    cx.sGD.bump(16)

            cx.w(GP, cx.sTE)
            GP.dma_start(out=Wbuf[:, :, :],
                         in_=chunked(woT)).then_inc(cx.sGD.h, 16)
            cx.sGD.bump(16)
            wo_gd = cx.sGD.v
            for tt in range(NT):
                bb = tt % 2
                cx.w(SY, cx.sVE)
                cx.w(SY, cx.sXP)
                SY.dma_start(out=P["xres"][:, bb, :],
                             in_=tiled(xspillB)[:, tt, :]).then_inc(
                    cx.sRS.h, 16); cx.sRS.bump(16)
                cx.gTE()
                if tt == 0:
                    cx.w(TE, cx.sGD, wo_gd)
                for b in range(2):
                    for dd in range(ND):
                        ch, rows = dd % 4, 64 * (dd // 4)
                        TE.matmul(pB[:, 512 * b:512 * (b + 1)],
                                  avT[:, dd, 128 * tt:128 * (tt + 1)],
                                  Wbuf[:, dd, 512 * b:512 * (b + 1)],
                                  start=(dd == 0), stop=False,
                                  skip_group_check=True)
                cx.w(TE, cx.sRS)
                for b in range(2):
                    t = TE.matmul(pB[:, 512 * b:512 * (b + 1)], onesr[:, :],
                                  brow4[0:1, 3, 512 * b:512 * (b + 1)],
                                  start=False, stop=True,
                                  skip_group_check=True)
                t.then_inc(cx.sTE.h, 1); cx.sTE.bump()
                emit_ln_tile(cx, P, pB[:, 0:C], P["xres"][:, bb, :],
                             P["xln"][:, bb, :],
                             out_wait=[(cx.sXP, cx.sXP.v), (cx.sST, cx.sST.v)])
                cx.w(SY, cx.sSC)
                SY.dma_start(out=tiled(x1_out)[:, tt, :],
                             in_=P["xln"][:, bb, :]).then_inc(cx.sST.h, 16)
                cx.sST.bump(16)
                cx.w(VE, cx.sSC)
                VE.reduce_max(P["col"][:, 0:1], P["xln"][:, bb, :], axis=AX.X,
                              apply_absolute_value=True)
                VE.tensor_max(P["runmax"][:, 0:1], P["runmax"][:, 0:1],
                              P["col"][:, 0:1]).then_inc(cx.sVE.h, 1)
                cx.sVE.bump()

        emit_colmax_scalar(cx, P, bhbm, P["runmax"][:, 0:1], srow[0:1, 0:1])
        cx.w(SY, cx.sVE)
        SY.dma_start(out=m1_out, in_=srow[0:1, 0:1]).then_inc(cx.sST.h, 16)
        cx.sST.bump(16)
        cx.w(GP, cx.sST)
        blk = perm.enter_context(nc.Block())
        cx.rec.replay(blk)
        return nc, inp


def _emit_p1_tail(cx, P, xsp, x1_out, m1_out, bhbm):
    SY, VE, GP = cx.SY, cx.VE, cx.GP
    for tt in range(NT):
        bb = tt % 2
        cx.w(SY, cx.sVE)
        cx.w(SY, cx.sST)
        cx.w(SY, cx.sXP)
        SY.dma_start(out=P["xres"][:, bb, :],
                     in_=tiled(xsp)[:, tt, :]).then_inc(cx.sRS.h, 16)
        cx.sRS.bump(16)
        cx.w(SY, cx.sRS)
        SY.dma_start(out=tiled(x1_out)[:, tt, :],
                     in_=P["xres"][:, bb, :]).then_inc(cx.sST.h, 16)
        cx.sST.bump(16)
        cx.w(VE, cx.sRS)
        VE.reduce_max(P["col"][:, 0:1], P["xres"][:, bb, :], axis=AX.X,
                      apply_absolute_value=True)
        VE.tensor_max(P["runmax"][:, 0:1], P["runmax"][:, 0:1],
                      P["col"][:, 0:1]).then_inc(cx.sVE.h, 1)
        cx.sVE.bump()
    emit_colmax_scalar(cx, P, bhbm, P["runmax"][:, 0:1], P["srow"][0:1, 0:1])
    cx.w(SY, cx.sVE)
    SY.dma_start(out=m1_out, in_=P["srow"][0:1, 0:1]).then_inc(cx.sST.h, 16)
    cx.sST.bump(16)
    cx.w(GP, cx.sST)


def build_pn(stage):
    nc = bass.Bass("TRN2", target_bir_lowering=False)
    dt = nc.dram_tensor
    inp = {}

    def I(name, shape, dtyp=F32):
        inp[name] = dt(name, shape, dtyp, kind="ExternalInput").ap()
        return inp[name]

    def O(name, shape, dtyp=F32):
        return dt(name, shape, dtyp, kind="ExternalOutput").ap()

    SPW = NH * T
    if stage == 2:
        x1 = I("x1", [T, C])
        wgate = I("wgate", [FH, C])
        wup = I("wup", [FH, C])
        M1 = I("M1", [1, 8])
        mo = O("m_out", [1, 2])
        gate_sp = O("gate_sp", [128, SPW])
        up_sp = O("up_sp", [128, SPW])
    elif stage == 3:
        gate_spi = I("gate_sp", [128, SPW])
        M2 = I("M2", [2, 8])
        mo = O("m_out", [1, 1])
        gq_sp = O("gq_sp", [128, SPW])
        squ_sp = O("squ_sp", [128, SPW])
    elif stage == 4:
        up_spi = I("up_sp", [128, SPW])
        gq_spi = I("gq_sp", [128, SPW])
        squ_spi = I("squ_sp", [128, SPW])
        M2 = I("M2", [2, 8])
        M3 = I("M3", [1, 8])
        mo = O("m_out", [1, 1])
        oraw_sp = O("oraw_sp", [128, SPW])
    else:
        x1 = I("x1", [T, C])
        wdown = I("wdown", [C, FH])
        oraw_spi = I("oraw_sp", [128, SPW])
        M2 = I("M2", [2, 8])
        M3 = I("M3", [1, 8])
        M4 = I("M4", [1, 8])
        y_out = O("y_out", [T, C])
    I("ones_row", [1, 128], BF16)
    I("ones_f32", [1, 128])
    bhbm = dt("bhbm", [128, 1], F32).ap()

    with ExitStack() as perm:
        E = perm.enter_context
        st = nc.sbuf_tensor
        cx = Cx(nc, E)
        SY, GP, VE, SC, TE = cx.SY, cx.GP, cx.VE, cx.SC, cx.TE
        P = alloc_perm(cx, E, with_xT=False)
        pA, pB = P["pA"], P["pB"]
        srow = P["srow"]
        scol = E(st("scol", [128, 16], F32))
        fscr = E(st("fscr", [128, T], F32))
        fscr2 = E(st("fscr2", [128, T], F32))
        iscr = E(st("iscr", [128, T], I32))
        c32k = E(st("c32k", [128, 1], F32))
        Ms = E(st("Ms", [1, 48], F32))

        if stage == 2:
            SY.dma_start(out=Ms[0:1, 0:8], in_=M1).then_inc(cx.sLD.h, 16)
            cx.sLD.bump(16)
        if stage >= 3:
            SY.dma_start(out=Ms[0:1, 8:24],
                         in_=M2.rearrange("b c -> (b c)").unsqueeze(0)).then_inc(
                cx.sLD.h, 16); cx.sLD.bump(16)
        if stage >= 4:
            SY.dma_start(out=Ms[0:1, 24:32], in_=M3).then_inc(cx.sLD.h, 16)
            cx.sLD.bump(16)
        if stage >= 5:
            SY.dma_start(out=Ms[0:1, 32:40], in_=M4).then_inc(cx.sLD.h, 16)
            cx.sLD.bump(16)
        load_base_consts(cx, P, inp)
        VE.memset(srow[0:1, :], 0.0)
        VE.memset(c32k[:], 32768.0)
        cx.w(VE, cx.sLD)

        def mk_scale(ds, dr, src_row):
            VE.reduce_max(srow[0:1, 40:41], src_row, axis=AX.X)
            VE.tensor_scalar(srow[0:1, ds:ds + 1], srow[0:1, 40:41],
                             1.0 / QA, 1e-12, ALU.mult, ALU.max)
            VE.reciprocal(srow[0:1, dr:dr + 1], srow[0:1, ds:ds + 1])
            cx.vfence()

        if stage == 2:
            mk_scale(0, 1, Ms[0:1, 0:8])
            VE.tensor_scalar(srow[0:1, 14:15], srow[0:1, 0:1], -32768.0,
                             None, ALU.mult)
        if stage >= 3:
            mk_scale(2, 3, Ms[0:1, 8:16])
            mk_scale(4, 5, Ms[0:1, 16:24])
            VE.tensor_scalar(srow[0:1, 13:14], srow[0:1, 2:3], -32768.0,
                             None, ALU.mult)
        if stage >= 4:
            VE.reduce_max(srow[0:1, 40:41], Ms[0:1, 24:32], axis=AX.X)
            VE.tensor_tensor(srow[0:1, 41:42], srow[0:1, 40:41],
                             srow[0:1, 2:3], ALU.mult)
            VE.tensor_scalar_mul(srow[0:1, 41:42], srow[0:1, 41:42],
                                 1.0 / 65536.0)
            VE.tensor_scalar(srow[0:1, 6:7], srow[0:1, 41:42],
                             1.0 / QA, 1e-12, ALU.mult, ALU.max)
            VE.reciprocal(srow[0:1, 7:8], srow[0:1, 6:7])
            cx.vfence()
            VE.tensor_tensor(srow[0:1, 10:11], srow[0:1, 2:3],
                             srow[0:1, 7:8], ALU.mult)
            VE.tensor_scalar_mul(srow[0:1, 10:11], srow[0:1, 10:11],
                                 1.0 / 65536.0)
        if stage >= 5:
            VE.reduce_max(srow[0:1, 40:41], Ms[0:1, 32:40], axis=AX.X)
            VE.tensor_tensor(srow[0:1, 41:42], srow[0:1, 40:41],
                             srow[0:1, 6:7], ALU.mult)
            VE.tensor_tensor(srow[0:1, 41:42], srow[0:1, 41:42],
                             srow[0:1, 4:5], ALU.mult)
            VE.tensor_scalar(srow[0:1, 8:9], srow[0:1, 41:42],
                             1.0 / QA, 1e-12, ALU.mult, ALU.max)
            VE.reciprocal(srow[0:1, 9:10], srow[0:1, 8:9])
            cx.vfence()
            VE.tensor_tensor(srow[0:1, 11:12], srow[0:1, 6:7],
                             srow[0:1, 4:5], ALU.mult)
            VE.tensor_tensor(srow[0:1, 11:12], srow[0:1, 11:12],
                             srow[0:1, 9:10], ALU.mult)
            VE.tensor_scalar(srow[0:1, 12:13], srow[0:1, 8:9], -32768.0,
                             None, ALU.mult)
        VE.engine_nop().then_inc(cx.sVE.h, 1); cx.sVE.bump()
        emit_bcast_cols(cx, P, srow[0:1, 0:15], scol[:, 0:15], 15)

        def act_round(src_ap, scale_col, dst_ap):
            cx.w(SC, cx.sVE)
            cx.w(SC, cx.sSC)
            SC.activation(fscr[:, :], src_ap, AF.Identity,
                          scale=scale_col,
                          bias=c32k[:, 0:1]).then_inc(cx.sSC.h, 1)
            cx.sSC.bump()
            cx.w(SC, cx.sSC)
            SC.activation(iscr[:, :], fscr[:, :],
                          AF.Identity).then_inc(cx.sSC.h, 1)
            cx.sSC.bump()
            cx.w(VE, cx.sSC)
            VE.tensor_tensor(fscr2[:, :], fscr[:, :], iscr[:, :],
                             ALU.subtract)
            VE.scalar_tensor_tensor(dst_ap, fscr2[:, :], 0.5, iscr[:, :],
                                    ALU.is_ge, ALU.add)

        if stage == 2:
            with ExitStack() as ph:
                Ep = ph.enter_context
                P["xT"] = Ep(st("xT", [128, ND, T], BF16))
                wstg = Ep(st("wstg", [128, 2, 2, C], F32))
                wqd = Ep(st("wqd", [128, 2, 2, C], BF16))
                wTr = Ep(st("wTr", [128, 2, 2, ND, 128], BF16))
                sblk = Ep(st("sblk", [128, 32], F32))
                rblk = Ep(st("rblk", [128, 32], F32))
                gsp = Ep(st("gsp", [128, 2, T], F32))
                usp = Ep(st("usp", [128, 2, T], F32))

                for tt in range(NT):
                    bb = tt % 2
                    cx.w(GP, cx.sVE)
                    GP.wait_ge(cx.sRS.h, cx.sRS.v)
                    GP.dma_start(out=P["xres"][:, bb, :],
                                 in_=tiled(x1)[:, tt, :]).then_inc(
                        cx.sRS.h, 16)
                    cx.sRS.bump(16)
                    cx.w(SC, cx.sRS)
                    cx.w(SC, cx.sVE)
                    SC.activation(fscr[:, 0:C], P["xres"][:, bb, :],
                                  AF.Identity, scale=scol[:, 1:2],
                                  bias=c32k[:, 0:1]).then_inc(cx.sSC.h, 1)
                    cx.sSC.bump()
                    cx.w(SC, cx.sSC)
                    SC.activation(iscr[:, 0:C], fscr[:, 0:C],
                                  AF.Identity).then_inc(cx.sSC.h, 1)
                    cx.sSC.bump()
                    cx.w(VE, cx.sSC)
                    VE.tensor_tensor(fscr2[:, 0:C], fscr[:, 0:C],
                                     iscr[:, 0:C], ALU.subtract)
                    VE.scalar_tensor_tensor(fscr[:, 0:C], fscr2[:, 0:C], 0.5,
                                            iscr[:, 0:C], ALU.is_ge, ALU.add)
                    VE.tensor_scalar(P["xln"][:, bb, :], fscr[:, 0:C],
                                     scol[:, 0:1], scol[:, 14:15],
                                     ALU.mult, ALU.add)
                    VE.engine_nop().then_inc(cx.sVE.h, 1); cx.sVE.bump()
                    emit_xT_tile(cx, P, P["xln"][:, bb, :], tt)
                xq_xp = cx.sXP.v

                ve_chunk, te_mark, xpm, sp_mark = {}, {}, {}, {}
                for hh in range(NH):
                    bb = hh % 2
                    for wi in range(2):
                        wsrc_d = wgate if wi == 0 else wup
                        if hh >= 2:
                            GP.wait_ge(cx.sVE.h, ve_chunk[hh - 2])
                        GP.wait_ge(cx.sGD.h, cx.sGD.v)
                        GP.dma_start(
                            out=wstg[:, bb, wi, :],
                            in_=wsrc_d[128 * hh:128 * (hh + 1),
                                       :]).then_inc(cx.sGD.h, 16)
                        cx.sGD.bump(16)
                        gd_here = cx.sGD.v
                        cx.w(VE, cx.sGD, gd_here)
                        if (hh - 2, wi) in xpm:
                            VE.wait_ge(cx.sXP.h, xpm[(hh - 2, wi)])
                        w3 = wstg[:, bb, wi, :].rearrange(
                            "p (b i) -> p b i", i=32)
                        fv = fscr[:, 0:C].rearrange("p (b i) -> p b i", i=32)
                        VE.reduce_max(sblk[:, :], w3, axis=AX.X,
                                      apply_absolute_value=True)
                        VE.tensor_scalar(sblk[:, :], sblk[:, :], 1.0 / 7.0,
                                         1e-12, ALU.mult, ALU.max)
                        VE.reciprocal(rblk[:, :], sblk[:, :])
                        cx.vfence()
                        VE.tensor_tensor(
                            fv, w3,
                            rblk[:, :].unsqueeze(2).broadcast_to(
                                [128, 32, 32]), ALU.mult)
                        VE.tensor_scalar(fscr[:, 0:C], fscr[:, 0:C], 16.0,
                                         None, ALU.add)
                        VE.engine_nop().then_inc(cx.sVE.h, 1); cx.sVE.bump()
                        cx.w(SC, cx.sVE)
                        SC.activation(iscr[:, 0:C], fscr[:, 0:C],
                                      AF.Identity).then_inc(cx.sSC.h, 1)
                        cx.sSC.bump()
                        cx.w(VE, cx.sSC)
                        VE.tensor_tensor(fscr2[:, 0:C], fscr[:, 0:C],
                                         iscr[:, 0:C], ALU.subtract)
                        VE.scalar_tensor_tensor(fscr[:, 0:C], fscr2[:, 0:C],
                                                0.5, iscr[:, 0:C],
                                                ALU.is_ge, ALU.add)
                        VE.tensor_scalar(fscr[:, 0:C], fscr[:, 0:C], -16.0,
                                         None, ALU.add)
                        VE.tensor_tensor(
                            wqd[:, bb, wi, :].rearrange(
                                "p (b i) -> p b i", i=32),
                            fv,
                            sblk[:, :].unsqueeze(2).broadcast_to(
                                [128, 32, 32]), ALU.mult)
                        VE.engine_nop().then_inc(cx.sVE.h, 1); cx.sVE.bump()
                        cx.w(SY, cx.sVE)
                        cx.w(SY, cx.sGD)
                        if hh >= 2:
                            SY.wait_ge(cx.sTE.h, te_mark[hh - 2])
                        SY.wait_ge(cx.sXP.h, cx.sXP.v)
                        for d in range(ND):
                            SY.dma_start_transpose(
                                wTr[:, bb, wi, d, :],
                                wqd[:, bb, wi,
                                    128 * d:128 * (d + 1)]).then_inc(
                                cx.sXP.h, 16)
                            cx.sXP.bump(16)
                        xpm[(hh, wi)] = cx.sXP.v
                    cx.gTE()
                    cx.w(TE, cx.sXP, xpm[(hh, 1)])
                    if hh == 0:
                        cx.w(TE, cx.sXP, xq_xp)
                    for wi, p in ((0, pA), (1, pB)):
                        for b in range(4):
                            for d in range(ND):
                                t = TE.matmul(
                                    p[:, 512 * b:512 * (b + 1)],
                                    wTr[:, bb, wi, d, :],
                                    P["xT"][:, d, 512 * b:512 * (b + 1)],
                                    start=(d == 0), stop=(d == ND - 1))
                        t.then_inc(cx.sTE.h, 1); cx.sTE.bump()
                    te_here = cx.sTE.v
                    te_mark[hh] = te_here

                    cx.w(VE, cx.sTE, te_here)
                    VE.reduce_max(P["col"][:, 0:1], pA[:, :], axis=AX.X,
                                  apply_absolute_value=True)
                    VE.tensor_max(P["runmax"][:, 0:1], P["runmax"][:, 0:1],
                                  P["col"][:, 0:1])
                    VE.reduce_max(P["col"][:, 1:2], pB[:, :], axis=AX.X,
                                  apply_absolute_value=True)
                    VE.tensor_max(P["runmax"][:, 1:2], P["runmax"][:, 1:2],
                                  P["col"][:, 1:2]).then_inc(cx.sVE.h, 1)
                    cx.sVE.bump()
                    ve_chunk[hh] = cx.sVE.v
                    if DEV.get("p2_nospill"):
                        continue
                    cx.w(SC, cx.sTE, te_here)
                    cx.w(SC, cx.sVE)
                    if hh >= 2:
                        SC.wait_ge(cx.sST.h, sp_mark[hh - 2])
                    SC.activation(gsp[:, bb, :], pA[:, :], AF.Identity)
                    SC.activation(usp[:, bb, :], pB[:, :],
                                  AF.Identity).then_inc(cx.sSC.h, 1)
                    cx.sSC.bump()
                    if DEV.get("p2_nodma"):
                        sp_mark[hh] = cx.sST.v
                        continue
                    cx.w(GP, cx.sSC)
                    GP.wait_ge(cx.sST.h, cx.sST.v)
                    GP.dma_start(out=gate_sp[:, T * hh:T * (hh + 1)],
                                 in_=gsp[:, bb, :]).then_inc(cx.sST.h, 16)
                    cx.sST.bump(16)
                    GP.dma_start(out=up_sp[:, T * hh:T * (hh + 1)],
                                 in_=usp[:, bb, :]).then_inc(cx.sST.h, 16)
                    cx.sST.bump(16)
                    sp_mark[hh] = cx.sST.v

            for j in range(2):
                emit_colmax_scalar(cx, P, bhbm, P["runmax"][:, j:j + 1],
                                   srow[0:1, 48 + j:49 + j])
            cx.w(SY, cx.sVE)
            SY.dma_start(out=mo, in_=srow[0:1, 48:50]).then_inc(cx.sLD.h, 16)
            cx.sLD.bump(16)
            cx.w(GP, cx.sST)
            cx.w(GP, cx.sLD)

        elif stage == 3:
            gbuf = E(st("gbuf", [128, 2, T], F32))
            f3buf = E(st("f3buf", [128, 2, T], F32))
            sqbuf = E(st("sqbuf", [128, 2, T], F32))
            ve_mark, sp_mark = {}, {}
            for hh in range(NH):
                bb = hh % 2
                if hh >= 2:
                    GP.wait_ge(cx.sVE.h, ve_mark[hh - 2])
                GP.wait_ge(cx.sGD.h, cx.sGD.v)
                GP.dma_start(out=gbuf[:, bb, :],
                             in_=gate_spi[:, T * hh:T * (hh + 1)]).then_inc(
                    cx.sGD.h, 16)
                cx.sGD.bump(16)
                gd_here = cx.sGD.v
                cx.w(VE, cx.sGD, gd_here)
                if hh >= 2:
                    VE.wait_ge(cx.sST.h, sp_mark[hh - 2])
                VE.engine_nop().then_inc(cx.sVE.h, 1); cx.sVE.bump()
                act_round(gbuf[:, bb, :], scol[:, 3:4], f3buf[:, bb, :])
                VE.engine_nop().then_inc(cx.sVE.h, 1); cx.sVE.bump()
                cx.w(SC, cx.sVE)
                SC.activation(fscr[:, :], f3buf[:, bb, :], AF.Sigmoid,
                              scale=scol[:, 2:3],
                              bias=scol[:, 13:14]).then_inc(cx.sSC.h, 1)
                cx.sSC.bump()
                cx.w(VE, cx.sSC)
                VE.engine_nop().then_inc(cx.sVE.h, 1); cx.sVE.bump()
                act_round(fscr[:, :], 65536.0, sqbuf[:, bb, :])
                VE.tensor_scalar(sqbuf[:, bb, :], sqbuf[:, bb, :], -32768.0,
                                 65535.0, ALU.add, ALU.min)
                VE.scalar_tensor_tensor(fscr2[:, :], f3buf[:, bb, :],
                                        -32768.0, sqbuf[:, bb, :],
                                        ALU.add, ALU.mult)
                VE.reduce_max(P["col"][:, 0:1], fscr2[:, :], axis=AX.X,
                              apply_absolute_value=True)
                VE.tensor_max(P["runmax"][:, 0:1], P["runmax"][:, 0:1],
                              P["col"][:, 0:1]).then_inc(cx.sVE.h, 1)
                cx.sVE.bump()
                ve_mark[hh] = cx.sVE.v
                cx.w(SY, cx.sVE)
                SY.wait_ge(cx.sST.h, cx.sST.v)
                SY.dma_start(out=gq_sp[:, T * hh:T * (hh + 1)],
                             in_=f3buf[:, bb, :]).then_inc(cx.sST.h, 16)
                cx.sST.bump(16)
                SY.dma_start(out=squ_sp[:, T * hh:T * (hh + 1)],
                             in_=sqbuf[:, bb, :]).then_inc(cx.sST.h, 16)
                cx.sST.bump(16)
                sp_mark[hh] = cx.sST.v
            emit_colmax_scalar(cx, P, bhbm, P["runmax"][:, 0:1],
                               srow[0:1, 48:49])
            cx.w(SY, cx.sVE)
            SY.dma_start(out=mo, in_=srow[0:1, 48:49]).then_inc(cx.sST.h, 16)
            cx.sST.bump(16)
            cx.w(GP, cx.sST)

        elif stage == 4:
            f3buf = E(st("f3buf", [128, 2, T], F32))
            sqbuf = E(st("sqbuf", [128, 2, T], F32))
            ubuf = E(st("ubuf", [128, 2, T], F32))
            obuf = E(st("obuf", [128, 2, T], F32))
            ve_mark, sp_mark = {}, {}
            for hh in range(NH):
                bb = hh % 2
                if hh >= 2:
                    GP.wait_ge(cx.sVE.h, ve_mark[hh - 2])
                GP.wait_ge(cx.sGD.h, cx.sGD.v)
                GP.dma_start(out=f3buf[:, bb, :],
                             in_=gq_spi[:, T * hh:T * (hh + 1)]).then_inc(
                    cx.sGD.h, 16)
                cx.sGD.bump(16)
                GP.dma_start(out=sqbuf[:, bb, :],
                             in_=squ_spi[:, T * hh:T * (hh + 1)]).then_inc(
                    cx.sGD.h, 16)
                cx.sGD.bump(16)
                GP.dma_start(out=ubuf[:, bb, :],
                             in_=up_spi[:, T * hh:T * (hh + 1)]).then_inc(
                    cx.sGD.h, 16)
                cx.sGD.bump(16)
                gd_here = cx.sGD.v
                cx.w(VE, cx.sGD, gd_here)
                VE.scalar_tensor_tensor(fscr2[:, :], f3buf[:, bb, :],
                                        -32768.0, sqbuf[:, bb, :],
                                        ALU.add, ALU.mult)
                VE.engine_nop().then_inc(cx.sVE.h, 1); cx.sVE.bump()
                act_round(fscr2[:, :], scol[:, 10:11], f3buf[:, bb, :])
                VE.engine_nop().then_inc(cx.sVE.h, 1); cx.sVE.bump()
                act_round(ubuf[:, bb, :], scol[:, 5:6], sqbuf[:, bb, :])
                VE.tensor_scalar(sqbuf[:, bb, :], sqbuf[:, bb, :], -32768.0,
                                 None, ALU.add)
                if hh >= 2:
                    VE.wait_ge(cx.sST.h, sp_mark[hh - 2])
                VE.scalar_tensor_tensor(obuf[:, bb, :], f3buf[:, bb, :],
                                        -32768.0, sqbuf[:, bb, :],
                                        ALU.add, ALU.mult)
                VE.reduce_max(P["col"][:, 0:1], obuf[:, bb, :], axis=AX.X,
                              apply_absolute_value=True)
                VE.tensor_max(P["runmax"][:, 0:1], P["runmax"][:, 0:1],
                              P["col"][:, 0:1]).then_inc(cx.sVE.h, 1)
                cx.sVE.bump()
                ve_mark[hh] = cx.sVE.v
                cx.w(SY, cx.sVE)
                SY.wait_ge(cx.sST.h, cx.sST.v)
                SY.dma_start(out=oraw_sp[:, T * hh:T * (hh + 1)],
                             in_=obuf[:, bb, :]).then_inc(cx.sST.h, 16)
                cx.sST.bump(16)
                sp_mark[hh] = cx.sST.v
            emit_colmax_scalar(cx, P, bhbm, P["runmax"][:, 0:1],
                               srow[0:1, 48:49])
            cx.w(SY, cx.sVE)
            SY.dma_start(out=mo, in_=srow[0:1, 48:49]).then_inc(cx.sST.h, 16)
            cx.sST.bump(16)
            cx.w(GP, cx.sST)

        else:
            oq = E(st("oq", [128, NH, T], BF16))
            with ExitStack() as p1s:
                obuf = p1s.enter_context(st("obuf", [128, 2, T], F32))
                sc_mark = {}
                for hh in range(NH):
                    bb = hh % 2
                    if hh >= 2:
                        GP.wait_ge(cx.sSC.h, sc_mark[hh - 2])
                    GP.wait_ge(cx.sGD.h, cx.sGD.v)
                    GP.dma_start(
                        out=obuf[:, bb, :],
                        in_=oraw_spi[:, T * hh:T * (hh + 1)]).then_inc(
                        cx.sGD.h, 16)
                    cx.sGD.bump(16)
                    gd_here = cx.sGD.v
                    cx.w(SC, cx.sGD, gd_here)
                    act_round(obuf[:, bb, :], scol[:, 11:12], fscr2[:, :])
                    sc_mark[hh] = cx.sSC.v
                    VE.tensor_scalar(oq[:, hh, :], fscr2[:, :],
                                     scol[:, 8:9], scol[:, 12:13],
                                     ALU.mult, ALU.add).then_inc(cx.sVE.h, 1)
                    cx.sVE.bump()

            with ExitStack() as ph:
                Ep = ph.enter_context
                wstg2 = Ep(st("wstg2", [128, 1536], F32))
                wqd2 = Ep(st("wqd2", [128, FH], BF16))
                sblk2 = Ep(st("sblk2", [128, 48], F32))
                rblk2 = Ep(st("rblk2", [128, 48], F32))
                wdT = Ep(st("wdT", [128, NH, C], BF16))
                cx.w(GP, cx.sTE); cx.w(GP, cx.sSC); cx.w(GP, cx.sVE)
                for dd in range(ND):
                    if dd >= 1:
                        GP.wait_ge(cx.sXP.h, xp_marks[dd - 1])
                    for hf in range(2):
                        if hf == 1:
                            GP.wait_ge(cx.sVE.h, cx.sVE.v)
                        GP.dma_start(
                            out=wstg2[:, :],
                            in_=wdown[128 * dd:128 * (dd + 1),
                                      1536 * hf:1536 * (hf + 1)]).then_inc(
                            cx.sGD.h, 16)
                        cx.sGD.bump(16)
                        gd2 = cx.sGD.v
                        cx.w(VE, cx.sGD, gd2)
                        cx.w(VE, cx.sXP)
                        w3 = wstg2[:, :].rearrange("p (b i) -> p b i", i=32)
                        c0 = 1536 * hf
                        VE.reduce_max(sblk2[:, :], w3, axis=AX.X,
                                      apply_absolute_value=True)
                        VE.tensor_scalar(sblk2[:, :], sblk2[:, :], 1.0 / 7.0,
                                         1e-12, ALU.mult, ALU.max)
                        VE.reciprocal(rblk2[:, :], sblk2[:, :])
                        cx.vfence()
                        VE.tensor_tensor(
                            fscr[:, 0:1536].rearrange(
                                "p (b i) -> p b i", i=32),
                            w3,
                            rblk2[:, :].unsqueeze(2).broadcast_to(
                                [128, 48, 32]),
                            ALU.mult)
                        VE.tensor_scalar(fscr[:, 0:1536], fscr[:, 0:1536],
                                         16.0, None, ALU.add)
                        VE.engine_nop().then_inc(cx.sVE.h, 1); cx.sVE.bump()
                        cx.w(SC, cx.sVE)
                        SC.activation(iscr[:, 0:1536], fscr[:, 0:1536],
                                      AF.Identity).then_inc(cx.sSC.h, 1)
                        cx.sSC.bump()
                        cx.w(VE, cx.sSC)
                        VE.tensor_tensor(fscr2[:, 0:1536], fscr[:, 0:1536],
                                         iscr[:, 0:1536], ALU.subtract)
                        VE.scalar_tensor_tensor(fscr[:, 0:1536],
                                                fscr2[:, 0:1536], 0.5,
                                                iscr[:, 0:1536], ALU.is_ge,
                                                ALU.add)
                        VE.tensor_scalar(fscr[:, 0:1536], fscr[:, 0:1536],
                                         -16.0, None, ALU.add)
                        VE.tensor_tensor(
                            wqd2[:, c0:c0 + 1536].rearrange(
                                "p (b i) -> p b i", i=32),
                            fscr[:, 0:1536].rearrange(
                                "p (b i) -> p b i", i=32),
                            sblk2[:, :].unsqueeze(2).broadcast_to(
                                [128, 48, 32]),
                            ALU.mult)
                        VE.engine_nop().then_inc(cx.sVE.h, 1); cx.sVE.bump()
                    cx.w(SY, cx.sVE)
                    cx.w(SY, cx.sGD)
                    for hh in range(NH):
                        SY.dma_start_transpose(
                            wdT[:, hh, 128 * dd:128 * (dd + 1)],
                            wqd2[:, 128 * hh:128 * (hh + 1)]).then_inc(
                            cx.sXP.h, 16)
                        cx.sXP.bump(16)
                    if dd == 0:
                        xp_marks = {}
                    xp_marks[dd] = cx.sXP.v
                wd_xp = cx.sXP.v

                for tt in range(NT):
                    bb = tt % 2
                    cx.w(GP, cx.sVE)
                    GP.wait_ge(cx.sRS.h, cx.sRS.v)
                    GP.dma_start(out=P["xres"][:, bb, :],
                                 in_=tiled(x1)[:, tt, :]).then_inc(
                        cx.sRS.h, 16); cx.sRS.bump(16)
                    cx.gTE()
                    cx.w(TE, cx.sRS)
                    if tt == 0:
                        cx.w(TE, cx.sXP, wd_xp)
                    for hh in range(NH):
                        for b in range(2):
                            t = TE.matmul(
                                pB[:, 512 * b:512 * (b + 1)],
                                oq[:, hh, 128 * tt:128 * (tt + 1)],
                                wdT[:, hh, 512 * b:512 * (b + 1)],
                                start=(hh == 0), stop=(hh == NH - 1),
                                skip_group_check=True)
                    t.then_inc(cx.sTE.h, 1); cx.sTE.bump()
                    emit_ln_tile(cx, P, pB[:, 0:C], P["xres"][:, bb, :],
                                 P["xln"][:, bb, :],
                                 out_wait=[(cx.sST, cx.sST.v)])
                    cx.w(GP, cx.sSC)
                    GP.wait_ge(cx.sST.h, cx.sST.v)
                    GP.dma_start(out=tiled(y_out)[:, tt, :],
                                 in_=P["xln"][:, bb, :]).then_inc(
                        cx.sST.h, 16); cx.sST.bump(16)
            cx.w(SY, cx.sST)

        blk = perm.enter_context(nc.Block())
        cx.rec.replay(blk)
        return nc, inp


_cache = {}


def _get(key, builder):
    if key not in _cache:
        _cache[key] = builder()
    return _cache[key]


def kernel(**inputs):
    cst = host_constants()

    def f32(x):
        return np.ascontiguousarray(np.asarray(x), dtype=None)

    cattn_w = np.ascontiguousarray(inputs["L0_attn_c_attn_w"])
    cattn_bp = np.ascontiguousarray(
        np.asarray(inputs["L0_attn_c_attn_b"]).reshape(24, 128).T)
    cproj_w = np.ascontiguousarray(inputs["L0_attn_c_proj_w"])
    cproj_b = np.ascontiguousarray(
        np.asarray(inputs["L0_attn_c_proj_b"]).reshape(1, C))
    wk_c = np.ascontiguousarray(inputs["L0_attn_wk_c"])
    wk_eT = np.ascontiguousarray(np.asarray(inputs["L0_attn_wk_e"]).T)
    wv_c = np.ascontiguousarray(inputs["L0_attn_wv_c"])
    wv_eT = np.ascontiguousarray(np.asarray(inputs["L0_attn_wv_e"]).T)
    wg0T = np.ascontiguousarray(np.asarray(inputs["L0_ffn_w_gate"]).T)
    wu0T = np.ascontiguousarray(np.asarray(inputs["L0_ffn_w_up"]).T)
    wd0T = np.ascontiguousarray(np.asarray(inputs["L0_ffn_w_down"]).T)
    wqT = np.ascontiguousarray(np.asarray(inputs["L1_attn_Wq"]).T)
    wkT = np.ascontiguousarray(np.asarray(inputs["L1_attn_Wk"]).T)
    wvT = np.ascontiguousarray(np.asarray(inputs["L1_attn_Wv"]).T)
    woT = np.ascontiguousarray(np.asarray(inputs["L1_attn_Wo"]).T)
    bq = np.ascontiguousarray(np.asarray(inputs["L1_attn_bq"]).reshape(1, C))
    bk = np.ascontiguousarray(np.asarray(inputs["L1_attn_bk"]).reshape(1, C))
    bv = np.ascontiguousarray(np.asarray(inputs["L1_attn_bv"]).reshape(1, C))
    bo = np.ascontiguousarray(np.asarray(inputs["L1_attn_bo"]).reshape(1, C))
    wgate1 = np.ascontiguousarray(inputs["L1_ffn_w_gate"])
    wup1 = np.ascontiguousarray(inputs["L1_ffn_w_up"])
    wdown1 = np.ascontiguousarray(inputs["L1_ffn_w_down"])
    x_all = np.ascontiguousarray(inputs["L0_attn_hidden_states"])

    nc1, _ = _get("p1", lambda: build_p1())
    base1 = dict(cattn_w=cattn_w, cattn_bp=cattn_bp, cproj_w=cproj_w,
                 cproj_b=cproj_b, wk_c=wk_c, wk_eT=wk_eT, wv_c=wv_c,
                 wv_eT=wv_eT, wg0T=wg0T, wu0T=wu0T, wd0T=wd0T,
                 wqT=wqT, wkT=wkT, wvT=wvT, woT=woT,
                 bq=bq, bk=bk, bv=bv, bo=bo,
                 ones_row=cst["ones_row"], ones_f32=cst["ones_f32"],
                 diagmask=cst["diagmask"], idn=cst["idn"],
                 madd=cst["madd"], ropecos=cst["ropecos"],
                 ropesin=cst["ropesin"])
    in_maps = [dict(base1, x0=np.ascontiguousarray(x_all[c]))
               for c in range(B)]
    r1 = run_bass_kernel_spmd(nc1, in_maps, core_ids=list(range(B)))
    x1 = [r1.results[c]["x1_out"] for c in range(B)]
    M1 = np.ascontiguousarray(
        np.stack([r1.results[c]["m1_out"][0, 0] for c in range(B)])[None, :])

    def run_stage(stage, extra, percore):
        nc, _ = _get(f"p{stage}", lambda: build_pn(stage))
        base = dict(ones_row=cst["ones_row"], ones_f32=cst["ones_f32"])
        if stage == 2:
            base.update(wgate=wgate1, wup=wup1, M1=M1)
        if stage == 5:
            base["wdown"] = wdown1
        base.update(extra)
        maps = [dict(base, **{k: v[c] for k, v in percore.items()})
                for c in range(B)]
        return run_bass_kernel_spmd(nc, maps, core_ids=list(range(B)))

    r2 = run_stage(2, {}, {"x1": x1})
    M2 = np.ascontiguousarray(
        np.stack([r2.results[c]["m_out"][0] for c in range(B)], axis=1))
    M2 = M2[[0, 1], :]
    gate_sp = [r2.results[c]["gate_sp"] for c in range(B)]
    up_sp = [r2.results[c]["up_sp"] for c in range(B)]
    r3 = run_stage(3, {"M2": M2}, {"gate_sp": gate_sp})
    M3 = np.ascontiguousarray(
        np.stack([r3.results[c]["m_out"][0, 0] for c in range(B)])[None, :])
    gq_sp = [r3.results[c]["gq_sp"] for c in range(B)]
    squ_sp = [r3.results[c]["squ_sp"] for c in range(B)]
    r4 = run_stage(4, {"M2": M2, "M3": M3},
                   {"up_sp": up_sp, "gq_sp": gq_sp, "squ_sp": squ_sp})
    M4 = np.ascontiguousarray(
        np.stack([r4.results[c]["m_out"][0, 0] for c in range(B)])[None, :])
    oraw_sp = [r4.results[c]["oraw_sp"] for c in range(B)]
    r5 = run_stage(5, {"M2": M2, "M3": M3, "M4": M4},
                   {"x1": x1, "oraw_sp": oraw_sp})
    y = np.stack([r5.results[c]["y_out"] for c in range(B)], axis=0)
    return y.astype(np.float32)


# version 09
# speedup vs baseline: 1.0163x; 1.0163x over previous
import numpy as np
import ml_dtypes
from contextlib import ExitStack

import concourse.bass as bass
import concourse.mybir as mybir
from concourse.bass_utils import run_bass_kernel_spmd

F32 = mybir.dt.float32
BF16 = mybir.dt.bfloat16
I32 = mybir.dt.int32
AF = mybir.ActivationFunctionType
ALU = mybir.AluOpType
AX = mybir.AxisListType

B, T, C = 8, 2048, 1024
H, HD, R = 16, 64, 32
FH = 3072
NT = T // 128
ND = C // 128
NH = FH // 128
S8 = 1.0 / 8.0
FSC = 1.0 / 127.0
QA = 32767.0
EPS = 1e-5
DEV = {}


class Rec:

    class Tok:
        __slots__ = ("incs",)

        def __init__(self):
            self.incs = []

        def then_inc(self, sem, n=1):
            self.incs.append((sem, n))
            return self

    def __init__(self):
        self.q = {k: [] for k in ("sync", "gpsimd", "vector", "scalar", "tensor")}

    def eng(self, name):
        rec = self

        class E:
            def __getattr__(self, op):
                def call(*a, **k):
                    tok = Rec.Tok()
                    rec.q[name].append((op, a, k, tok))
                    return tok
                return call
        return E()

    def replay(self, block):
        def mk(name):
            def run(e):
                for op, a, k, tok in self.q[name]:
                    inst = getattr(e, op)(*a, **k)
                    for sem, n in tok.incs:
                        inst.then_inc(sem, n)
            return run
        block.sync(mk("sync"))
        block.gpsimd(mk("gpsimd"))
        block.vector(mk("vector"))
        block.scalar(mk("scalar"))
        block.tensor(mk("tensor"))


class Sem:
    def __init__(self, E, nc, name):
        self.h = E(nc.semaphore(name))
        self.v = 0

    def bump(self, n=1):
        self.v += n
        return self.v


def tiled(x):
    return x.rearrange("(tt p) d -> p tt d", p=128)


def chunked(w):
    return w.rearrange("(c p) e -> p c e", p=128)


def host_constants():
    c = {}
    c["ones_row"] = np.ones((1, 128), dtype=ml_dtypes.bfloat16)
    c["ones_f32"] = np.ones((1, 128), dtype=np.float32)
    kk = np.arange(128)[:, None]
    qq = np.arange(128)[None, :]
    c["diagmask"] = (kk <= qq).astype(ml_dtypes.bfloat16)
    c["idn"] = np.eye(128, dtype=ml_dtypes.bfloat16)
    c["madd"] = np.where(kk > qq, -1e30, 0.0).astype(ml_dtypes.bfloat16)
    half = HD // 2
    theta = 1.0 / (10000.0 ** (np.arange(0, HD, 2, dtype=np.float64) / HD))
    pos = np.arange(1, T + 1, dtype=np.float64)
    ang = pos[:, None] * theta[None, :]
    ang = np.concatenate([ang, ang], axis=1)
    sgn = np.where(np.arange(HD) < half, -1.0, 1.0)
    c["ropecos"] = np.tile(np.cos(ang), (1, H)).astype(ml_dtypes.bfloat16)
    c["ropesin"] = np.tile(np.sin(ang) * sgn[None, :], (1, H)).astype(
        ml_dtypes.bfloat16)
    return c


class _VProxy:

    def __init__(self, cx, raw):
        self._cx, self._raw = cx, raw
        self._written = set()
        self._read = set()

    @staticmethod
    def _names(objs):
        out = set()
        for o in objs:
            t = getattr(o, "tensor", None)
            n = getattr(t, "name", None)
            if n is not None:
                out.add(n)
        return out

    def __getattr__(self, op):
        raw = self._raw

        def call(*a, **k):
            if op not in ("wait_ge", "engine_nop"):
                names = self._names(list(a) + list(k.values()))
                outs = []
                if a and getattr(a[0], "tensor", None) is not None:
                    outs.append(a[0])
                if "accum_out" in k and k["accum_out"] is not None:
                    outs.append(k["accum_out"])
                onames = self._names(outs)
                if (names & self._written) or (onames & self._read):
                    cx = self._cx
                    raw.engine_nop().then_inc(cx.sVE.h, 1)
                    cx.sVE.bump()
                    raw.wait_ge(cx.sVE.h, cx.sVE.v)
                    self._written.clear()
                    self._read.clear()
                self._written |= onames
                self._read |= names - onames
            return getattr(raw, op)(*a, **k)
        return call


class Cx:
    def __init__(self, nc, E):
        self.nc = nc
        self.rec = Rec()
        r = self.rec
        self.SY, self.GP, self.SC, self.TE = (
            r.eng("sync"), r.eng("gpsimd"), r.eng("scalar"), r.eng("tensor"))
        for nm in ("sLD", "sGD", "sXP", "sTE", "sVE", "sSC", "sST", "sRS"):
            setattr(self, nm, Sem(E, nc, nm))
        self.VE = _VProxy(self, r.eng("vector"))

    def w(self, eng, sem, val=None):
        eng.wait_ge(sem.h, sem.v if val is None else val)

    def gTE(self):
        self.w(self.TE, self.sSC)
        self.w(self.TE, self.sVE)

    def vfence(self):
        self.VE.engine_nop().then_inc(self.sVE.h, 1)
        self.sVE.bump()
        self.VE.wait_ge(self.sVE.h, self.sVE.v)


def emit_ln_tile(cx, P, pin, resid_ap, out_ap, resid_scale=None,
                 out_wait=None):
    VE, SC = cx.VE, cx.SC
    col, lnA, lnsq = P["col"], P["lnA"], P["lnsq"]
    cx.w(VE, cx.sTE)
    if resid_scale is None:
        VE.tensor_add(lnA[:, 0:C], pin, resid_ap).then_inc(cx.sVE.h, 1)
    else:
        VE.scalar_tensor_tensor(lnA[:, 0:C], pin, resid_scale, resid_ap,
                                ALU.mult, ALU.add).then_inc(cx.sVE.h, 1)
    cx.sVE.bump()
    pin_mark = cx.sVE.v
    VE.reduce_sum(col[:, 0:1], lnA[:, 0:C], axis=AX.X)
    VE.tensor_scalar_mul(col[:, 0:1], col[:, 0:1], 1.0 / C)
    VE.engine_nop().then_inc(cx.sVE.h, 1); cx.sVE.bump()
    cx.w(SC, cx.sVE)
    SC.activation(lnsq[:, 0:C], lnA[:, 0:C], AF.Square,
                  accum_out=col[:, 1:2]).then_inc(cx.sSC.h, 1); cx.sSC.bump()
    cx.w(VE, cx.sSC)
    VE.tensor_scalar(col[:, 2:3], col[:, 1:2], 1.0 / C, None, ALU.mult)
    VE.tensor_tensor(col[:, 3:4], col[:, 0:1], col[:, 0:1], ALU.mult)
    VE.tensor_sub(col[:, 2:3], col[:, 2:3], col[:, 3:4])
    VE.tensor_scalar_add(col[:, 2:3], col[:, 2:3], EPS)
    VE.reciprocal(col[:, 4:5], col[:, 2:3]).then_inc(cx.sVE.h, 1); cx.sVE.bump()
    cx.w(SC, cx.sVE)
    SC.activation(col[:, 5:6], col[:, 4:5], AF.Sqrt).then_inc(cx.sSC.h, 1)
    cx.sSC.bump()
    cx.w(VE, cx.sSC)
    VE.tensor_tensor(col[:, 6:7], col[:, 5:6], col[:, 5:6], ALU.mult)
    VE.tensor_tensor(col[:, 6:7], col[:, 6:7], col[:, 2:3], ALU.mult)
    VE.tensor_scalar(col[:, 6:7], col[:, 6:7], -0.5, 1.5, ALU.mult, ALU.add)
    VE.tensor_tensor(col[:, 5:6], col[:, 5:6], col[:, 6:7], ALU.mult)
    VE.tensor_scalar(col[:, 7:8], col[:, 0:1], -1.0, None, ALU.mult)
    VE.tensor_tensor(col[:, 7:8], col[:, 7:8], col[:, 5:6], ALU.mult)
    VE.engine_nop().then_inc(cx.sVE.h, 1); cx.sVE.bump()
    cx.w(SC, cx.sVE)
    for (sm, vv) in (out_wait or ()):
        cx.w(SC, sm, vv)
    SC.activation(out_ap, lnA[:, 0:C], AF.Identity,
                  bias=col[:, 7:8], scale=col[:, 5:6]).then_inc(cx.sSC.h, 1)
    cx.sSC.bump()
    return pin_mark


def emit_xT_tile(cx, P, src_ap, tt):
    buf = P["xbfr"][:, 0, :]
    cx.w(cx.VE, cx.sXP)
    cx.VE.tensor_copy(buf, src_ap).then_inc(cx.sVE.h, 1); cx.sVE.bump()
    cx.w(cx.SY, cx.sVE)
    cx.w(cx.SY, cx.sGD)
    for d in range(ND):
        cx.SY.dma_start_transpose(
            P["xT"][:, d, 128 * tt:128 * (tt + 1)],
            buf[:, 128 * d:128 * (d + 1)]).then_inc(cx.sXP.h, 16)
        cx.sXP.bump(16)


def emit_colmax_scalar(cx, P, bhbm, col_ap, out11):
    SY, VE = cx.SY, cx.VE
    cx.w(SY, cx.sVE)
    SY.dma_start(out=bhbm, in_=col_ap).then_inc(cx.sXP.h, 16)
    cx.sXP.bump(16)
    cx.w(SY, cx.sXP)
    SY.dma_start(out=P["grow"][0:1, :],
                 in_=bhbm.rearrange("a b -> b a")).then_inc(cx.sXP.h, 16)
    cx.sXP.bump(16)
    cx.w(VE, cx.sXP)
    VE.reduce_max(out11, P["grow"][0:1, :], axis=AX.X).then_inc(cx.sVE.h, 1)
    cx.sVE.bump()


def emit_bcast_cols(cx, P, src_row_ap, dst_cols_ap, n):
    cx.gTE()
    t = cx.TE.matmul(P["pA"][:, 0:n], P["onesf"][0:1, :], src_row_ap,
                     start=True, stop=True)
    t.then_inc(cx.sTE.h, 1); cx.sTE.bump()
    cx.w(cx.VE, cx.sTE)
    cx.VE.tensor_copy(dst_cols_ap, P["pA"][:, 0:n]).then_inc(cx.sVE.h, 1)
    cx.sVE.bump()


def alloc_perm(cx, E, with_xT=True):
    st, pt = cx.nc.sbuf_tensor, cx.nc.psum_tensor
    P = {}
    if with_xT:
        P["xT"] = E(st("xT", [128, ND, T], BF16))
    P["xbfr"] = E(st("xbfr", [128, 1, C], BF16))
    P["xres"] = E(st("xres", [128, 2, C], F32))
    P["xln"] = E(st("xln", [128, 2, C], F32))
    P["onesr"] = E(st("onesr", [1, 128], BF16))
    P["onesf"] = E(st("onesf", [1, 128], F32))
    P["onescol"] = E(st("onescol", [128, 1], BF16))
    P["col"] = E(st("col", [128, 16], F32))
    P["lnA"] = E(st("lnA", [128, C], F32))
    P["lnsq"] = E(st("lnsq", [128, C], BF16))
    P["runmax"] = E(st("runmax", [128, 4], F32))
    P["grow"] = E(st("grow", [1, 128], F32))
    P["srow"] = E(st("srow", [1, 64], F32))
    P["pA"] = E(pt("pA", [128, T], F32))
    P["pB"] = E(pt("pB", [128, T], F32))
    return P


def load_base_consts(cx, P, inp):
    SY, VE = cx.SY, cx.VE
    SY.dma_start(out=P["onesr"][:], in_=inp["ones_row"]).then_inc(cx.sLD.h, 16)
    cx.sLD.bump(16)
    SY.dma_start(out=P["onesf"][:], in_=inp["ones_f32"]).then_inc(cx.sLD.h, 16)
    cx.sLD.bump(16)
    cx.w(VE, cx.sLD)
    VE.memset(P["onescol"][:], 1.0)
    VE.memset(P["runmax"][:], 0.0)
    VE.engine_nop().then_inc(cx.sVE.h, 1); cx.sVE.bump()


def build_p1(stop_after=None):
    nc = bass.Bass("TRN2", target_bir_lowering=False)
    dt = nc.dram_tensor
    inp = {}

    def I(name, shape, dtyp=F32):
        inp[name] = dt(name, shape, dtyp, kind="ExternalInput").ap()
        return inp[name]

    x0 = I("x0", [T, C])
    cattn_w = I("cattn_w", [C, 3 * C])
    cattn_bp = I("cattn_bp", [128, 24])
    cproj_w = I("cproj_w", [C, C])
    cproj_b = I("cproj_b", [1, C])
    wk_c = I("wk_c", [HD, R]); wk_eT = I("wk_eT", [HD, R])
    wv_c = I("wv_c", [HD, R]); wv_eT = I("wv_eT", [HD, R])
    wg0T = I("wg0T", [C, FH], I32)
    wu0T = I("wu0T", [C, FH], I32)
    wd0T = I("wd0T", [FH, C], I32)
    wqT = I("wqT", [C, C]); wkT = I("wkT", [C, C])
    wvT = I("wvT", [C, C]); woT = I("woT", [C, C])
    bqr = I("bq", [1, C]); bkr = I("bk", [1, C])
    bvr = I("bv", [1, C]); bor = I("bo", [1, C])
    I("ones_row", [1, 128], BF16)
    I("ones_f32", [1, 128])
    diagmask = I("diagmask", [128, 128], BF16)
    idn_h = I("idn", [128, 128], BF16)
    madd_h = I("madd", [128, 128], BF16)
    ropecos = I("ropecos", [T, C], BF16)
    ropesin = I("ropesin", [T, C], BF16)

    x1_out = dt("x1_out", [T, C], F32, kind="ExternalOutput").ap()
    m1_out = dt("m1_out", [1, 1], F32, kind="ExternalOutput").ap()
    xspillA = dt("xspillA", [T, C], F32).ap()
    xspillB = dt("xspillB", [T, C], F32).ap()
    bhbm = dt("bhbm", [128, 1], F32).ap()

    with ExitStack() as perm:
        E = perm.enter_context
        st = nc.sbuf_tensor
        cx = Cx(nc, E)
        SY, GP, VE, SC, TE = cx.SY, cx.GP, cx.VE, cx.SC, cx.TE
        P = alloc_perm(cx, E)
        pA, pB = P["pA"], P["pB"]
        onesr, onesf, onescol = P["onesr"], P["onesf"], P["onescol"]
        srow = P["srow"]
        dmask = E(st("dmask", [128, 128], BF16))
        idn = E(st("idn_s", [128, 128], BF16))
        madd = E(st("madd_s", [128, 128], BF16))
        negsB = E(st("negsB", [128, H], F32))

        SY.dma_start(out=dmask[:], in_=diagmask).then_inc(cx.sLD.h, 16)
        cx.sLD.bump(16)
        SY.dma_start(out=idn[:], in_=idn_h).then_inc(cx.sLD.h, 16)
        cx.sLD.bump(16)
        SY.dma_start(out=madd[:], in_=madd_h).then_inc(cx.sLD.h, 16)
        cx.sLD.bump(16)
        load_base_consts(cx, P, inp)

        for tt in range(NT):
            bb = tt % 2
            cx.w(SY, cx.sVE)
            SY.dma_start(out=P["xres"][:, bb, :],
                         in_=tiled(x0)[:, tt, :]).then_inc(cx.sLD.h, 16)
            cx.sLD.bump(16)
            cx.w(VE, cx.sLD)
            emit_xT_tile(cx, P, P["xres"][:, bb, :], tt)
        xT0_xp = cx.sXP.v

        with ExitStack() as ph:
            Ep = ph.enter_context
            WAr = Ep(st("WAr", [128, 2, 3, ND, 128], BF16))
            WP = Ep(st("WP", [128, ND, C], BF16))
            bias24 = Ep(st("bias24", [128, 24], F32))
            w64 = Ep(st("w64", [128, 4 * R], BF16))
            browp = Ep(st("browp", [1, C], BF16))
            qT2 = Ep(st("qT2", [128, T], BF16))
            kT2 = Ep(st("kT2", [128, T], BF16))
            vT2 = Ep(st("vT2", [128, T], BF16))
            qpT = Ep(st("qpT", [128, 4, T], BF16))
            klT = Ep(st("klT", [128, 4, T], BF16))
            atT = Ep(st("atT", [128, 4, T], BF16))
            atst = Ep(st("atst", [32, T], BF16))
            vones = Ep(st("vones", [128, H, NT, R + 1], BF16))
            sqs = Ep(st("sqs", [128, T], BF16))
            probsT = Ep(st("probsT", [128, T], BF16))
            Weff = Ep(st("Weff", [128, 4, C], BF16))
            reciprow = Ep(st("reciprow", [1, T], F32))
            recipbf = Ep(st("recipbf", [1, T], BF16))

            GP.dma_start(out=WP[:], in_=chunked(cproj_w)).then_inc(cx.sGD.h, 16)
            cx.sGD.bump(16)
            for half64 in range(2):
                for i, wsrc in enumerate((wk_c, wk_eT, wv_c, wv_eT)):
                    GP.dma_start(out=w64[64 * half64:64 * (half64 + 1),
                                         R * i:R * (i + 1)],
                                 in_=wsrc).then_inc(cx.sGD.h, 16)
                    cx.sGD.bump(16)
            GP.dma_start(out=browp[:], in_=cproj_b).then_inc(cx.sGD.h, 16)
            cx.sGD.bump(16)
            SY.dma_start(out=bias24[:], in_=cattn_bp).then_inc(cx.sLD.h, 16)
            cx.sLD.bump(16)
            a_ld, a_gd = cx.sLD.v, cx.sGD.v
            VE.memset(vones[:, :, :, R:R + 1], 1.0)
            VE.engine_nop().then_inc(cx.sVE.h, 1); cx.sVE.bump()

            te_qkv = {}
            for hp in range(DEV.get("n_hp", 8)):
                heads = (2 * hp, 2 * hp + 1)
                bbw = hp % 2
                if hp >= 2:
                    GP.wait_ge(cx.sTE.h, te_qkv[hp - 2])
                for which, ech in enumerate((hp, 8 + hp, 16 + hp)):
                    GP.dma_start(
                        out=WAr[:, bbw, which, :, :],
                        in_=cattn_w[:, 128 * ech:128 * (ech + 1)].rearrange(
                            "(c p) e -> p c e", p=128)).then_inc(cx.sGD.h, 16)
                    cx.sGD.bump(16)
                war_gd = cx.sGD.v
                for which, (dst, ech) in enumerate(
                        [(qT2, hp), (kT2, 8 + hp), (vT2, 16 + hp)]):
                    p = pA if which % 2 == 0 else pB
                    cx.gTE()
                    cx.w(TE, cx.sGD, war_gd)
                    if hp == 0 and which == 0:
                        cx.w(TE, cx.sXP, xT0_xp)
                        cx.w(TE, cx.sLD, a_ld)
                    for b in range(4):
                        for d in range(ND):
                            t = TE.matmul(
                                p[:, 512 * b:512 * (b + 1)],
                                WAr[:, bbw, which, d, :],
                                P["xT"][:, d, 512 * b:512 * (b + 1)],
                                start=(d == 0), stop=(d == ND - 1))
                    t.then_inc(cx.sTE.h, 1); cx.sTE.bump()
                    if which == 2:
                        te_qkv[hp] = cx.sTE.v
                    cx.w(SC, cx.sTE)
                    SC.activation(dst[:], p[:], AF.Identity,
                                  bias=bias24[:, ech:ech + 1]).then_inc(
                        cx.sSC.h, 1); cx.sSC.bump()

                for hi, h in enumerate(heads):
                    if "qk" in DEV.get("skip", ()):
                        break
                    rows = 64 * (h % 2)
                    g, cc = h % 4, h // 4
                    gb = 32 * g
                    for which, (srcT, wof, dstT) in enumerate(
                            [(qT2, R, qpT), (kT2, 0, klT)]):
                        p = pA if which == 0 else pB
                        cx.gTE()
                        for b in range(4):
                            t = TE.matmul(
                                p[gb:gb + 32, 512 * b:512 * (b + 1)],
                                w64[rows:rows + 64, wof:wof + R],
                                srcT[rows:rows + 64, 512 * b:512 * (b + 1)],
                                start=True, stop=True,
                                tile_position=(rows, gb))
                        t.then_inc(cx.sTE.h, 1); cx.sTE.bump()
                        cx.w(SC, cx.sTE)
                        SC.activation(dstT[gb:gb + 32, cc, :],
                                      p[gb:gb + 32, :],
                                      AF.Identity).then_inc(cx.sSC.h, 1)
                        cx.sSC.bump()
                        SC.activation(sqs[gb:gb + 32, :], p[gb:gb + 32, :],
                                      AF.Square).then_inc(cx.sSC.h, 1)
                        cx.sSC.bump()
                        cx.gTE()
                        for b in range(4):
                            t = TE.matmul(
                                p[0:1, 512 * b:512 * (b + 1)],
                                onescol[gb:gb + 32, 0:1],
                                sqs[gb:gb + 32, 512 * b:512 * (b + 1)],
                                start=True, stop=True,
                                tile_position=(gb, 0))
                        t.then_inc(cx.sTE.h, 1); cx.sTE.bump()
                        cx.w(VE, cx.sTE)
                        idx = 2 * hi + which
                        VE.reduce_max(srow[0:1, idx:idx + 1], p[0:1, 0:T],
                                      axis=AX.X).then_inc(cx.sVE.h, 1)
                        cx.sVE.bump()

                cx.w(VE, cx.sVE)
                VE.tensor_mul(
                    srow[0:1, 4:6],
                    srow[0:1, 0:4].rearrange("a (h k) -> a h k", k=2)[:, :, 0],
                    srow[0:1, 0:4].rearrange("a (h k) -> a h k", k=2)[:, :, 1])
                VE.engine_nop().then_inc(cx.sVE.h, 1); cx.sVE.bump()
                cx.w(SC, cx.sVE)
                SC.activation(srow[0:1, 6:8], srow[0:1, 4:6],
                              AF.Sqrt).then_inc(cx.sSC.h, 1); cx.sSC.bump()
                cx.w(VE, cx.sSC)
                VE.tensor_scalar_mul(srow[0:1, 8:10], srow[0:1, 6:8], -S8)
                VE.engine_nop().then_inc(cx.sVE.h, 1); cx.sVE.bump()
                emit_bcast_cols(cx, P, srow[0:1, 8:10],
                                negsB[:, 2 * hp:2 * hp + 2], 2)

                for hi, h in enumerate(heads):
                    if "vlat" in DEV.get("skip", ()):
                        break
                    rows = 64 * (h % 2)
                    cx.gTE()
                    for tt in range(NT):
                        t = TE.matmul(
                            pA[:, 32 * tt:32 * (tt + 1)],
                            vT2[rows:rows + 64, 128 * tt:128 * (tt + 1)],
                            w64[rows:rows + 64, 2 * R:3 * R],
                            start=True, stop=True,
                            tile_position=(rows, 0))
                    t.then_inc(cx.sTE.h, 1); cx.sTE.bump()
                    cx.w(VE, cx.sTE)
                    VE.tensor_copy(
                        vones[:, h, :, 0:R],
                        pA[:, 0:512].rearrange("p (tt r) -> p tt r", r=R),
                    ).then_inc(cx.sVE.h, 1); cx.sVE.bump()

                for hi, h in enumerate(heads):
                    if "scores" in DEV.get("skip", ()):
                        break
                    g, cc = h % 4, h // 4
                    gb = 32 * g
                    for kt in range(NT):
                        q0 = 128 * kt
                        b0 = q0 // 512
                        cx.gTE()
                        segv = {}
                        for b in range(b0, 4):
                            lo, hi2 = max(512 * b, q0), 512 * (b + 1)
                            t = TE.matmul(pA[:, lo:hi2],
                                          klT[gb:gb + 32, cc, q0:q0 + 128],
                                          qpT[gb:gb + 32, cc, lo:hi2],
                                          start=True, stop=True,
                                          tile_position=(gb, 0))
                            if b == b0:
                                t = TE.matmul(pA[:, q0:q0 + 128], idn[:, :],
                                              madd[:, :], start=False,
                                              stop=True,
                                              skip_group_check=True)
                            t.then_inc(cx.sTE.h, 1); cx.sTE.bump()
                            segv[b] = cx.sTE.v
                        expv = {}
                        for b in range(b0, 4):
                            lo, hi2 = max(512 * b, q0), 512 * (b + 1)
                            SC.wait_ge(cx.sTE.h, segv[b])
                            SC.activation(probsT[:, lo:hi2], pA[:, lo:hi2],
                                          AF.Exp, bias=negsB[:, h:h + 1],
                                          scale=S8).then_inc(cx.sSC.h, 1)
                            cx.sSC.bump()
                            expv[b] = cx.sSC.v
                        for b in range(b0, 4):
                            lo, hi2 = max(512 * b, q0), 512 * (b + 1)
                            TE.wait_ge(cx.sSC.h, expv[b])
                            t = TE.matmul(
                                pB[0:R + 1, lo:hi2], vones[:, h, kt, :],
                                probsT[:, lo:hi2],
                                start=(kt == 0),
                                stop=(kt == min(4 * b + 3, NT - 1)),
                                skip_group_check=True)
                            t.then_inc(cx.sTE.h, 1); cx.sTE.bump()
                    cx.w(VE, cx.sTE)
                    VE.reciprocal(reciprow[0:1, :], pB[R:R + 1, :])
                    cx.vfence()
                    VE.tensor_copy(recipbf[0:1, :],
                                   reciprow[0:1, :]).then_inc(cx.sVE.h, 1)
                    cx.sVE.bump()
                    cx.gTE()
                    for b in range(4):
                        t = TE.matmul(pA[0:R, 512 * b:512 * (b + 1)],
                                      onesr[:, 0:R],
                                      recipbf[0:1, 512 * b:512 * (b + 1)],
                                      start=True, stop=True)
                    t.then_inc(cx.sTE.h, 1); cx.sTE.bump()
                    cx.w(SC, cx.sTE)
                    cx.w(SC, cx.sGD)
                    cx.w(SC, cx.sVE)
                    SC.activation(atst[:, :], pB[0:R, :],
                                  AF.Identity).then_inc(cx.sSC.h, 1)
                    cx.sSC.bump()
                    cx.w(VE, cx.sSC)
                    VE.tensor_mul(atst[:, :], atst[:, :],
                                  pA[0:R, :]).then_inc(cx.sVE.h, 1)
                    cx.sVE.bump()
                    cx.w(GP, cx.sVE)
                    GP.dma_start(out=atT[gb:gb + 32, cc, :],
                                 in_=atst[:, :]).then_inc(cx.sGD.h, 16)
                    cx.sGD.bump(16)

            for h in range(0 if "weff" in DEV.get("skip", ()) else H):
                g, cc = h % 4, h // 4
                gb = 32 * g
                rc, ro = (h * HD) // 128, (h * HD) % 128
                cx.gTE()
                for b in range(2):
                    t = TE.matmul(pA[gb:gb + 32, 512 * b:512 * (b + 1)],
                                  w64[ro:ro + 64, 3 * R:4 * R],
                                  WP[ro:ro + 64, rc, 512 * b:512 * (b + 1)],
                                  start=True, stop=True,
                                  tile_position=(ro, gb))
                t.then_inc(cx.sTE.h, 1); cx.sTE.bump()
                cx.w(VE, cx.sTE)
                VE.tensor_copy(Weff[gb:gb + 32, cc, :],
                               pA[gb:gb + 32, 0:C]).then_inc(cx.sVE.h, 1)
                cx.sVE.bump()

            for tt in range(NT):
                bb = tt % 2
                cx.w(SY, cx.sVE)
                SY.dma_start(out=P["xres"][:, bb, :],
                             in_=tiled(x0)[:, tt, :]).then_inc(cx.sRS.h, 16)
                cx.sRS.bump(16)
                cx.gTE()
                cx.w(TE, cx.sGD)
                first = True
                if "proj" not in DEV.get("skip", ()):
                    for cc in range(4):
                        for b in range(2):
                            TE.matmul(pB[:, 512 * b:512 * (b + 1)],
                                      atT[:, cc, 128 * tt:128 * (tt + 1)],
                                      Weff[:, cc, 512 * b:512 * (b + 1)],
                                      start=first, stop=False,
                                      skip_group_check=True)
                        first = False
                cx.w(TE, cx.sRS)
                for b in range(2):
                    t = TE.matmul(pB[:, 512 * b:512 * (b + 1)], onesr[:, :],
                                  browp[0:1, 512 * b:512 * (b + 1)],
                                  start=(first and b == 0), stop=True,
                                  skip_group_check=True)
                t.then_inc(cx.sTE.h, 1); cx.sTE.bump()
                emit_ln_tile(cx, P, pB[:, 0:C], P["xres"][:, bb, :],
                             P["xln"][:, bb, :],
                             out_wait=[(cx.sXP, cx.sXP.v)])
                cx.w(SY, cx.sSC)
                SY.dma_start(out=tiled(xspillA)[:, tt, :],
                             in_=P["xln"][:, bb, :]).then_inc(cx.sXP.h, 16)
                cx.sXP.bump(16)
                if stop_after != "l0attn":
                    emit_xT_tile(cx, P, P["xln"][:, bb, :], tt)

        if stop_after == "l0attn":
            _emit_p1_tail(cx, P, xspillA, x1_out, m1_out, bhbm)
            blk = perm.enter_context(nc.Block())
            cx.rec.replay(blk)
            return nc, inp

        with ExitStack() as ph:
            Ep = ph.enter_context
            g2 = Ep(st("g2", [128, NH, T], BF16))
            with ExitStack() as ph2:
                Ep2 = ph2.enter_context
                wring = Ep2(st("wring", [128, 2, 2, ND, 128], BF16))
                silu = Ep2(st("silu", [128, T], F32))
                sW = [Sem(Ep2, nc, "sW0"), Sem(Ep2, nc, "sW1")]
                te_after = {}
                for hh in range(NH):
                    bb = hh % 2
                    if hh >= 2:
                        GP.wait_ge(cx.sTE.h, te_after[hh - 2])
                    GP.dma_start(
                        out=wring[:, bb, 0, :, :],
                        in_=inp["wg0T"][:, 128 * hh:128 * (hh + 1)].rearrange(
                            "(c p) h -> p c h", p=128)).then_inc(
                        sW[bb].h, 16); sW[bb].bump(16)
                    GP.dma_start(
                        out=wring[:, bb, 1, :, :],
                        in_=inp["wu0T"][:, 128 * hh:128 * (hh + 1)].rearrange(
                            "(c p) h -> p c h", p=128)).then_inc(
                        sW[bb].h, 16); sW[bb].bump(16)
                    cx.gTE()
                    cx.w(TE, sW[bb])
                    if hh == 0:
                        cx.w(TE, cx.sXP, cx.sXP.v)
                    for w_i, p in ((0, pA), (1, pB)):
                        for b in range(4):
                            for d in range(ND):
                                t = TE.matmul(
                                    p[:, 512 * b:512 * (b + 1)],
                                    wring[:, bb, w_i, d, :],
                                    P["xT"][:, d, 512 * b:512 * (b + 1)],
                                    start=(d == 0), stop=(d == ND - 1))
                        t.then_inc(cx.sTE.h, 1); cx.sTE.bump()
                    te_after[hh] = cx.sTE.v
                    cx.w(SC, cx.sTE)
                    SC.activation(silu[:, :], pA[:, :], AF.Silu,
                                  scale=FSC).then_inc(cx.sSC.h, 1)
                    cx.sSC.bump()
                    cx.w(VE, cx.sSC)
                    VE.tensor_mul(g2[:, hh, :], silu[:, :],
                                  pB[:, :]).then_inc(cx.sVE.h, 1); cx.sVE.bump()

            wdsb = Ep(st("wdsb", [128, NH, C], BF16))
            cx.w(GP, cx.sTE); cx.w(GP, cx.sSC); cx.w(GP, cx.sVE)
            GP.dma_start(out=wdsb[:], in_=chunked(inp["wd0T"])).then_inc(
                cx.sGD.h, 16); cx.sGD.bump(16)
            wd_gd = cx.sGD.v
            for tt in range(NT):
                bb = tt % 2
                cx.w(SY, cx.sVE)
                cx.w(SY, cx.sXP)
                SY.dma_start(out=P["xres"][:, bb, :],
                             in_=tiled(xspillA)[:, tt, :]).then_inc(
                    cx.sRS.h, 16); cx.sRS.bump(16)
                cx.gTE()
                cx.w(TE, cx.sRS)
                if tt == 0:
                    cx.w(TE, cx.sGD, wd_gd)
                for hh in range(NH):
                    for b in range(2):
                        t = TE.matmul(pB[:, 512 * b:512 * (b + 1)],
                                      g2[:, hh, 128 * tt:128 * (tt + 1)],
                                      wdsb[:, hh, 512 * b:512 * (b + 1)],
                                      start=(hh == 0), stop=(hh == NH - 1),
                                      skip_group_check=True)
                t.then_inc(cx.sTE.h, 1); cx.sTE.bump()
                emit_ln_tile(cx, P, pB[:, 0:C], P["xres"][:, bb, :],
                             P["xln"][:, bb, :], resid_scale=FSC * FSC,
                             out_wait=[(cx.sXP, cx.sXP.v)])
                cx.w(SY, cx.sSC)
                SY.dma_start(out=tiled(xspillB)[:, tt, :],
                             in_=P["xln"][:, bb, :]).then_inc(cx.sXP.h, 16)
                cx.sXP.bump(16)
                if stop_after != "l0ffn":
                    emit_xT_tile(cx, P, P["xln"][:, bb, :], tt)

        if stop_after == "l0ffn":
            _emit_p1_tail(cx, P, xspillB, x1_out, m1_out, bhbm)
            blk = perm.enter_context(nc.Block())
            cx.rec.replay(blk)
            return nc, inp

        with ExitStack() as ph:
            Ep = ph.enter_context
            Wbuf = Ep(st("Wbuf", [128, ND, C], BF16))
            qroT = Ep(st("qroT", [128, 4, T], BF16))
            kroT = Ep(st("kroT", [128, 4, T], BF16))
            von1 = Ep(st("von1", [128, NT, 8, HD + 1], BF16))
            avT = Ep(st("avT", [128, ND, T], BF16))
            avst = Ep(st("avst", [64, T], BF16))
            probsT = Ep(st("probsT1", [128, T], BF16))
            qtile = Ep(st("qtile", [128, 512], BF16))
            rtmp = Ep(st("rtmp", [128, 512], BF16))
            rcos = Ep(st("rcos", [128, 1, C], BF16))
            rsin = Ep(st("rsin", [128, 1, C], BF16))
            nsq = Ep(st("nsq", [128, 512], F32))
            nstat = Ep(st("nstat", [128, 16], F32))
            brow4 = Ep(st("brow4", [1, 4, C], BF16))
            growL = Ep(st("growL", [1, 2048], F32))
            reciprow = Ep(st("reciprow1", [1, T], F32))
            recipbf = Ep(st("recipbf1", [1, T], BF16))
            ghbm = nc.dram_tensor("ghbm16", [128, 16], F32).ap()

            cx.w(GP, cx.sTE); cx.w(GP, cx.sSC); cx.w(GP, cx.sVE)
            for i, bsrc in enumerate((bqr, bkr, bvr, bor)):
                GP.dma_start(out=brow4[:, i, :], in_=bsrc).then_inc(
                    cx.sGD.h, 16); cx.sGD.bump(16)
            brow_gd = cx.sGD.v

            for half in range(2):
                hh0 = 8 * half
                cx.w(VE, cx.sSC)
                VE.memset(nstat[:, :], 0.0)
                VE.memset(von1[:, :, :, HD:HD + 1], 1.0)
                VE.engine_nop().then_inc(cx.sVE.h, 1); cx.sVE.bump()
                for wi, (wsrc, brix) in enumerate(
                        [(wqT, 0), (wkT, 1), (wvT, 2)]):
                    cx.w(GP, cx.sTE)
                    GP.dma_start(out=Wbuf[:, :, :],
                                 in_=chunked(wsrc)).then_inc(cx.sGD.h, 16)
                    cx.sGD.bump(16)
                    w_gd = cx.sGD.v
                    for tt in range(NT):
                        rb = 0
                        if wi != 2:
                            cx.w(SY, cx.sVE)
                            SY.dma_start(
                                out=rcos[:, rb, :],
                                in_=tiled(ropecos)[:, tt, :]).then_inc(
                                cx.sRS.h, 16); cx.sRS.bump(16)
                            SY.dma_start(
                                out=rsin[:, rb, :],
                                in_=tiled(ropesin)[:, tt, :]).then_inc(
                                cx.sRS.h, 16); cx.sRS.bump(16)
                        rs_here = cx.sRS.v
                        cx.gTE()
                        if tt == 0:
                            cx.w(TE, cx.sGD, w_gd)
                            if wi == 0 and half == 0:
                                cx.w(TE, cx.sXP, cx.sXP.v)
                        for d in range(ND):
                            TE.matmul(
                                pA[:, 0:512],
                                P["xT"][:, d, 128 * tt:128 * (tt + 1)],
                                Wbuf[:, d, 512 * half:512 * (half + 1)],
                                start=(d == 0), stop=False,
                                skip_group_check=True)
                        t = TE.matmul(pA[:, 0:512], onesr[:, :],
                                      brow4[0:1, brix,
                                            512 * half:512 * (half + 1)],
                                      start=False, stop=True,
                                      skip_group_check=True)
                        t.then_inc(cx.sTE.h, 1); cx.sTE.bump()
                        if wi == 2:
                            cx.w(SC, cx.sTE)
                            SC.activation(
                                von1[:, tt, :, 0:HD],
                                pA[:, 0:512].rearrange(
                                    "p (h d) -> p h d", d=HD),
                                AF.Identity).then_inc(
                                cx.sSC.h, 1); cx.sSC.bump()
                            continue
                        cx.w(SC, cx.sTE)
                        cx.w(SC, cx.sXP)
                        SC.activation(qtile[:, :], pA[:, 0:512],
                                      AF.Identity).then_inc(cx.sSC.h, 1)
                        cx.sSC.bump()
                        qh = qtile[:, :].rearrange("p (h d) -> p h d", d=HD)
                        th = rtmp[:, :].rearrange("p (h d) -> p h d", d=HD)
                        cs = rcos[:, rb, 512 * half:512 * (half + 1)]
                        sn = rsin[:, rb, 512 * half:512 * (half + 1)].rearrange(
                            "p (h d) -> p h d", d=HD)
                        cx.w(VE, cx.sSC)
                        cx.w(VE, cx.sRS, rs_here)
                        VE.tensor_mul(th[:, :, 0:32], qh[:, :, 32:64],
                                      sn[:, :, 0:32])
                        VE.tensor_mul(th[:, :, 32:64], qh[:, :, 0:32],
                                      sn[:, :, 32:64])
                        VE.tensor_mul(qtile[:, :], qtile[:, :], cs)
                        VE.tensor_add(qtile[:, :], qtile[:, :], rtmp[:, :])
                        VE.tensor_mul(nsq[:, :], qtile[:, :], qtile[:, :])
                        VE.reduce_sum(
                            P["col"][:, 8:16],
                            nsq[:, :].rearrange("p (h d) -> p h d", d=HD),
                            axis=AX.X)
                        VE.tensor_max(nstat[:, 8 * wi:8 * (wi + 1)],
                                      nstat[:, 8 * wi:8 * (wi + 1)],
                                      P["col"][:, 8:16])
                        VE.engine_nop().then_inc(cx.sVE.h, 1); cx.sVE.bump()
                        dstT = qroT if wi == 0 else kroT
                        cx.w(SY, cx.sVE)
                        cx.w(SY, cx.sGD)
                        for d4 in range(4):
                            SY.dma_start_transpose(
                                dstT[:, d4, 128 * tt:128 * (tt + 1)],
                                qtile[:, 128 * d4:128 * (d4 + 1)]).then_inc(
                                cx.sXP.h, 16)
                            cx.sXP.bump(16)

                cx.w(SY, cx.sVE)
                SY.dma_start(out=ghbm, in_=nstat[:, :]).then_inc(cx.sXP.h, 16)
                cx.sXP.bump(16)
                cx.w(SY, cx.sXP)
                SY.dma_start(out=growL[0:1, :],
                             in_=ghbm.rearrange("p s -> (p s)").unsqueeze(0)).then_inc(
                    cx.sXP.h, 16); cx.sXP.bump(16)
                cx.w(VE, cx.sXP)
                VE.reduce_max(srow[0:1, 0:16],
                              growL[0:1, :].rearrange("a (p s) -> a s p", s=16),
                              axis=AX.X)
                VE.engine_nop().then_inc(cx.sVE.h, 1); cx.sVE.bump()
                cx.w(VE, cx.sVE)
                VE.tensor_mul(srow[0:1, 16:24], srow[0:1, 0:8],
                              srow[0:1, 8:16])
                VE.engine_nop().then_inc(cx.sVE.h, 1); cx.sVE.bump()
                cx.w(SC, cx.sVE)
                SC.activation(srow[0:1, 24:32], srow[0:1, 16:24],
                              AF.Sqrt).then_inc(cx.sSC.h, 1); cx.sSC.bump()
                cx.w(VE, cx.sSC)
                VE.tensor_scalar_mul(srow[0:1, 32:40], srow[0:1, 24:32], -S8)
                VE.engine_nop().then_inc(cx.sVE.h, 1); cx.sVE.bump()
                emit_bcast_cols(cx, P, srow[0:1, 32:40],
                                negsB[:, hh0:hh0 + 8], 8)

                for hl in range(8):
                    h = hh0 + hl
                    ch, rows = hl // 2, 64 * (hl % 2)
                    for kt in range(NT):
                        q0 = 128 * kt
                        b0 = q0 // 512
                        cx.gTE()
                        segv = {}
                        for b in range(b0, 4):
                            lo, hi2 = max(512 * b, q0), 512 * (b + 1)
                            t = TE.matmul(pA[:, lo:hi2],
                                          kroT[rows:rows + 64, ch, q0:q0 + 128],
                                          qroT[rows:rows + 64, ch, lo:hi2],
                                          start=True, stop=True,
                                          tile_position=(rows, 0))
                            if b == b0:
                                t = TE.matmul(pA[:, q0:q0 + 128], idn[:, :],
                                              madd[:, :], start=False,
                                              stop=True,
                                              skip_group_check=True)
                            t.then_inc(cx.sTE.h, 1); cx.sTE.bump()
                            segv[b] = cx.sTE.v
                        expv = {}
                        for b in range(b0, 4):
                            lo, hi2 = max(512 * b, q0), 512 * (b + 1)
                            SC.wait_ge(cx.sTE.h, segv[b])
                            SC.activation(probsT[:, lo:hi2], pA[:, lo:hi2],
                                          AF.Exp, bias=negsB[:, h:h + 1],
                                          scale=S8).then_inc(cx.sSC.h, 1)
                            cx.sSC.bump()
                            expv[b] = cx.sSC.v
                        for b in range(b0, 4):
                            lo, hi2 = max(512 * b, q0), 512 * (b + 1)
                            TE.wait_ge(cx.sSC.h, expv[b])
                            t = TE.matmul(
                                pB[0:HD + 1, lo:hi2],
                                von1[:, kt, hl, :],
                                probsT[:, lo:hi2],
                                start=(kt == 0),
                                stop=(kt == min(4 * b + 3, NT - 1)),
                                skip_group_check=True)
                            t.then_inc(cx.sTE.h, 1); cx.sTE.bump()
                    cx.w(VE, cx.sTE)
                    VE.reciprocal(reciprow[0:1, :], pB[HD:HD + 1, :])
                    cx.vfence()
                    VE.tensor_copy(recipbf[0:1, :],
                                   reciprow[0:1, :]).then_inc(cx.sVE.h, 1)
                    cx.sVE.bump()
                    cx.gTE()
                    for b in range(4):
                        t = TE.matmul(pA[0:HD, 512 * b:512 * (b + 1)],
                                      onesr[:, 0:HD],
                                      recipbf[0:1, 512 * b:512 * (b + 1)],
                                      start=True, stop=True)
                    t.then_inc(cx.sTE.h, 1); cx.sTE.bump()
                    cx.w(SC, cx.sTE)
                    cx.w(SC, cx.sGD)
                    cx.w(SC, cx.sVE)
                    SC.activation(avst[:, :], pB[0:HD, :],
                                  AF.Identity).then_inc(cx.sSC.h, 1)
                    cx.sSC.bump()
                    cx.w(VE, cx.sSC)
                    VE.tensor_mul(avst[:, :], avst[:, :],
                                  pA[0:HD, :]).then_inc(cx.sVE.h, 1)
                    cx.sVE.bump()
                    cx.w(GP, cx.sVE)
                    GP.dma_start(out=avT[rows:rows + 64, ch + 4 * half, :],
                                 in_=avst[:, :]).then_inc(cx.sGD.h, 16)
                    cx.sGD.bump(16)

            cx.w(GP, cx.sTE)
            GP.dma_start(out=Wbuf[:, :, :],
                         in_=chunked(woT)).then_inc(cx.sGD.h, 16)
            cx.sGD.bump(16)
            wo_gd = cx.sGD.v
            for tt in range(NT):
                bb = tt % 2
                cx.w(SY, cx.sVE)
                cx.w(SY, cx.sXP)
                SY.dma_start(out=P["xres"][:, bb, :],
                             in_=tiled(xspillB)[:, tt, :]).then_inc(
                    cx.sRS.h, 16); cx.sRS.bump(16)
                cx.gTE()
                if tt == 0:
                    cx.w(TE, cx.sGD, wo_gd)
                for b in range(2):
                    for dd in range(ND):
                        ch, rows = dd % 4, 64 * (dd // 4)
                        TE.matmul(pB[:, 512 * b:512 * (b + 1)],
                                  avT[:, dd, 128 * tt:128 * (tt + 1)],
                                  Wbuf[:, dd, 512 * b:512 * (b + 1)],
                                  start=(dd == 0), stop=False,
                                  skip_group_check=True)
                cx.w(TE, cx.sRS)
                for b in range(2):
                    t = TE.matmul(pB[:, 512 * b:512 * (b + 1)], onesr[:, :],
                                  brow4[0:1, 3, 512 * b:512 * (b + 1)],
                                  start=False, stop=True,
                                  skip_group_check=True)
                t.then_inc(cx.sTE.h, 1); cx.sTE.bump()
                emit_ln_tile(cx, P, pB[:, 0:C], P["xres"][:, bb, :],
                             P["xln"][:, bb, :],
                             out_wait=[(cx.sXP, cx.sXP.v), (cx.sST, cx.sST.v)])
                cx.w(SY, cx.sSC)
                SY.dma_start(out=tiled(x1_out)[:, tt, :],
                             in_=P["xln"][:, bb, :]).then_inc(cx.sST.h, 16)
                cx.sST.bump(16)
                cx.w(VE, cx.sSC)
                VE.reduce_max(P["col"][:, 0:1], P["xln"][:, bb, :], axis=AX.X,
                              apply_absolute_value=True)
                VE.tensor_max(P["runmax"][:, 0:1], P["runmax"][:, 0:1],
                              P["col"][:, 0:1]).then_inc(cx.sVE.h, 1)
                cx.sVE.bump()

        emit_colmax_scalar(cx, P, bhbm, P["runmax"][:, 0:1], srow[0:1, 0:1])
        cx.w(SY, cx.sVE)
        SY.dma_start(out=m1_out, in_=srow[0:1, 0:1]).then_inc(cx.sST.h, 16)
        cx.sST.bump(16)
        cx.w(GP, cx.sST)
        blk = perm.enter_context(nc.Block())
        cx.rec.replay(blk)
        return nc, inp


def _emit_p1_tail(cx, P, xsp, x1_out, m1_out, bhbm):
    SY, VE, GP = cx.SY, cx.VE, cx.GP
    for tt in range(NT):
        bb = tt % 2
        cx.w(SY, cx.sVE)
        cx.w(SY, cx.sST)
        cx.w(SY, cx.sXP)
        SY.dma_start(out=P["xres"][:, bb, :],
                     in_=tiled(xsp)[:, tt, :]).then_inc(cx.sRS.h, 16)
        cx.sRS.bump(16)
        cx.w(SY, cx.sRS)
        SY.dma_start(out=tiled(x1_out)[:, tt, :],
                     in_=P["xres"][:, bb, :]).then_inc(cx.sST.h, 16)
        cx.sST.bump(16)
        cx.w(VE, cx.sRS)
        VE.reduce_max(P["col"][:, 0:1], P["xres"][:, bb, :], axis=AX.X,
                      apply_absolute_value=True)
        VE.tensor_max(P["runmax"][:, 0:1], P["runmax"][:, 0:1],
                      P["col"][:, 0:1]).then_inc(cx.sVE.h, 1)
        cx.sVE.bump()
    emit_colmax_scalar(cx, P, bhbm, P["runmax"][:, 0:1], P["srow"][0:1, 0:1])
    cx.w(SY, cx.sVE)
    SY.dma_start(out=m1_out, in_=P["srow"][0:1, 0:1]).then_inc(cx.sST.h, 16)
    cx.sST.bump(16)
    cx.w(GP, cx.sST)


def build_pn(stage):
    nc = bass.Bass("TRN2", target_bir_lowering=False)
    dt = nc.dram_tensor
    inp = {}

    def I(name, shape, dtyp=F32):
        inp[name] = dt(name, shape, dtyp, kind="ExternalInput").ap()
        return inp[name]

    def O(name, shape, dtyp=F32):
        return dt(name, shape, dtyp, kind="ExternalOutput").ap()

    SPW = NH * T
    if stage == 2:
        x1 = I("x1", [T, C])
        wgate = I("wgate", [FH, C])
        wup = I("wup", [FH, C])
        M1 = I("M1", [1, 8])
        mo = O("m_out", [1, 2])
        gate_sp = O("gate_sp", [128, SPW])
        up_sp = O("up_sp", [128, SPW])
    elif stage == 3:
        gate_spi = I("gate_sp", [128, SPW])
        M2 = I("M2", [2, 8])
        mo = O("m_out", [1, 1])
        gq_sp = O("gq_sp", [128, SPW])
        squ_sp = O("squ_sp", [128, SPW])
    elif stage == 4:
        up_spi = I("up_sp", [128, SPW])
        gq_spi = I("gq_sp", [128, SPW])
        squ_spi = I("squ_sp", [128, SPW])
        M2 = I("M2", [2, 8])
        M3 = I("M3", [1, 8])
        mo = O("m_out", [1, 1])
        oraw_sp = O("oraw_sp", [128, SPW])
    else:
        x1 = I("x1", [T, C])
        wdown = I("wdown", [C, FH])
        oraw_spi = I("oraw_sp", [128, SPW])
        M2 = I("M2", [2, 8])
        M3 = I("M3", [1, 8])
        M4 = I("M4", [1, 8])
        y_out = O("y_out", [T, C])
    I("ones_row", [1, 128], BF16)
    I("ones_f32", [1, 128])
    bhbm = dt("bhbm", [128, 1], F32).ap()

    with ExitStack() as perm:
        E = perm.enter_context
        st = nc.sbuf_tensor
        cx = Cx(nc, E)
        SY, GP, VE, SC, TE = cx.SY, cx.GP, cx.VE, cx.SC, cx.TE
        P = alloc_perm(cx, E, with_xT=False)
        pA, pB = P["pA"], P["pB"]
        srow = P["srow"]
        scol = E(st("scol", [128, 16], F32))
        fscr = E(st("fscr", [128, T], F32))
        fscr2 = E(st("fscr2", [128, T], F32))
        iscr = E(st("iscr", [128, T], I32))
        c32k = E(st("c32k", [128, 1], F32))
        Ms = E(st("Ms", [1, 48], F32))

        if stage == 2:
            SY.dma_start(out=Ms[0:1, 0:8], in_=M1).then_inc(cx.sLD.h, 16)
            cx.sLD.bump(16)
        if stage >= 3:
            SY.dma_start(out=Ms[0:1, 8:24],
                         in_=M2.rearrange("b c -> (b c)").unsqueeze(0)).then_inc(
                cx.sLD.h, 16); cx.sLD.bump(16)
        if stage >= 4:
            SY.dma_start(out=Ms[0:1, 24:32], in_=M3).then_inc(cx.sLD.h, 16)
            cx.sLD.bump(16)
        if stage >= 5:
            SY.dma_start(out=Ms[0:1, 32:40], in_=M4).then_inc(cx.sLD.h, 16)
            cx.sLD.bump(16)
        load_base_consts(cx, P, inp)
        VE.memset(srow[0:1, :], 0.0)
        VE.memset(c32k[:], 32768.0)
        cx.w(VE, cx.sLD)

        def mk_scale(ds, dr, src_row):
            VE.reduce_max(srow[0:1, 40:41], src_row, axis=AX.X)
            VE.tensor_scalar(srow[0:1, ds:ds + 1], srow[0:1, 40:41],
                             1.0 / QA, 1e-12, ALU.mult, ALU.max)
            VE.reciprocal(srow[0:1, dr:dr + 1], srow[0:1, ds:ds + 1])
            cx.vfence()

        if stage == 2:
            mk_scale(0, 1, Ms[0:1, 0:8])
            VE.tensor_scalar(srow[0:1, 14:15], srow[0:1, 0:1], -32768.0,
                             None, ALU.mult)
        if stage >= 3:
            mk_scale(2, 3, Ms[0:1, 8:16])
            mk_scale(4, 5, Ms[0:1, 16:24])
            VE.tensor_scalar(srow[0:1, 13:14], srow[0:1, 2:3], -32768.0,
                             None, ALU.mult)
        if stage >= 4:
            VE.reduce_max(srow[0:1, 40:41], Ms[0:1, 24:32], axis=AX.X)
            VE.tensor_tensor(srow[0:1, 41:42], srow[0:1, 40:41],
                             srow[0:1, 2:3], ALU.mult)
            VE.tensor_scalar_mul(srow[0:1, 41:42], srow[0:1, 41:42],
                                 1.0 / 65536.0)
            VE.tensor_scalar(srow[0:1, 6:7], srow[0:1, 41:42],
                             1.0 / QA, 1e-12, ALU.mult, ALU.max)
            VE.reciprocal(srow[0:1, 7:8], srow[0:1, 6:7])
            cx.vfence()
            VE.tensor_tensor(srow[0:1, 10:11], srow[0:1, 2:3],
                             srow[0:1, 7:8], ALU.mult)
            VE.tensor_scalar_mul(srow[0:1, 10:11], srow[0:1, 10:11],
                                 1.0 / 65536.0)
        if stage >= 5:
            VE.reduce_max(srow[0:1, 40:41], Ms[0:1, 32:40], axis=AX.X)
            VE.tensor_tensor(srow[0:1, 41:42], srow[0:1, 40:41],
                             srow[0:1, 6:7], ALU.mult)
            VE.tensor_tensor(srow[0:1, 41:42], srow[0:1, 41:42],
                             srow[0:1, 4:5], ALU.mult)
            VE.tensor_scalar(srow[0:1, 8:9], srow[0:1, 41:42],
                             1.0 / QA, 1e-12, ALU.mult, ALU.max)
            VE.reciprocal(srow[0:1, 9:10], srow[0:1, 8:9])
            cx.vfence()
            VE.tensor_tensor(srow[0:1, 11:12], srow[0:1, 6:7],
                             srow[0:1, 4:5], ALU.mult)
            VE.tensor_tensor(srow[0:1, 11:12], srow[0:1, 11:12],
                             srow[0:1, 9:10], ALU.mult)
            VE.tensor_scalar(srow[0:1, 12:13], srow[0:1, 8:9], -32768.0,
                             None, ALU.mult)
        VE.engine_nop().then_inc(cx.sVE.h, 1); cx.sVE.bump()
        emit_bcast_cols(cx, P, srow[0:1, 0:15], scol[:, 0:15], 15)

        def act_round(src_ap, scale_col, dst_ap):
            cx.w(SC, cx.sVE)
            cx.w(SC, cx.sSC)
            SC.activation(fscr[:, :], src_ap, AF.Identity,
                          scale=scale_col,
                          bias=c32k[:, 0:1]).then_inc(cx.sSC.h, 1)
            cx.sSC.bump()
            cx.w(SC, cx.sSC)
            SC.activation(iscr[:, :], fscr[:, :],
                          AF.Identity).then_inc(cx.sSC.h, 1)
            cx.sSC.bump()
            cx.w(VE, cx.sSC)
            VE.tensor_tensor(fscr2[:, :], fscr[:, :], iscr[:, :],
                             ALU.subtract)
            VE.scalar_tensor_tensor(dst_ap, fscr2[:, :], 0.5, iscr[:, :],
                                    ALU.is_ge, ALU.add)

        if stage == 2:
            with ExitStack() as ph:
                Ep = ph.enter_context
                P["xT"] = Ep(st("xT", [128, ND, T], BF16))
                wstg = Ep(st("wstg", [128, 2, 2, C], F32))
                wqd = Ep(st("wqd", [128, 2, 2, C], BF16))
                wTr = Ep(st("wTr", [128, 2, 2, ND, 128], BF16))
                sblk = Ep(st("sblk", [128, 32], F32))
                rblk = Ep(st("rblk", [128, 32], F32))
                gsp = Ep(st("gsp", [128, 2, T], F32))
                usp = Ep(st("usp", [128, 2, T], F32))

                for tt in range(NT):
                    bb = tt % 2
                    cx.w(SY, cx.sVE)
                    SY.dma_start(out=P["xres"][:, bb, :],
                                 in_=tiled(x1)[:, tt, :]).then_inc(
                        cx.sRS.h, 16)
                    cx.sRS.bump(16)
                    cx.w(SC, cx.sRS)
                    cx.w(SC, cx.sVE)
                    SC.activation(fscr[:, 0:C], P["xres"][:, bb, :],
                                  AF.Identity, scale=scol[:, 1:2],
                                  bias=c32k[:, 0:1]).then_inc(cx.sSC.h, 1)
                    cx.sSC.bump()
                    cx.w(SC, cx.sSC)
                    SC.activation(iscr[:, 0:C], fscr[:, 0:C],
                                  AF.Identity).then_inc(cx.sSC.h, 1)
                    cx.sSC.bump()
                    cx.w(VE, cx.sSC)
                    VE.tensor_tensor(fscr2[:, 0:C], fscr[:, 0:C],
                                     iscr[:, 0:C], ALU.subtract)
                    VE.scalar_tensor_tensor(fscr[:, 0:C], fscr2[:, 0:C], 0.5,
                                            iscr[:, 0:C], ALU.is_ge, ALU.add)
                    VE.tensor_scalar(P["xln"][:, bb, :], fscr[:, 0:C],
                                     scol[:, 0:1], scol[:, 14:15],
                                     ALU.mult, ALU.add)
                    VE.engine_nop().then_inc(cx.sVE.h, 1); cx.sVE.bump()
                    emit_xT_tile(cx, P, P["xln"][:, bb, :], tt)
                xq_xp = cx.sXP.v

                ve_chunk, te_mark, xpm, sp_mark, sc_mark = {}, {}, {}, {}, {}
                spilled = set()

                def emit_spill(j, force=False):
                    j = j if force else j - 1
                    if j < 0 or j in spilled or j not in sc_mark:
                        return
                    spilled.add(j)
                    jb = j % 2
                    cx.w(SY, cx.sSC, sc_mark[j])
                    SY.wait_ge(cx.sST.h, cx.sST.v)
                    SY.dma_start(out=gate_sp[:, T * j:T * (j + 1)],
                                 in_=gsp[:, jb, :]).then_inc(cx.sST.h, 16)
                    cx.sST.bump(16)
                    SY.dma_start(out=up_sp[:, T * j:T * (j + 1)],
                                 in_=usp[:, jb, :]).then_inc(cx.sST.h, 16)
                    cx.sST.bump(16)
                    sp_mark[j] = cx.sST.v

                for hh in range(NH):
                    bb = hh % 2
                    for wi in range(2):
                        wsrc_d = wgate if wi == 0 else wup
                        if hh >= 2:
                            GP.wait_ge(cx.sVE.h, ve_chunk[hh - 2])
                        GP.wait_ge(cx.sGD.h, cx.sGD.v)
                        GP.dma_start(
                            out=wstg[:, bb, wi, :],
                            in_=wsrc_d[128 * hh:128 * (hh + 1),
                                       :]).then_inc(cx.sGD.h, 16)
                        cx.sGD.bump(16)
                        gd_here = cx.sGD.v
                        cx.w(VE, cx.sGD, gd_here)
                        if (hh - 2, wi) in xpm:
                            VE.wait_ge(cx.sXP.h, xpm[(hh - 2, wi)])
                        w3 = wstg[:, bb, wi, :].rearrange(
                            "p (b i) -> p b i", i=32)
                        fv = fscr[:, 0:C].rearrange("p (b i) -> p b i", i=32)
                        VE.reduce_max(sblk[:, :], w3, axis=AX.X,
                                      apply_absolute_value=True)
                        VE.tensor_scalar(sblk[:, :], sblk[:, :], 1.0 / 7.0,
                                         1e-12, ALU.mult, ALU.max)
                        VE.reciprocal(rblk[:, :], sblk[:, :])
                        cx.vfence()
                        VE.tensor_tensor(
                            fv, w3,
                            rblk[:, :].unsqueeze(2).broadcast_to(
                                [128, 32, 32]), ALU.mult)
                        VE.tensor_scalar(fscr[:, 0:C], fscr[:, 0:C], 16.0,
                                         None, ALU.add)
                        VE.engine_nop().then_inc(cx.sVE.h, 1); cx.sVE.bump()
                        cx.w(SC, cx.sVE)
                        SC.activation(iscr[:, 0:C], fscr[:, 0:C],
                                      AF.Identity).then_inc(cx.sSC.h, 1)
                        cx.sSC.bump()
                        cx.w(VE, cx.sSC)
                        VE.tensor_tensor(fscr2[:, 0:C], fscr[:, 0:C],
                                         iscr[:, 0:C], ALU.subtract)
                        VE.scalar_tensor_tensor(fscr[:, 0:C], fscr2[:, 0:C],
                                                0.5, iscr[:, 0:C],
                                                ALU.is_ge, ALU.add)
                        VE.tensor_scalar(fscr[:, 0:C], fscr[:, 0:C], -16.0,
                                         None, ALU.add)
                        VE.tensor_tensor(
                            wqd[:, bb, wi, :].rearrange(
                                "p (b i) -> p b i", i=32),
                            fv,
                            sblk[:, :].unsqueeze(2).broadcast_to(
                                [128, 32, 32]), ALU.mult)
                        VE.engine_nop().then_inc(cx.sVE.h, 1); cx.sVE.bump()
                        cx.w(SY, cx.sVE)
                        cx.w(SY, cx.sGD)
                        if hh >= 2:
                            SY.wait_ge(cx.sTE.h, te_mark[hh - 2])
                        SY.wait_ge(cx.sXP.h, cx.sXP.v)
                        for d in range(ND):
                            SY.dma_start_transpose(
                                wTr[:, bb, wi, d, :],
                                wqd[:, bb, wi,
                                    128 * d:128 * (d + 1)]).then_inc(
                                cx.sXP.h, 16)
                            cx.sXP.bump(16)
                        xpm[(hh, wi)] = cx.sXP.v
                    emit_spill(hh)
                    cx.gTE()
                    cx.w(TE, cx.sXP, xpm[(hh, 1)])
                    if hh == 0:
                        cx.w(TE, cx.sXP, xq_xp)
                    for wi, p in ((0, pA), (1, pB)):
                        for b in range(4):
                            for d in range(ND):
                                t = TE.matmul(
                                    p[:, 512 * b:512 * (b + 1)],
                                    wTr[:, bb, wi, d, :],
                                    P["xT"][:, d, 512 * b:512 * (b + 1)],
                                    start=(d == 0), stop=(d == ND - 1))
                        t.then_inc(cx.sTE.h, 1); cx.sTE.bump()
                    te_here = cx.sTE.v
                    te_mark[hh] = te_here

                    cx.w(VE, cx.sTE, te_here)
                    VE.reduce_max(P["col"][:, 0:1], pA[:, :], axis=AX.X,
                                  apply_absolute_value=True)
                    VE.tensor_max(P["runmax"][:, 0:1], P["runmax"][:, 0:1],
                                  P["col"][:, 0:1])
                    VE.reduce_max(P["col"][:, 1:2], pB[:, :], axis=AX.X,
                                  apply_absolute_value=True)
                    VE.tensor_max(P["runmax"][:, 1:2], P["runmax"][:, 1:2],
                                  P["col"][:, 1:2]).then_inc(cx.sVE.h, 1)
                    cx.sVE.bump()
                    ve_chunk[hh] = cx.sVE.v
                    cx.w(SC, cx.sTE, te_here)
                    cx.w(SC, cx.sVE)
                    if hh >= 2:
                        SC.wait_ge(cx.sST.h, sp_mark[hh - 2])
                    SC.activation(gsp[:, bb, :], pA[:, :], AF.Identity)
                    SC.activation(usp[:, bb, :], pB[:, :],
                                  AF.Identity).then_inc(cx.sSC.h, 1)
                    cx.sSC.bump()
                    sc_mark[hh] = cx.sSC.v
                emit_spill(NH - 1, force=True)

            for j in range(2):
                emit_colmax_scalar(cx, P, bhbm, P["runmax"][:, j:j + 1],
                                   srow[0:1, 48 + j:49 + j])
            cx.w(SY, cx.sVE)
            SY.dma_start(out=mo, in_=srow[0:1, 48:50]).then_inc(cx.sLD.h, 16)
            cx.sLD.bump(16)
            cx.w(GP, cx.sST)
            cx.w(GP, cx.sLD)

        elif stage == 3:
            gbuf = E(st("gbuf", [128, 2, T], F32))
            f3buf = E(st("f3buf", [128, 2, T], F32))
            sqbuf = E(st("sqbuf", [128, 2, T], F32))
            ve_mark, sp_mark = {}, {}
            for hh in range(NH):
                bb = hh % 2
                if hh >= 2:
                    GP.wait_ge(cx.sVE.h, ve_mark[hh - 2])
                GP.wait_ge(cx.sGD.h, cx.sGD.v)
                GP.dma_start(out=gbuf[:, bb, :],
                             in_=gate_spi[:, T * hh:T * (hh + 1)]).then_inc(
                    cx.sGD.h, 16)
                cx.sGD.bump(16)
                gd_here = cx.sGD.v
                cx.w(VE, cx.sGD, gd_here)
                if hh >= 2:
                    VE.wait_ge(cx.sST.h, sp_mark[hh - 2])
                VE.engine_nop().then_inc(cx.sVE.h, 1); cx.sVE.bump()
                act_round(gbuf[:, bb, :], scol[:, 3:4], f3buf[:, bb, :])
                VE.engine_nop().then_inc(cx.sVE.h, 1); cx.sVE.bump()
                cx.w(SC, cx.sVE)
                SC.activation(fscr[:, :], f3buf[:, bb, :], AF.Sigmoid,
                              scale=scol[:, 2:3],
                              bias=scol[:, 13:14]).then_inc(cx.sSC.h, 1)
                cx.sSC.bump()
                cx.w(VE, cx.sSC)
                VE.engine_nop().then_inc(cx.sVE.h, 1); cx.sVE.bump()
                act_round(fscr[:, :], 65536.0, sqbuf[:, bb, :])
                VE.tensor_scalar(sqbuf[:, bb, :], sqbuf[:, bb, :], -32768.0,
                                 65535.0, ALU.add, ALU.min)
                VE.scalar_tensor_tensor(fscr2[:, :], f3buf[:, bb, :],
                                        -32768.0, sqbuf[:, bb, :],
                                        ALU.add, ALU.mult)
                VE.reduce_max(P["col"][:, 0:1], fscr2[:, :], axis=AX.X,
                              apply_absolute_value=True)
                VE.tensor_max(P["runmax"][:, 0:1], P["runmax"][:, 0:1],
                              P["col"][:, 0:1]).then_inc(cx.sVE.h, 1)
                cx.sVE.bump()
                ve_mark[hh] = cx.sVE.v
                cx.w(SY, cx.sVE)
                SY.wait_ge(cx.sST.h, cx.sST.v)
                SY.dma_start(out=gq_sp[:, T * hh:T * (hh + 1)],
                             in_=f3buf[:, bb, :]).then_inc(cx.sST.h, 16)
                cx.sST.bump(16)
                SY.dma_start(out=squ_sp[:, T * hh:T * (hh + 1)],
                             in_=sqbuf[:, bb, :]).then_inc(cx.sST.h, 16)
                cx.sST.bump(16)
                sp_mark[hh] = cx.sST.v
            emit_colmax_scalar(cx, P, bhbm, P["runmax"][:, 0:1],
                               srow[0:1, 48:49])
            cx.w(SY, cx.sVE)
            SY.dma_start(out=mo, in_=srow[0:1, 48:49]).then_inc(cx.sST.h, 16)
            cx.sST.bump(16)
            cx.w(GP, cx.sST)

        elif stage == 4:
            f3buf = E(st("f3buf", [128, 2, T], F32))
            sqbuf = E(st("sqbuf", [128, 2, T], F32))
            ubuf = E(st("ubuf", [128, 2, T], F32))
            obuf = E(st("obuf", [128, 2, T], F32))
            ve_mark, sp_mark = {}, {}
            for hh in range(NH):
                bb = hh % 2
                if hh >= 2:
                    GP.wait_ge(cx.sVE.h, ve_mark[hh - 2])
                GP.wait_ge(cx.sGD.h, cx.sGD.v)
                GP.dma_start(out=f3buf[:, bb, :],
                             in_=gq_spi[:, T * hh:T * (hh + 1)]).then_inc(
                    cx.sGD.h, 16)
                cx.sGD.bump(16)
                GP.dma_start(out=sqbuf[:, bb, :],
                             in_=squ_spi[:, T * hh:T * (hh + 1)]).then_inc(
                    cx.sGD.h, 16)
                cx.sGD.bump(16)
                GP.dma_start(out=ubuf[:, bb, :],
                             in_=up_spi[:, T * hh:T * (hh + 1)]).then_inc(
                    cx.sGD.h, 16)
                cx.sGD.bump(16)
                gd_here = cx.sGD.v
                cx.w(VE, cx.sGD, gd_here)
                VE.scalar_tensor_tensor(fscr2[:, :], f3buf[:, bb, :],
                                        -32768.0, sqbuf[:, bb, :],
                                        ALU.add, ALU.mult)
                VE.engine_nop().then_inc(cx.sVE.h, 1); cx.sVE.bump()
                act_round(fscr2[:, :], scol[:, 10:11], f3buf[:, bb, :])
                VE.engine_nop().then_inc(cx.sVE.h, 1); cx.sVE.bump()
                act_round(ubuf[:, bb, :], scol[:, 5:6], sqbuf[:, bb, :])
                VE.tensor_scalar(sqbuf[:, bb, :], sqbuf[:, bb, :], -32768.0,
                                 None, ALU.add)
                if hh >= 2:
                    VE.wait_ge(cx.sST.h, sp_mark[hh - 2])
                VE.scalar_tensor_tensor(obuf[:, bb, :], f3buf[:, bb, :],
                                        -32768.0, sqbuf[:, bb, :],
                                        ALU.add, ALU.mult)
                VE.reduce_max(P["col"][:, 0:1], obuf[:, bb, :], axis=AX.X,
                              apply_absolute_value=True)
                VE.tensor_max(P["runmax"][:, 0:1], P["runmax"][:, 0:1],
                              P["col"][:, 0:1]).then_inc(cx.sVE.h, 1)
                cx.sVE.bump()
                ve_mark[hh] = cx.sVE.v
                cx.w(SY, cx.sVE)
                SY.wait_ge(cx.sST.h, cx.sST.v)
                SY.dma_start(out=oraw_sp[:, T * hh:T * (hh + 1)],
                             in_=obuf[:, bb, :]).then_inc(cx.sST.h, 16)
                cx.sST.bump(16)
                sp_mark[hh] = cx.sST.v
            emit_colmax_scalar(cx, P, bhbm, P["runmax"][:, 0:1],
                               srow[0:1, 48:49])
            cx.w(SY, cx.sVE)
            SY.dma_start(out=mo, in_=srow[0:1, 48:49]).then_inc(cx.sST.h, 16)
            cx.sST.bump(16)
            cx.w(GP, cx.sST)

        else:
            oq = E(st("oq", [128, NH, T], BF16))
            with ExitStack() as p1s:
                obuf = p1s.enter_context(st("obuf", [128, 2, T], F32))
                sc_mark = {}
                for hh in range(NH):
                    bb = hh % 2
                    if hh >= 2:
                        GP.wait_ge(cx.sSC.h, sc_mark[hh - 2])
                    GP.wait_ge(cx.sGD.h, cx.sGD.v)
                    GP.dma_start(
                        out=obuf[:, bb, :],
                        in_=oraw_spi[:, T * hh:T * (hh + 1)]).then_inc(
                        cx.sGD.h, 16)
                    cx.sGD.bump(16)
                    gd_here = cx.sGD.v
                    cx.w(SC, cx.sGD, gd_here)
                    act_round(obuf[:, bb, :], scol[:, 11:12], fscr2[:, :])
                    sc_mark[hh] = cx.sSC.v
                    VE.tensor_scalar(oq[:, hh, :], fscr2[:, :],
                                     scol[:, 8:9], scol[:, 12:13],
                                     ALU.mult, ALU.add).then_inc(cx.sVE.h, 1)
                    cx.sVE.bump()

            with ExitStack() as ph:
                Ep = ph.enter_context
                wstg2 = Ep(st("wstg2", [128, 1536], F32))
                wqd2 = Ep(st("wqd2", [128, FH], BF16))
                sblk2 = Ep(st("sblk2", [128, 48], F32))
                rblk2 = Ep(st("rblk2", [128, 48], F32))
                wdT = Ep(st("wdT", [128, NH, C], BF16))
                cx.w(GP, cx.sTE); cx.w(GP, cx.sSC); cx.w(GP, cx.sVE)
                for dd in range(ND):
                    if dd >= 1:
                        GP.wait_ge(cx.sXP.h, xp_marks[dd - 1])
                    for hf in range(2):
                        if hf == 1:
                            GP.wait_ge(cx.sVE.h, cx.sVE.v)
                        GP.dma_start(
                            out=wstg2[:, :],
                            in_=wdown[128 * dd:128 * (dd + 1),
                                      1536 * hf:1536 * (hf + 1)]).then_inc(
                            cx.sGD.h, 16)
                        cx.sGD.bump(16)
                        gd2 = cx.sGD.v
                        cx.w(VE, cx.sGD, gd2)
                        cx.w(VE, cx.sXP)
                        w3 = wstg2[:, :].rearrange("p (b i) -> p b i", i=32)
                        c0 = 1536 * hf
                        VE.reduce_max(sblk2[:, :], w3, axis=AX.X,
                                      apply_absolute_value=True)
                        VE.tensor_scalar(sblk2[:, :], sblk2[:, :], 1.0 / 7.0,
                                         1e-12, ALU.mult, ALU.max)
                        VE.reciprocal(rblk2[:, :], sblk2[:, :])
                        cx.vfence()
                        VE.tensor_tensor(
                            fscr[:, 0:1536].rearrange(
                                "p (b i) -> p b i", i=32),
                            w3,
                            rblk2[:, :].unsqueeze(2).broadcast_to(
                                [128, 48, 32]),
                            ALU.mult)
                        VE.tensor_scalar(fscr[:, 0:1536], fscr[:, 0:1536],
                                         16.0, None, ALU.add)
                        VE.engine_nop().then_inc(cx.sVE.h, 1); cx.sVE.bump()
                        cx.w(SC, cx.sVE)
                        SC.activation(iscr[:, 0:1536], fscr[:, 0:1536],
                                      AF.Identity).then_inc(cx.sSC.h, 1)
                        cx.sSC.bump()
                        cx.w(VE, cx.sSC)
                        VE.tensor_tensor(fscr2[:, 0:1536], fscr[:, 0:1536],
                                         iscr[:, 0:1536], ALU.subtract)
                        VE.scalar_tensor_tensor(fscr[:, 0:1536],
                                                fscr2[:, 0:1536], 0.5,
                                                iscr[:, 0:1536], ALU.is_ge,
                                                ALU.add)
                        VE.tensor_scalar(fscr[:, 0:1536], fscr[:, 0:1536],
                                         -16.0, None, ALU.add)
                        VE.tensor_tensor(
                            wqd2[:, c0:c0 + 1536].rearrange(
                                "p (b i) -> p b i", i=32),
                            fscr[:, 0:1536].rearrange(
                                "p (b i) -> p b i", i=32),
                            sblk2[:, :].unsqueeze(2).broadcast_to(
                                [128, 48, 32]),
                            ALU.mult)
                        VE.engine_nop().then_inc(cx.sVE.h, 1); cx.sVE.bump()
                    cx.w(SY, cx.sVE)
                    cx.w(SY, cx.sGD)
                    for hh in range(NH):
                        SY.dma_start_transpose(
                            wdT[:, hh, 128 * dd:128 * (dd + 1)],
                            wqd2[:, 128 * hh:128 * (hh + 1)]).then_inc(
                            cx.sXP.h, 16)
                        cx.sXP.bump(16)
                    if dd == 0:
                        xp_marks = {}
                    xp_marks[dd] = cx.sXP.v
                wd_xp = cx.sXP.v

                for tt in range(NT):
                    bb = tt % 2
                    cx.w(SY, cx.sVE)
                    SY.dma_start(out=P["xres"][:, bb, :],
                                 in_=tiled(x1)[:, tt, :]).then_inc(
                        cx.sRS.h, 16); cx.sRS.bump(16)
                    cx.gTE()
                    cx.w(TE, cx.sRS)
                    if tt == 0:
                        cx.w(TE, cx.sXP, wd_xp)
                    for hh in range(NH):
                        for b in range(2):
                            t = TE.matmul(
                                pB[:, 512 * b:512 * (b + 1)],
                                oq[:, hh, 128 * tt:128 * (tt + 1)],
                                wdT[:, hh, 512 * b:512 * (b + 1)],
                                start=(hh == 0), stop=(hh == NH - 1),
                                skip_group_check=True)
                    t.then_inc(cx.sTE.h, 1); cx.sTE.bump()
                    emit_ln_tile(cx, P, pB[:, 0:C], P["xres"][:, bb, :],
                                 P["xln"][:, bb, :],
                                 out_wait=[(cx.sST, cx.sST.v)])
                    cx.w(SY, cx.sSC)
                    SY.dma_start(out=tiled(y_out)[:, tt, :],
                                 in_=P["xln"][:, bb, :]).then_inc(
                        cx.sST.h, 16); cx.sST.bump(16)
            cx.w(GP, cx.sST)

        blk = perm.enter_context(nc.Block())
        cx.rec.replay(blk)
        return nc, inp


_cache = {}


def _get(key, builder):
    if key not in _cache:
        _cache[key] = builder()
    return _cache[key]


def kernel(**inputs):
    cst = host_constants()

    def f32(x):
        return np.ascontiguousarray(np.asarray(x), dtype=None)

    cattn_w = np.ascontiguousarray(inputs["L0_attn_c_attn_w"])
    cattn_bp = np.ascontiguousarray(
        np.asarray(inputs["L0_attn_c_attn_b"]).reshape(24, 128).T)
    cproj_w = np.ascontiguousarray(inputs["L0_attn_c_proj_w"])
    cproj_b = np.ascontiguousarray(
        np.asarray(inputs["L0_attn_c_proj_b"]).reshape(1, C))
    wk_c = np.ascontiguousarray(inputs["L0_attn_wk_c"])
    wk_eT = np.ascontiguousarray(np.asarray(inputs["L0_attn_wk_e"]).T)
    wv_c = np.ascontiguousarray(inputs["L0_attn_wv_c"])
    wv_eT = np.ascontiguousarray(np.asarray(inputs["L0_attn_wv_e"]).T)
    wg0T = np.ascontiguousarray(np.asarray(inputs["L0_ffn_w_gate"]).T)
    wu0T = np.ascontiguousarray(np.asarray(inputs["L0_ffn_w_up"]).T)
    wd0T = np.ascontiguousarray(np.asarray(inputs["L0_ffn_w_down"]).T)
    wqT = np.ascontiguousarray(np.asarray(inputs["L1_attn_Wq"]).T)
    wkT = np.ascontiguousarray(np.asarray(inputs["L1_attn_Wk"]).T)
    wvT = np.ascontiguousarray(np.asarray(inputs["L1_attn_Wv"]).T)
    woT = np.ascontiguousarray(np.asarray(inputs["L1_attn_Wo"]).T)
    bq = np.ascontiguousarray(np.asarray(inputs["L1_attn_bq"]).reshape(1, C))
    bk = np.ascontiguousarray(np.asarray(inputs["L1_attn_bk"]).reshape(1, C))
    bv = np.ascontiguousarray(np.asarray(inputs["L1_attn_bv"]).reshape(1, C))
    bo = np.ascontiguousarray(np.asarray(inputs["L1_attn_bo"]).reshape(1, C))
    wgate1 = np.ascontiguousarray(inputs["L1_ffn_w_gate"])
    wup1 = np.ascontiguousarray(inputs["L1_ffn_w_up"])
    wdown1 = np.ascontiguousarray(inputs["L1_ffn_w_down"])
    x_all = np.ascontiguousarray(inputs["L0_attn_hidden_states"])

    nc1, _ = _get("p1", lambda: build_p1())
    base1 = dict(cattn_w=cattn_w, cattn_bp=cattn_bp, cproj_w=cproj_w,
                 cproj_b=cproj_b, wk_c=wk_c, wk_eT=wk_eT, wv_c=wv_c,
                 wv_eT=wv_eT, wg0T=wg0T, wu0T=wu0T, wd0T=wd0T,
                 wqT=wqT, wkT=wkT, wvT=wvT, woT=woT,
                 bq=bq, bk=bk, bv=bv, bo=bo,
                 ones_row=cst["ones_row"], ones_f32=cst["ones_f32"],
                 diagmask=cst["diagmask"], idn=cst["idn"],
                 madd=cst["madd"], ropecos=cst["ropecos"],
                 ropesin=cst["ropesin"])
    in_maps = [dict(base1, x0=np.ascontiguousarray(x_all[c]))
               for c in range(B)]
    r1 = run_bass_kernel_spmd(nc1, in_maps, core_ids=list(range(B)))
    x1 = [r1.results[c]["x1_out"] for c in range(B)]
    M1 = np.ascontiguousarray(
        np.stack([r1.results[c]["m1_out"][0, 0] for c in range(B)])[None, :])

    def run_stage(stage, extra, percore):
        nc, _ = _get(f"p{stage}", lambda: build_pn(stage))
        base = dict(ones_row=cst["ones_row"], ones_f32=cst["ones_f32"])
        if stage == 2:
            base.update(wgate=wgate1, wup=wup1, M1=M1)
        if stage == 5:
            base["wdown"] = wdown1
        base.update(extra)
        maps = [dict(base, **{k: v[c] for k, v in percore.items()})
                for c in range(B)]
        return run_bass_kernel_spmd(nc, maps, core_ids=list(range(B)))

    r2 = run_stage(2, {}, {"x1": x1})
    M2 = np.ascontiguousarray(
        np.stack([r2.results[c]["m_out"][0] for c in range(B)], axis=1))
    M2 = M2[[0, 1], :]
    gate_sp = [r2.results[c]["gate_sp"] for c in range(B)]
    up_sp = [r2.results[c]["up_sp"] for c in range(B)]
    r3 = run_stage(3, {"M2": M2}, {"gate_sp": gate_sp})
    M3 = np.ascontiguousarray(
        np.stack([r3.results[c]["m_out"][0, 0] for c in range(B)])[None, :])
    gq_sp = [r3.results[c]["gq_sp"] for c in range(B)]
    squ_sp = [r3.results[c]["squ_sp"] for c in range(B)]
    r4 = run_stage(4, {"M2": M2, "M3": M3},
                   {"up_sp": up_sp, "gq_sp": gq_sp, "squ_sp": squ_sp})
    M4 = np.ascontiguousarray(
        np.stack([r4.results[c]["m_out"][0, 0] for c in range(B)])[None, :])
    oraw_sp = [r4.results[c]["oraw_sp"] for c in range(B)]
    r5 = run_stage(5, {"M2": M2, "M3": M3, "M4": M4},
                   {"x1": x1, "oraw_sp": oraw_sp})
    y = np.stack([r5.results[c]["y_out"] for c in range(B)], axis=0)
    return y.astype(np.float32)


# version 10
# speedup vs baseline: 1.0190x; 1.0027x over previous
import numpy as np
import ml_dtypes
from contextlib import ExitStack

import concourse.bass as bass
import concourse.mybir as mybir
from concourse.bass_utils import run_bass_kernel_spmd

F32 = mybir.dt.float32
BF16 = mybir.dt.bfloat16
I32 = mybir.dt.int32
AF = mybir.ActivationFunctionType
ALU = mybir.AluOpType
AX = mybir.AxisListType

B, T, C = 8, 2048, 1024
H, HD, R = 16, 64, 32
FH = 3072
NT = T // 128
ND = C // 128
NH = FH // 128
S8 = 1.0 / 8.0
FSC = 1.0 / 127.0
QA = 32767.0
EPS = 1e-5
DEV = {}


class Rec:

    class Tok:
        __slots__ = ("incs",)

        def __init__(self):
            self.incs = []

        def then_inc(self, sem, n=1):
            self.incs.append((sem, n))
            return self

    def __init__(self):
        self.q = {k: [] for k in ("sync", "gpsimd", "vector", "scalar", "tensor")}

    def eng(self, name):
        rec = self

        class E:
            def __getattr__(self, op):
                def call(*a, **k):
                    tok = Rec.Tok()
                    rec.q[name].append((op, a, k, tok))
                    return tok
                return call
        return E()

    def replay(self, block):
        def mk(name):
            def run(e):
                for op, a, k, tok in self.q[name]:
                    inst = getattr(e, op)(*a, **k)
                    for sem, n in tok.incs:
                        inst.then_inc(sem, n)
            return run
        block.sync(mk("sync"))
        block.gpsimd(mk("gpsimd"))
        block.vector(mk("vector"))
        block.scalar(mk("scalar"))
        block.tensor(mk("tensor"))


class Sem:
    def __init__(self, E, nc, name):
        self.h = E(nc.semaphore(name))
        self.v = 0

    def bump(self, n=1):
        self.v += n
        return self.v


def tiled(x):
    return x.rearrange("(tt p) d -> p tt d", p=128)


def chunked(w):
    return w.rearrange("(c p) e -> p c e", p=128)


def host_constants():
    c = {}
    c["ones_row"] = np.ones((1, 128), dtype=ml_dtypes.bfloat16)
    c["ones_f32"] = np.ones((1, 128), dtype=np.float32)
    kk = np.arange(128)[:, None]
    qq = np.arange(128)[None, :]
    c["diagmask"] = (kk <= qq).astype(ml_dtypes.bfloat16)
    c["idn"] = np.eye(128, dtype=ml_dtypes.bfloat16)
    c["madd"] = np.where(kk > qq, -1e30, 0.0).astype(ml_dtypes.bfloat16)
    half = HD // 2
    theta = 1.0 / (10000.0 ** (np.arange(0, HD, 2, dtype=np.float64) / HD))
    pos = np.arange(1, T + 1, dtype=np.float64)
    ang = pos[:, None] * theta[None, :]
    ang = np.concatenate([ang, ang], axis=1)
    sgn = np.where(np.arange(HD) < half, -1.0, 1.0)
    c["ropecos"] = np.tile(np.cos(ang), (1, H)).astype(ml_dtypes.bfloat16)
    c["ropesin"] = np.tile(np.sin(ang) * sgn[None, :], (1, H)).astype(
        ml_dtypes.bfloat16)
    return c


class _VProxy:

    def __init__(self, cx, raw):
        self._cx, self._raw = cx, raw
        self._written = set()
        self._read = set()

    @staticmethod
    def _names(objs):
        out = set()
        for o in objs:
            t = getattr(o, "tensor", None)
            n = getattr(t, "name", None)
            if n is not None:
                out.add(n)
        return out

    def __getattr__(self, op):
        raw = self._raw

        def call(*a, **k):
            if op not in ("wait_ge", "engine_nop"):
                names = self._names(list(a) + list(k.values()))
                outs = []
                if a and getattr(a[0], "tensor", None) is not None:
                    outs.append(a[0])
                if "accum_out" in k and k["accum_out"] is not None:
                    outs.append(k["accum_out"])
                onames = self._names(outs)
                if (names & self._written) or (onames & self._read):
                    cx = self._cx
                    raw.engine_nop().then_inc(cx.sVE.h, 1)
                    cx.sVE.bump()
                    raw.wait_ge(cx.sVE.h, cx.sVE.v)
                    self._written.clear()
                    self._read.clear()
                self._written |= onames
                self._read |= names - onames
            return getattr(raw, op)(*a, **k)
        return call


class Cx:
    def __init__(self, nc, E):
        self.nc = nc
        self.rec = Rec()
        r = self.rec
        self.SY, self.GP, self.SC, self.TE = (
            r.eng("sync"), r.eng("gpsimd"), r.eng("scalar"), r.eng("tensor"))
        for nm in ("sLD", "sGD", "sXP", "sTE", "sVE", "sSC", "sST", "sRS"):
            setattr(self, nm, Sem(E, nc, nm))
        self.VE = _VProxy(self, r.eng("vector"))

    def w(self, eng, sem, val=None):
        eng.wait_ge(sem.h, sem.v if val is None else val)

    def gTE(self):
        self.w(self.TE, self.sSC)
        self.w(self.TE, self.sVE)

    def vfence(self):
        self.VE.engine_nop().then_inc(self.sVE.h, 1)
        self.sVE.bump()
        self.VE.wait_ge(self.sVE.h, self.sVE.v)


def emit_ln_tile(cx, P, pin, resid_ap, out_ap, resid_scale=None,
                 out_wait=None):
    VE, SC = cx.VE, cx.SC
    col, lnA, lnsq = P["col"], P["lnA"], P["lnsq"]
    cx.w(VE, cx.sTE)
    if resid_scale is None:
        VE.tensor_add(lnA[:, 0:C], pin, resid_ap).then_inc(cx.sVE.h, 1)
    else:
        VE.scalar_tensor_tensor(lnA[:, 0:C], pin, resid_scale, resid_ap,
                                ALU.mult, ALU.add).then_inc(cx.sVE.h, 1)
    cx.sVE.bump()
    pin_mark = cx.sVE.v
    VE.reduce_sum(col[:, 0:1], lnA[:, 0:C], axis=AX.X)
    VE.tensor_scalar_mul(col[:, 0:1], col[:, 0:1], 1.0 / C)
    VE.engine_nop().then_inc(cx.sVE.h, 1); cx.sVE.bump()
    cx.w(SC, cx.sVE)
    SC.activation(lnsq[:, 0:C], lnA[:, 0:C], AF.Square,
                  accum_out=col[:, 1:2]).then_inc(cx.sSC.h, 1); cx.sSC.bump()
    cx.w(VE, cx.sSC)
    VE.tensor_scalar(col[:, 2:3], col[:, 1:2], 1.0 / C, None, ALU.mult)
    VE.tensor_tensor(col[:, 3:4], col[:, 0:1], col[:, 0:1], ALU.mult)
    VE.tensor_sub(col[:, 2:3], col[:, 2:3], col[:, 3:4])
    VE.tensor_scalar_add(col[:, 2:3], col[:, 2:3], EPS)
    VE.reciprocal(col[:, 4:5], col[:, 2:3]).then_inc(cx.sVE.h, 1); cx.sVE.bump()
    cx.w(SC, cx.sVE)
    SC.activation(col[:, 5:6], col[:, 4:5], AF.Sqrt).then_inc(cx.sSC.h, 1)
    cx.sSC.bump()
    cx.w(VE, cx.sSC)
    VE.tensor_tensor(col[:, 6:7], col[:, 5:6], col[:, 5:6], ALU.mult)
    VE.tensor_tensor(col[:, 6:7], col[:, 6:7], col[:, 2:3], ALU.mult)
    VE.tensor_scalar(col[:, 6:7], col[:, 6:7], -0.5, 1.5, ALU.mult, ALU.add)
    VE.tensor_tensor(col[:, 5:6], col[:, 5:6], col[:, 6:7], ALU.mult)
    VE.tensor_scalar(col[:, 7:8], col[:, 0:1], -1.0, None, ALU.mult)
    VE.tensor_tensor(col[:, 7:8], col[:, 7:8], col[:, 5:6], ALU.mult)
    VE.engine_nop().then_inc(cx.sVE.h, 1); cx.sVE.bump()
    cx.w(SC, cx.sVE)
    for (sm, vv) in (out_wait or ()):
        cx.w(SC, sm, vv)
    SC.activation(out_ap, lnA[:, 0:C], AF.Identity,
                  bias=col[:, 7:8], scale=col[:, 5:6]).then_inc(cx.sSC.h, 1)
    cx.sSC.bump()
    return pin_mark


def emit_xT_tile(cx, P, src_ap, tt):
    buf = P["xbfr"][:, 0, :]
    cx.w(cx.VE, cx.sXP)
    cx.VE.tensor_copy(buf, src_ap).then_inc(cx.sVE.h, 1); cx.sVE.bump()
    cx.w(cx.SY, cx.sVE)
    cx.w(cx.SY, cx.sGD)
    for d in range(ND):
        cx.SY.dma_start_transpose(
            P["xT"][:, d, 128 * tt:128 * (tt + 1)],
            buf[:, 128 * d:128 * (d + 1)]).then_inc(cx.sXP.h, 16)
        cx.sXP.bump(16)


def emit_colmax_scalar(cx, P, bhbm, col_ap, out11):
    SY, VE = cx.SY, cx.VE
    cx.w(SY, cx.sVE)
    SY.dma_start(out=bhbm, in_=col_ap).then_inc(cx.sXP.h, 16)
    cx.sXP.bump(16)
    cx.w(SY, cx.sXP)
    SY.dma_start(out=P["grow"][0:1, :],
                 in_=bhbm.rearrange("a b -> b a")).then_inc(cx.sXP.h, 16)
    cx.sXP.bump(16)
    cx.w(VE, cx.sXP)
    VE.reduce_max(out11, P["grow"][0:1, :], axis=AX.X).then_inc(cx.sVE.h, 1)
    cx.sVE.bump()


def emit_bcast_cols(cx, P, src_row_ap, dst_cols_ap, n):
    cx.gTE()
    t = cx.TE.matmul(P["pA"][:, 0:n], P["onesf"][0:1, :], src_row_ap,
                     start=True, stop=True)
    t.then_inc(cx.sTE.h, 1); cx.sTE.bump()
    cx.w(cx.VE, cx.sTE)
    cx.VE.tensor_copy(dst_cols_ap, P["pA"][:, 0:n]).then_inc(cx.sVE.h, 1)
    cx.sVE.bump()


def alloc_perm(cx, E, with_xT=True):
    st, pt = cx.nc.sbuf_tensor, cx.nc.psum_tensor
    P = {}
    if with_xT:
        P["xT"] = E(st("xT", [128, ND, T], BF16))
    P["xbfr"] = E(st("xbfr", [128, 1, C], BF16))
    P["xres"] = E(st("xres", [128, 2, C], F32))
    P["xln"] = E(st("xln", [128, 2, C], F32))
    P["onesr"] = E(st("onesr", [1, 128], BF16))
    P["onesf"] = E(st("onesf", [1, 128], F32))
    P["onescol"] = E(st("onescol", [128, 1], BF16))
    P["col"] = E(st("col", [128, 16], F32))
    P["lnA"] = E(st("lnA", [128, C], F32))
    P["lnsq"] = E(st("lnsq", [128, C], BF16))
    P["runmax"] = E(st("runmax", [128, 4], F32))
    P["grow"] = E(st("grow", [1, 128], F32))
    P["srow"] = E(st("srow", [1, 64], F32))
    P["pA"] = E(pt("pA", [128, T], F32))
    P["pB"] = E(pt("pB", [128, T], F32))
    return P


def load_base_consts(cx, P, inp):
    SY, VE = cx.SY, cx.VE
    SY.dma_start(out=P["onesr"][:], in_=inp["ones_row"]).then_inc(cx.sLD.h, 16)
    cx.sLD.bump(16)
    SY.dma_start(out=P["onesf"][:], in_=inp["ones_f32"]).then_inc(cx.sLD.h, 16)
    cx.sLD.bump(16)
    cx.w(VE, cx.sLD)
    VE.memset(P["onescol"][:], 1.0)
    VE.memset(P["runmax"][:], 0.0)
    VE.engine_nop().then_inc(cx.sVE.h, 1); cx.sVE.bump()


def build_p1(stop_after=None):
    nc = bass.Bass("TRN2", target_bir_lowering=False)
    dt = nc.dram_tensor
    inp = {}

    def I(name, shape, dtyp=F32):
        inp[name] = dt(name, shape, dtyp, kind="ExternalInput").ap()
        return inp[name]

    x0 = I("x0", [T, C])
    cattn_w = I("cattn_w", [C, 3 * C])
    cattn_bp = I("cattn_bp", [128, 24])
    cproj_w = I("cproj_w", [C, C])
    cproj_b = I("cproj_b", [1, C])
    wk_c = I("wk_c", [HD, R]); wk_eT = I("wk_eT", [HD, R])
    wv_c = I("wv_c", [HD, R]); wv_eT = I("wv_eT", [HD, R])
    wg0T = I("wg0T", [C, FH], I32)
    wu0T = I("wu0T", [C, FH], I32)
    wd0T = I("wd0T", [FH, C], I32)
    wqT = I("wqT", [C, C]); wkT = I("wkT", [C, C])
    wvT = I("wvT", [C, C]); woT = I("woT", [C, C])
    bqr = I("bq", [1, C]); bkr = I("bk", [1, C])
    bvr = I("bv", [1, C]); bor = I("bo", [1, C])
    I("ones_row", [1, 128], BF16)
    I("ones_f32", [1, 128])
    diagmask = I("diagmask", [128, 128], BF16)
    idn_h = I("idn", [128, 128], BF16)
    madd_h = I("madd", [128, 128], BF16)
    ropecos = I("ropecos", [T, C], BF16)
    ropesin = I("ropesin", [T, C], BF16)

    x1_out = dt("x1_out", [T, C], F32, kind="ExternalOutput").ap()
    m1_out = dt("m1_out", [1, 1], F32, kind="ExternalOutput").ap()
    xspillA = dt("xspillA", [T, C], F32).ap()
    xspillB = dt("xspillB", [T, C], F32).ap()
    bhbm = dt("bhbm", [128, 1], F32).ap()

    with ExitStack() as perm:
        E = perm.enter_context
        st = nc.sbuf_tensor
        cx = Cx(nc, E)
        SY, GP, VE, SC, TE = cx.SY, cx.GP, cx.VE, cx.SC, cx.TE
        P = alloc_perm(cx, E)
        pA, pB = P["pA"], P["pB"]
        onesr, onesf, onescol = P["onesr"], P["onesf"], P["onescol"]
        srow = P["srow"]
        dmask = E(st("dmask", [128, 128], BF16))
        idn = E(st("idn_s", [128, 128], BF16))
        madd = E(st("madd_s", [128, 128], BF16))
        negsB = E(st("negsB", [128, H], F32))

        SY.dma_start(out=dmask[:], in_=diagmask).then_inc(cx.sLD.h, 16)
        cx.sLD.bump(16)
        SY.dma_start(out=idn[:], in_=idn_h).then_inc(cx.sLD.h, 16)
        cx.sLD.bump(16)
        SY.dma_start(out=madd[:], in_=madd_h).then_inc(cx.sLD.h, 16)
        cx.sLD.bump(16)
        load_base_consts(cx, P, inp)

        for tt in range(NT):
            bb = tt % 2
            cx.w(SY, cx.sVE)
            SY.dma_start(out=P["xres"][:, bb, :],
                         in_=tiled(x0)[:, tt, :]).then_inc(cx.sLD.h, 16)
            cx.sLD.bump(16)
            cx.w(VE, cx.sLD)
            emit_xT_tile(cx, P, P["xres"][:, bb, :], tt)
        xT0_xp = cx.sXP.v

        with ExitStack() as ph:
            Ep = ph.enter_context
            WAr = Ep(st("WAr", [128, 2, 3, ND, 128], BF16))
            WP = Ep(st("WP", [128, ND, C], BF16))
            bias24 = Ep(st("bias24", [128, 24], F32))
            w64 = Ep(st("w64", [128, 4 * R], BF16))
            browp = Ep(st("browp", [1, C], BF16))
            qT2 = Ep(st("qT2", [128, T], BF16))
            kT2 = Ep(st("kT2", [128, T], BF16))
            vT2 = Ep(st("vT2", [128, T], BF16))
            qpT = Ep(st("qpT", [128, 4, T], BF16))
            klT = Ep(st("klT", [128, 4, T], BF16))
            atT = Ep(st("atT", [128, 4, T], BF16))
            atst = Ep(st("atst", [32, T], BF16))
            vones = Ep(st("vones", [128, H, NT, R + 1], BF16))
            sqs = Ep(st("sqs", [128, T], BF16))
            probsT = Ep(st("probsT", [128, T], BF16))
            Weff = Ep(st("Weff", [128, 4, C], BF16))
            reciprow = Ep(st("reciprow", [1, T], F32))
            recipbf = Ep(st("recipbf", [1, T], BF16))

            GP.dma_start(out=WP[:], in_=chunked(cproj_w)).then_inc(cx.sGD.h, 16)
            cx.sGD.bump(16)
            for half64 in range(2):
                for i, wsrc in enumerate((wk_c, wk_eT, wv_c, wv_eT)):
                    GP.dma_start(out=w64[64 * half64:64 * (half64 + 1),
                                         R * i:R * (i + 1)],
                                 in_=wsrc).then_inc(cx.sGD.h, 16)
                    cx.sGD.bump(16)
            GP.dma_start(out=browp[:], in_=cproj_b).then_inc(cx.sGD.h, 16)
            cx.sGD.bump(16)
            SY.dma_start(out=bias24[:], in_=cattn_bp).then_inc(cx.sLD.h, 16)
            cx.sLD.bump(16)
            a_ld, a_gd = cx.sLD.v, cx.sGD.v
            VE.memset(vones[:, :, :, R:R + 1], 1.0)
            VE.engine_nop().then_inc(cx.sVE.h, 1); cx.sVE.bump()

            te_qkv = {}
            for hp in range(DEV.get("n_hp", 8)):
                heads = (2 * hp, 2 * hp + 1)
                bbw = hp % 2
                if hp >= 2:
                    GP.wait_ge(cx.sTE.h, te_qkv[hp - 2])
                for which, ech in enumerate((hp, 8 + hp, 16 + hp)):
                    GP.dma_start(
                        out=WAr[:, bbw, which, :, :],
                        in_=cattn_w[:, 128 * ech:128 * (ech + 1)].rearrange(
                            "(c p) e -> p c e", p=128)).then_inc(cx.sGD.h, 16)
                    cx.sGD.bump(16)
                war_gd = cx.sGD.v
                for which, (dst, ech) in enumerate(
                        [(qT2, hp), (kT2, 8 + hp), (vT2, 16 + hp)]):
                    p = pA if which % 2 == 0 else pB
                    cx.gTE()
                    cx.w(TE, cx.sGD, war_gd)
                    if hp == 0 and which == 0:
                        cx.w(TE, cx.sXP, xT0_xp)
                        cx.w(TE, cx.sLD, a_ld)
                    for b in range(4):
                        for d in range(ND):
                            t = TE.matmul(
                                p[:, 512 * b:512 * (b + 1)],
                                WAr[:, bbw, which, d, :],
                                P["xT"][:, d, 512 * b:512 * (b + 1)],
                                start=(d == 0), stop=(d == ND - 1))
                    t.then_inc(cx.sTE.h, 1); cx.sTE.bump()
                    if which == 2:
                        te_qkv[hp] = cx.sTE.v
                    cx.w(SC, cx.sTE)
                    SC.activation(dst[:], p[:], AF.Identity,
                                  bias=bias24[:, ech:ech + 1]).then_inc(
                        cx.sSC.h, 1); cx.sSC.bump()

                for hi, h in enumerate(heads):
                    if "qk" in DEV.get("skip", ()):
                        break
                    rows = 64 * (h % 2)
                    g, cc = h % 4, h // 4
                    gb = 32 * g
                    for which, (srcT, wof, dstT) in enumerate(
                            [(qT2, R, qpT), (kT2, 0, klT)]):
                        p = pA if which == 0 else pB
                        cx.gTE()
                        for b in range(4):
                            t = TE.matmul(
                                p[gb:gb + 32, 512 * b:512 * (b + 1)],
                                w64[rows:rows + 64, wof:wof + R],
                                srcT[rows:rows + 64, 512 * b:512 * (b + 1)],
                                start=True, stop=True,
                                tile_position=(rows, gb))
                        t.then_inc(cx.sTE.h, 1); cx.sTE.bump()
                        cx.w(SC, cx.sTE)
                        SC.activation(dstT[gb:gb + 32, cc, :],
                                      p[gb:gb + 32, :],
                                      AF.Identity).then_inc(cx.sSC.h, 1)
                        cx.sSC.bump()
                        SC.activation(sqs[gb:gb + 32, :], p[gb:gb + 32, :],
                                      AF.Square).then_inc(cx.sSC.h, 1)
                        cx.sSC.bump()
                        cx.gTE()
                        for b in range(4):
                            t = TE.matmul(
                                p[0:1, 512 * b:512 * (b + 1)],
                                onescol[gb:gb + 32, 0:1],
                                sqs[gb:gb + 32, 512 * b:512 * (b + 1)],
                                start=True, stop=True,
                                tile_position=(gb, 0))
                        t.then_inc(cx.sTE.h, 1); cx.sTE.bump()
                        cx.w(VE, cx.sTE)
                        idx = 2 * hi + which
                        VE.reduce_max(srow[0:1, idx:idx + 1], p[0:1, 0:T],
                                      axis=AX.X).then_inc(cx.sVE.h, 1)
                        cx.sVE.bump()

                cx.w(VE, cx.sVE)
                VE.tensor_mul(
                    srow[0:1, 4:6],
                    srow[0:1, 0:4].rearrange("a (h k) -> a h k", k=2)[:, :, 0],
                    srow[0:1, 0:4].rearrange("a (h k) -> a h k", k=2)[:, :, 1])
                VE.engine_nop().then_inc(cx.sVE.h, 1); cx.sVE.bump()
                cx.w(SC, cx.sVE)
                SC.activation(srow[0:1, 6:8], srow[0:1, 4:6],
                              AF.Sqrt).then_inc(cx.sSC.h, 1); cx.sSC.bump()
                cx.w(VE, cx.sSC)
                VE.tensor_scalar_mul(srow[0:1, 8:10], srow[0:1, 6:8], -S8)
                VE.engine_nop().then_inc(cx.sVE.h, 1); cx.sVE.bump()
                emit_bcast_cols(cx, P, srow[0:1, 8:10],
                                negsB[:, 2 * hp:2 * hp + 2], 2)

                for hi, h in enumerate(heads):
                    if "vlat" in DEV.get("skip", ()):
                        break
                    rows = 64 * (h % 2)
                    cx.gTE()
                    for tt in range(NT):
                        t = TE.matmul(
                            pA[:, 32 * tt:32 * (tt + 1)],
                            vT2[rows:rows + 64, 128 * tt:128 * (tt + 1)],
                            w64[rows:rows + 64, 2 * R:3 * R],
                            start=True, stop=True,
                            tile_position=(rows, 0))
                    t.then_inc(cx.sTE.h, 1); cx.sTE.bump()
                    cx.w(VE, cx.sTE)
                    VE.tensor_copy(
                        vones[:, h, :, 0:R],
                        pA[:, 0:512].rearrange("p (tt r) -> p tt r", r=R),
                    ).then_inc(cx.sVE.h, 1); cx.sVE.bump()

                for hi, h in enumerate(heads):
                    if "scores" in DEV.get("skip", ()):
                        break
                    g, cc = h % 4, h // 4
                    gb = 32 * g
                    for kt in range(NT):
                        q0 = 128 * kt
                        b0 = q0 // 512
                        cx.gTE()
                        segv = {}
                        for b in range(b0, 4):
                            lo, hi2 = max(512 * b, q0), 512 * (b + 1)
                            t = TE.matmul(pA[:, lo:hi2],
                                          klT[gb:gb + 32, cc, q0:q0 + 128],
                                          qpT[gb:gb + 32, cc, lo:hi2],
                                          start=True, stop=True,
                                          tile_position=(gb, 0))
                            if b == b0:
                                t = TE.matmul(pA[:, q0:q0 + 128], idn[:, :],
                                              madd[:, :], start=False,
                                              stop=True,
                                              skip_group_check=True)
                            t.then_inc(cx.sTE.h, 1); cx.sTE.bump()
                            segv[b] = cx.sTE.v
                        expv = {}
                        for b in range(b0, 4):
                            lo, hi2 = max(512 * b, q0), 512 * (b + 1)
                            SC.wait_ge(cx.sTE.h, segv[b])
                            SC.activation(probsT[:, lo:hi2], pA[:, lo:hi2],
                                          AF.Exp, bias=negsB[:, h:h + 1],
                                          scale=S8).then_inc(cx.sSC.h, 1)
                            cx.sSC.bump()
                            expv[b] = cx.sSC.v
                        for b in range(b0, 4):
                            lo, hi2 = max(512 * b, q0), 512 * (b + 1)
                            TE.wait_ge(cx.sSC.h, expv[b])
                            t = TE.matmul(
                                pB[0:R + 1, lo:hi2], vones[:, h, kt, :],
                                probsT[:, lo:hi2],
                                start=(kt == 0),
                                stop=(kt == min(4 * b + 3, NT - 1)),
                                skip_group_check=True)
                            t.then_inc(cx.sTE.h, 1); cx.sTE.bump()
                    cx.w(VE, cx.sTE)
                    VE.reciprocal(reciprow[0:1, :], pB[R:R + 1, :])
                    cx.vfence()
                    VE.tensor_copy(recipbf[0:1, :],
                                   reciprow[0:1, :]).then_inc(cx.sVE.h, 1)
                    cx.sVE.bump()
                    cx.gTE()
                    for b in range(4):
                        t = TE.matmul(pA[0:R, 512 * b:512 * (b + 1)],
                                      onesr[:, 0:R],
                                      recipbf[0:1, 512 * b:512 * (b + 1)],
                                      start=True, stop=True)
                    t.then_inc(cx.sTE.h, 1); cx.sTE.bump()
                    cx.w(SC, cx.sTE)
                    cx.w(SC, cx.sGD)
                    cx.w(SC, cx.sVE)
                    SC.activation(atst[:, :], pB[0:R, :],
                                  AF.Identity).then_inc(cx.sSC.h, 1)
                    cx.sSC.bump()
                    cx.w(VE, cx.sSC)
                    VE.tensor_mul(atst[:, :], atst[:, :],
                                  pA[0:R, :]).then_inc(cx.sVE.h, 1)
                    cx.sVE.bump()
                    cx.w(GP, cx.sVE)
                    GP.dma_start(out=atT[gb:gb + 32, cc, :],
                                 in_=atst[:, :]).then_inc(cx.sGD.h, 16)
                    cx.sGD.bump(16)

            for h in range(0 if "weff" in DEV.get("skip", ()) else H):
                g, cc = h % 4, h // 4
                gb = 32 * g
                rc, ro = (h * HD) // 128, (h * HD) % 128
                cx.gTE()
                for b in range(2):
                    t = TE.matmul(pA[gb:gb + 32, 512 * b:512 * (b + 1)],
                                  w64[ro:ro + 64, 3 * R:4 * R],
                                  WP[ro:ro + 64, rc, 512 * b:512 * (b + 1)],
                                  start=True, stop=True,
                                  tile_position=(ro, gb))
                t.then_inc(cx.sTE.h, 1); cx.sTE.bump()
                cx.w(VE, cx.sTE)
                VE.tensor_copy(Weff[gb:gb + 32, cc, :],
                               pA[gb:gb + 32, 0:C]).then_inc(cx.sVE.h, 1)
                cx.sVE.bump()

            for tt in range(NT):
                bb = tt % 2
                cx.w(SY, cx.sVE)
                SY.dma_start(out=P["xres"][:, bb, :],
                             in_=tiled(x0)[:, tt, :]).then_inc(cx.sRS.h, 16)
                cx.sRS.bump(16)
                cx.gTE()
                cx.w(TE, cx.sGD)
                first = True
                if "proj" not in DEV.get("skip", ()):
                    for cc in range(4):
                        for b in range(2):
                            TE.matmul(pB[:, 512 * b:512 * (b + 1)],
                                      atT[:, cc, 128 * tt:128 * (tt + 1)],
                                      Weff[:, cc, 512 * b:512 * (b + 1)],
                                      start=first, stop=False,
                                      skip_group_check=True)
                        first = False
                cx.w(TE, cx.sRS)
                for b in range(2):
                    t = TE.matmul(pB[:, 512 * b:512 * (b + 1)], onesr[:, :],
                                  browp[0:1, 512 * b:512 * (b + 1)],
                                  start=(first and b == 0), stop=True,
                                  skip_group_check=True)
                t.then_inc(cx.sTE.h, 1); cx.sTE.bump()
                emit_ln_tile(cx, P, pB[:, 0:C], P["xres"][:, bb, :],
                             P["xln"][:, bb, :],
                             out_wait=[(cx.sXP, cx.sXP.v)])
                cx.w(SY, cx.sSC)
                SY.dma_start(out=tiled(xspillA)[:, tt, :],
                             in_=P["xln"][:, bb, :]).then_inc(cx.sXP.h, 16)
                cx.sXP.bump(16)
                if stop_after != "l0attn":
                    emit_xT_tile(cx, P, P["xln"][:, bb, :], tt)

        if stop_after == "l0attn":
            _emit_p1_tail(cx, P, xspillA, x1_out, m1_out, bhbm)
            blk = perm.enter_context(nc.Block())
            cx.rec.replay(blk)
            return nc, inp

        with ExitStack() as ph:
            Ep = ph.enter_context
            g2 = Ep(st("g2", [128, NH, T], BF16))
            with ExitStack() as ph2:
                Ep2 = ph2.enter_context
                wring = Ep2(st("wring", [128, 2, 2, ND, 128], BF16))
                silu = Ep2(st("silu", [128, T], F32))
                sW = [Sem(Ep2, nc, "sW0"), Sem(Ep2, nc, "sW1")]
                te_after = {}
                for hh in range(NH):
                    bb = hh % 2
                    if hh >= 2:
                        GP.wait_ge(cx.sTE.h, te_after[hh - 2])
                    GP.dma_start(
                        out=wring[:, bb, 0, :, :],
                        in_=inp["wg0T"][:, 128 * hh:128 * (hh + 1)].rearrange(
                            "(c p) h -> p c h", p=128)).then_inc(
                        sW[bb].h, 16); sW[bb].bump(16)
                    GP.dma_start(
                        out=wring[:, bb, 1, :, :],
                        in_=inp["wu0T"][:, 128 * hh:128 * (hh + 1)].rearrange(
                            "(c p) h -> p c h", p=128)).then_inc(
                        sW[bb].h, 16); sW[bb].bump(16)
                    cx.gTE()
                    cx.w(TE, sW[bb])
                    if hh == 0:
                        cx.w(TE, cx.sXP, cx.sXP.v)
                    for w_i, p in ((0, pA), (1, pB)):
                        for b in range(4):
                            for d in range(ND):
                                t = TE.matmul(
                                    p[:, 512 * b:512 * (b + 1)],
                                    wring[:, bb, w_i, d, :],
                                    P["xT"][:, d, 512 * b:512 * (b + 1)],
                                    start=(d == 0), stop=(d == ND - 1))
                        t.then_inc(cx.sTE.h, 1); cx.sTE.bump()
                    te_after[hh] = cx.sTE.v
                    cx.w(SC, cx.sTE)
                    SC.activation(silu[:, :], pA[:, :], AF.Silu,
                                  scale=FSC).then_inc(cx.sSC.h, 1)
                    cx.sSC.bump()
                    cx.w(VE, cx.sSC)
                    VE.tensor_mul(g2[:, hh, :], silu[:, :],
                                  pB[:, :]).then_inc(cx.sVE.h, 1); cx.sVE.bump()

            wdsb = Ep(st("wdsb", [128, NH, C], BF16))
            cx.w(GP, cx.sTE); cx.w(GP, cx.sSC); cx.w(GP, cx.sVE)
            GP.dma_start(out=wdsb[:], in_=chunked(inp["wd0T"])).then_inc(
                cx.sGD.h, 16); cx.sGD.bump(16)
            wd_gd = cx.sGD.v
            for tt in range(NT):
                bb = tt % 2
                cx.w(SY, cx.sVE)
                cx.w(SY, cx.sXP)
                SY.dma_start(out=P["xres"][:, bb, :],
                             in_=tiled(xspillA)[:, tt, :]).then_inc(
                    cx.sRS.h, 16); cx.sRS.bump(16)
                cx.gTE()
                cx.w(TE, cx.sRS)
                if tt == 0:
                    cx.w(TE, cx.sGD, wd_gd)
                for hh in range(NH):
                    for b in range(2):
                        t = TE.matmul(pB[:, 512 * b:512 * (b + 1)],
                                      g2[:, hh, 128 * tt:128 * (tt + 1)],
                                      wdsb[:, hh, 512 * b:512 * (b + 1)],
                                      start=(hh == 0), stop=(hh == NH - 1),
                                      skip_group_check=True)
                t.then_inc(cx.sTE.h, 1); cx.sTE.bump()
                emit_ln_tile(cx, P, pB[:, 0:C], P["xres"][:, bb, :],
                             P["xln"][:, bb, :], resid_scale=FSC * FSC,
                             out_wait=[(cx.sXP, cx.sXP.v)])
                cx.w(SY, cx.sSC)
                SY.dma_start(out=tiled(xspillB)[:, tt, :],
                             in_=P["xln"][:, bb, :]).then_inc(cx.sXP.h, 16)
                cx.sXP.bump(16)
                if stop_after != "l0ffn":
                    emit_xT_tile(cx, P, P["xln"][:, bb, :], tt)

        if stop_after == "l0ffn":
            _emit_p1_tail(cx, P, xspillB, x1_out, m1_out, bhbm)
            blk = perm.enter_context(nc.Block())
            cx.rec.replay(blk)
            return nc, inp

        with ExitStack() as ph:
            Ep = ph.enter_context
            Wbuf = Ep(st("Wbuf", [128, ND, C], BF16))
            qroT = Ep(st("qroT", [128, 4, T], BF16))
            kroT = Ep(st("kroT", [128, 4, T], BF16))
            von1 = Ep(st("von1", [128, NT, 8, HD + 1], BF16))
            avT = Ep(st("avT", [128, ND, T], BF16))
            avst = Ep(st("avst", [64, T], BF16))
            probsT = Ep(st("probsT1", [128, T], BF16))
            qtile = Ep(st("qtile", [128, 512], BF16))
            rtmp = Ep(st("rtmp", [128, 512], BF16))
            rcos = Ep(st("rcos", [128, 1, C], BF16))
            rsin = Ep(st("rsin", [128, 1, C], BF16))
            nsq = Ep(st("nsq", [128, 512], F32))
            nstat = Ep(st("nstat", [128, 16], F32))
            brow4 = Ep(st("brow4", [1, 4, C], BF16))
            growL = Ep(st("growL", [1, 2048], F32))
            reciprow = Ep(st("reciprow1", [1, T], F32))
            recipbf = Ep(st("recipbf1", [1, T], BF16))
            ghbm = nc.dram_tensor("ghbm16", [128, 16], F32).ap()

            cx.w(GP, cx.sTE); cx.w(GP, cx.sSC); cx.w(GP, cx.sVE)
            for i, bsrc in enumerate((bqr, bkr, bvr, bor)):
                GP.dma_start(out=brow4[:, i, :], in_=bsrc).then_inc(
                    cx.sGD.h, 16); cx.sGD.bump(16)
            brow_gd = cx.sGD.v

            for half in range(2):
                hh0 = 8 * half
                cx.w(VE, cx.sSC)
                VE.memset(nstat[:, :], 0.0)
                VE.memset(von1[:, :, :, HD:HD + 1], 1.0)
                VE.engine_nop().then_inc(cx.sVE.h, 1); cx.sVE.bump()
                for wi, (wsrc, brix) in enumerate(
                        [(wqT, 0), (wkT, 1), (wvT, 2)]):
                    cx.w(GP, cx.sTE)
                    GP.dma_start(out=Wbuf[:, :, :],
                                 in_=chunked(wsrc)).then_inc(cx.sGD.h, 16)
                    cx.sGD.bump(16)
                    w_gd = cx.sGD.v
                    for tt in range(NT):
                        rb = 0
                        if wi != 2:
                            cx.w(SY, cx.sVE)
                            SY.dma_start(
                                out=rcos[:, rb, :],
                                in_=tiled(ropecos)[:, tt, :]).then_inc(
                                cx.sRS.h, 16); cx.sRS.bump(16)
                            SY.dma_start(
                                out=rsin[:, rb, :],
                                in_=tiled(ropesin)[:, tt, :]).then_inc(
                                cx.sRS.h, 16); cx.sRS.bump(16)
                        rs_here = cx.sRS.v
                        cx.gTE()
                        if tt == 0:
                            cx.w(TE, cx.sGD, w_gd)
                            if wi == 0 and half == 0:
                                cx.w(TE, cx.sXP, cx.sXP.v)
                        for d in range(ND):
                            TE.matmul(
                                pA[:, 0:512],
                                P["xT"][:, d, 128 * tt:128 * (tt + 1)],
                                Wbuf[:, d, 512 * half:512 * (half + 1)],
                                start=(d == 0), stop=False,
                                skip_group_check=True)
                        t = TE.matmul(pA[:, 0:512], onesr[:, :],
                                      brow4[0:1, brix,
                                            512 * half:512 * (half + 1)],
                                      start=False, stop=True,
                                      skip_group_check=True)
                        t.then_inc(cx.sTE.h, 1); cx.sTE.bump()
                        if wi == 2:
                            cx.w(SC, cx.sTE)
                            SC.activation(
                                von1[:, tt, :, 0:HD],
                                pA[:, 0:512].rearrange(
                                    "p (h d) -> p h d", d=HD),
                                AF.Identity).then_inc(
                                cx.sSC.h, 1); cx.sSC.bump()
                            continue
                        cx.w(SC, cx.sTE)
                        cx.w(SC, cx.sXP)
                        SC.activation(qtile[:, :], pA[:, 0:512],
                                      AF.Identity).then_inc(cx.sSC.h, 1)
                        cx.sSC.bump()
                        qh = qtile[:, :].rearrange("p (h d) -> p h d", d=HD)
                        th = rtmp[:, :].rearrange("p (h d) -> p h d", d=HD)
                        cs = rcos[:, rb, 512 * half:512 * (half + 1)]
                        sn = rsin[:, rb, 512 * half:512 * (half + 1)].rearrange(
                            "p (h d) -> p h d", d=HD)
                        cx.w(VE, cx.sSC)
                        cx.w(VE, cx.sRS, rs_here)
                        VE.tensor_mul(th[:, :, 0:32], qh[:, :, 32:64],
                                      sn[:, :, 0:32])
                        VE.tensor_mul(th[:, :, 32:64], qh[:, :, 0:32],
                                      sn[:, :, 32:64])
                        VE.tensor_mul(qtile[:, :], qtile[:, :], cs)
                        VE.tensor_add(qtile[:, :], qtile[:, :], rtmp[:, :])
                        VE.tensor_mul(nsq[:, :], qtile[:, :], qtile[:, :])
                        VE.reduce_sum(
                            P["col"][:, 8:16],
                            nsq[:, :].rearrange("p (h d) -> p h d", d=HD),
                            axis=AX.X)
                        VE.tensor_max(nstat[:, 8 * wi:8 * (wi + 1)],
                                      nstat[:, 8 * wi:8 * (wi + 1)],
                                      P["col"][:, 8:16])
                        VE.engine_nop().then_inc(cx.sVE.h, 1); cx.sVE.bump()
                        dstT = qroT if wi == 0 else kroT
                        cx.w(SY, cx.sVE)
                        cx.w(SY, cx.sGD)
                        for d4 in range(4):
                            SY.dma_start_transpose(
                                dstT[:, d4, 128 * tt:128 * (tt + 1)],
                                qtile[:, 128 * d4:128 * (d4 + 1)]).then_inc(
                                cx.sXP.h, 16)
                            cx.sXP.bump(16)

                cx.w(SY, cx.sVE)
                SY.dma_start(out=ghbm, in_=nstat[:, :]).then_inc(cx.sXP.h, 16)
                cx.sXP.bump(16)
                cx.w(SY, cx.sXP)
                SY.dma_start(out=growL[0:1, :],
                             in_=ghbm.rearrange("p s -> (p s)").unsqueeze(0)).then_inc(
                    cx.sXP.h, 16); cx.sXP.bump(16)
                cx.w(VE, cx.sXP)
                VE.reduce_max(srow[0:1, 0:16],
                              growL[0:1, :].rearrange("a (p s) -> a s p", s=16),
                              axis=AX.X)
                VE.engine_nop().then_inc(cx.sVE.h, 1); cx.sVE.bump()
                cx.w(VE, cx.sVE)
                VE.tensor_mul(srow[0:1, 16:24], srow[0:1, 0:8],
                              srow[0:1, 8:16])
                VE.engine_nop().then_inc(cx.sVE.h, 1); cx.sVE.bump()
                cx.w(SC, cx.sVE)
                SC.activation(srow[0:1, 24:32], srow[0:1, 16:24],
                              AF.Sqrt).then_inc(cx.sSC.h, 1); cx.sSC.bump()
                cx.w(VE, cx.sSC)
                VE.tensor_scalar_mul(srow[0:1, 32:40], srow[0:1, 24:32], -S8)
                VE.engine_nop().then_inc(cx.sVE.h, 1); cx.sVE.bump()
                emit_bcast_cols(cx, P, srow[0:1, 32:40],
                                negsB[:, hh0:hh0 + 8], 8)

                for hl in range(8):
                    h = hh0 + hl
                    ch, rows = hl // 2, 64 * (hl % 2)
                    for kt in range(NT):
                        q0 = 128 * kt
                        b0 = q0 // 512
                        cx.gTE()
                        segv = {}
                        for b in range(b0, 4):
                            lo, hi2 = max(512 * b, q0), 512 * (b + 1)
                            t = TE.matmul(pA[:, lo:hi2],
                                          kroT[rows:rows + 64, ch, q0:q0 + 128],
                                          qroT[rows:rows + 64, ch, lo:hi2],
                                          start=True, stop=True,
                                          tile_position=(rows, 0))
                            if b == b0:
                                t = TE.matmul(pA[:, q0:q0 + 128], idn[:, :],
                                              madd[:, :], start=False,
                                              stop=True,
                                              skip_group_check=True)
                            t.then_inc(cx.sTE.h, 1); cx.sTE.bump()
                            segv[b] = cx.sTE.v
                        expv = {}
                        for b in range(b0, 4):
                            lo, hi2 = max(512 * b, q0), 512 * (b + 1)
                            SC.wait_ge(cx.sTE.h, segv[b])
                            SC.activation(probsT[:, lo:hi2], pA[:, lo:hi2],
                                          AF.Exp, bias=negsB[:, h:h + 1],
                                          scale=S8).then_inc(cx.sSC.h, 1)
                            cx.sSC.bump()
                            expv[b] = cx.sSC.v
                        for b in range(b0, 4):
                            lo, hi2 = max(512 * b, q0), 512 * (b + 1)
                            TE.wait_ge(cx.sSC.h, expv[b])
                            t = TE.matmul(
                                pB[0:HD + 1, lo:hi2],
                                von1[:, kt, hl, :],
                                probsT[:, lo:hi2],
                                start=(kt == 0),
                                stop=(kt == min(4 * b + 3, NT - 1)),
                                skip_group_check=True)
                            t.then_inc(cx.sTE.h, 1); cx.sTE.bump()
                    cx.w(VE, cx.sTE)
                    VE.reciprocal(reciprow[0:1, :], pB[HD:HD + 1, :])
                    cx.vfence()
                    VE.tensor_copy(recipbf[0:1, :],
                                   reciprow[0:1, :]).then_inc(cx.sVE.h, 1)
                    cx.sVE.bump()
                    cx.gTE()
                    for b in range(4):
                        t = TE.matmul(pA[0:HD, 512 * b:512 * (b + 1)],
                                      onesr[:, 0:HD],
                                      recipbf[0:1, 512 * b:512 * (b + 1)],
                                      start=True, stop=True)
                    t.then_inc(cx.sTE.h, 1); cx.sTE.bump()
                    cx.w(SC, cx.sTE)
                    cx.w(SC, cx.sGD)
                    cx.w(SC, cx.sVE)
                    SC.activation(avst[:, :], pB[0:HD, :],
                                  AF.Identity).then_inc(cx.sSC.h, 1)
                    cx.sSC.bump()
                    cx.w(VE, cx.sSC)
                    VE.tensor_mul(avst[:, :], avst[:, :],
                                  pA[0:HD, :]).then_inc(cx.sVE.h, 1)
                    cx.sVE.bump()
                    cx.w(GP, cx.sVE)
                    GP.dma_start(out=avT[rows:rows + 64, ch + 4 * half, :],
                                 in_=avst[:, :]).then_inc(cx.sGD.h, 16)
                    cx.sGD.bump(16)

            cx.w(GP, cx.sTE)
            GP.dma_start(out=Wbuf[:, :, :],
                         in_=chunked(woT)).then_inc(cx.sGD.h, 16)
            cx.sGD.bump(16)
            wo_gd = cx.sGD.v
            for tt in range(NT):
                bb = tt % 2
                cx.w(SY, cx.sVE)
                cx.w(SY, cx.sXP)
                SY.dma_start(out=P["xres"][:, bb, :],
                             in_=tiled(xspillB)[:, tt, :]).then_inc(
                    cx.sRS.h, 16); cx.sRS.bump(16)
                cx.gTE()
                if tt == 0:
                    cx.w(TE, cx.sGD, wo_gd)
                for b in range(2):
                    for dd in range(ND):
                        ch, rows = dd % 4, 64 * (dd // 4)
                        TE.matmul(pB[:, 512 * b:512 * (b + 1)],
                                  avT[:, dd, 128 * tt:128 * (tt + 1)],
                                  Wbuf[:, dd, 512 * b:512 * (b + 1)],
                                  start=(dd == 0), stop=False,
                                  skip_group_check=True)
                cx.w(TE, cx.sRS)
                for b in range(2):
                    t = TE.matmul(pB[:, 512 * b:512 * (b + 1)], onesr[:, :],
                                  brow4[0:1, 3, 512 * b:512 * (b + 1)],
                                  start=False, stop=True,
                                  skip_group_check=True)
                t.then_inc(cx.sTE.h, 1); cx.sTE.bump()
                emit_ln_tile(cx, P, pB[:, 0:C], P["xres"][:, bb, :],
                             P["xln"][:, bb, :],
                             out_wait=[(cx.sXP, cx.sXP.v), (cx.sST, cx.sST.v)])
                cx.w(SY, cx.sSC)
                SY.dma_start(out=tiled(x1_out)[:, tt, :],
                             in_=P["xln"][:, bb, :]).then_inc(cx.sST.h, 16)
                cx.sST.bump(16)
                cx.w(VE, cx.sSC)
                VE.reduce_max(P["col"][:, 0:1], P["xln"][:, bb, :], axis=AX.X,
                              apply_absolute_value=True)
                VE.tensor_max(P["runmax"][:, 0:1], P["runmax"][:, 0:1],
                              P["col"][:, 0:1]).then_inc(cx.sVE.h, 1)
                cx.sVE.bump()

        emit_colmax_scalar(cx, P, bhbm, P["runmax"][:, 0:1], srow[0:1, 0:1])
        cx.w(SY, cx.sVE)
        SY.dma_start(out=m1_out, in_=srow[0:1, 0:1]).then_inc(cx.sST.h, 16)
        cx.sST.bump(16)
        cx.w(GP, cx.sST)
        blk = perm.enter_context(nc.Block())
        cx.rec.replay(blk)
        return nc, inp


def _emit_p1_tail(cx, P, xsp, x1_out, m1_out, bhbm):
    SY, VE, GP = cx.SY, cx.VE, cx.GP
    for tt in range(NT):
        bb = tt % 2
        cx.w(SY, cx.sVE)
        cx.w(SY, cx.sST)
        cx.w(SY, cx.sXP)
        SY.dma_start(out=P["xres"][:, bb, :],
                     in_=tiled(xsp)[:, tt, :]).then_inc(cx.sRS.h, 16)
        cx.sRS.bump(16)
        cx.w(SY, cx.sRS)
        SY.dma_start(out=tiled(x1_out)[:, tt, :],
                     in_=P["xres"][:, bb, :]).then_inc(cx.sST.h, 16)
        cx.sST.bump(16)
        cx.w(VE, cx.sRS)
        VE.reduce_max(P["col"][:, 0:1], P["xres"][:, bb, :], axis=AX.X,
                      apply_absolute_value=True)
        VE.tensor_max(P["runmax"][:, 0:1], P["runmax"][:, 0:1],
                      P["col"][:, 0:1]).then_inc(cx.sVE.h, 1)
        cx.sVE.bump()
    emit_colmax_scalar(cx, P, bhbm, P["runmax"][:, 0:1], P["srow"][0:1, 0:1])
    cx.w(SY, cx.sVE)
    SY.dma_start(out=m1_out, in_=P["srow"][0:1, 0:1]).then_inc(cx.sST.h, 16)
    cx.sST.bump(16)
    cx.w(GP, cx.sST)


def build_pn(stage):
    nc = bass.Bass("TRN2", target_bir_lowering=False)
    dt = nc.dram_tensor
    inp = {}

    def I(name, shape, dtyp=F32):
        inp[name] = dt(name, shape, dtyp, kind="ExternalInput").ap()
        return inp[name]

    def O(name, shape, dtyp=F32):
        return dt(name, shape, dtyp, kind="ExternalOutput").ap()

    SPW = NH * T
    if stage == 2:
        x1 = I("x1", [T, C])
        wgate = I("wgate", [FH, C])
        wup = I("wup", [FH, C])
        M1 = I("M1", [1, 8])
        mo = O("m_out", [1, 2])
        gate_sp = O("gate_sp", [128, SPW])
        up_sp = O("up_sp", [128, SPW])
    elif stage == 3:
        gate_spi = I("gate_sp", [128, SPW])
        M2 = I("M2", [2, 8])
        mo = O("m_out", [1, 1])
        gq_sp = O("gq_sp", [128, SPW])
        squ_sp = O("squ_sp", [128, SPW])
    elif stage == 4:
        up_spi = I("up_sp", [128, SPW])
        gq_spi = I("gq_sp", [128, SPW])
        squ_spi = I("squ_sp", [128, SPW])
        M2 = I("M2", [2, 8])
        M3 = I("M3", [1, 8])
        mo = O("m_out", [1, 1])
        oraw_sp = O("oraw_sp", [128, SPW])
    else:
        x1 = I("x1", [T, C])
        wdown = I("wdown", [C, FH])
        oraw_spi = I("oraw_sp", [128, SPW])
        M2 = I("M2", [2, 8])
        M3 = I("M3", [1, 8])
        M4 = I("M4", [1, 8])
        y_out = O("y_out", [T, C])
    I("ones_row", [1, 128], BF16)
    I("ones_f32", [1, 128])
    bhbm = dt("bhbm", [128, 1], F32).ap()

    with ExitStack() as perm:
        E = perm.enter_context
        st = nc.sbuf_tensor
        cx = Cx(nc, E)
        SY, GP, VE, SC, TE = cx.SY, cx.GP, cx.VE, cx.SC, cx.TE
        P = alloc_perm(cx, E, with_xT=False)
        pA, pB = P["pA"], P["pB"]
        srow = P["srow"]
        scol = E(st("scol", [128, 16], F32))
        fscr = E(st("fscr", [128, T], F32))
        fscr2 = E(st("fscr2", [128, T], F32))
        iscr = E(st("iscr", [128, T], I32))
        c32k = E(st("c32k", [128, 1], F32))
        Ms = E(st("Ms", [1, 48], F32))

        if stage == 2:
            SY.dma_start(out=Ms[0:1, 0:8], in_=M1).then_inc(cx.sLD.h, 16)
            cx.sLD.bump(16)
        if stage >= 3:
            SY.dma_start(out=Ms[0:1, 8:24],
                         in_=M2.rearrange("b c -> (b c)").unsqueeze(0)).then_inc(
                cx.sLD.h, 16); cx.sLD.bump(16)
        if stage >= 4:
            SY.dma_start(out=Ms[0:1, 24:32], in_=M3).then_inc(cx.sLD.h, 16)
            cx.sLD.bump(16)
        if stage >= 5:
            SY.dma_start(out=Ms[0:1, 32:40], in_=M4).then_inc(cx.sLD.h, 16)
            cx.sLD.bump(16)
        load_base_consts(cx, P, inp)
        VE.memset(srow[0:1, :], 0.0)
        VE.memset(c32k[:], 32768.0)
        cx.w(VE, cx.sLD)

        def mk_scale(ds, dr, src_row):
            VE.reduce_max(srow[0:1, 40:41], src_row, axis=AX.X)
            VE.tensor_scalar(srow[0:1, ds:ds + 1], srow[0:1, 40:41],
                             1.0 / QA, 1e-12, ALU.mult, ALU.max)
            VE.reciprocal(srow[0:1, dr:dr + 1], srow[0:1, ds:ds + 1])
            cx.vfence()

        if stage == 2:
            mk_scale(0, 1, Ms[0:1, 0:8])
            VE.tensor_scalar(srow[0:1, 14:15], srow[0:1, 0:1], -32768.0,
                             None, ALU.mult)
        if stage >= 3:
            mk_scale(2, 3, Ms[0:1, 8:16])
            mk_scale(4, 5, Ms[0:1, 16:24])
            VE.tensor_scalar(srow[0:1, 13:14], srow[0:1, 2:3], -32768.0,
                             None, ALU.mult)
        if stage >= 4:
            VE.reduce_max(srow[0:1, 40:41], Ms[0:1, 24:32], axis=AX.X)
            VE.tensor_tensor(srow[0:1, 41:42], srow[0:1, 40:41],
                             srow[0:1, 2:3], ALU.mult)
            VE.tensor_scalar_mul(srow[0:1, 41:42], srow[0:1, 41:42],
                                 1.0 / 65536.0)
            VE.tensor_scalar(srow[0:1, 6:7], srow[0:1, 41:42],
                             1.0 / QA, 1e-12, ALU.mult, ALU.max)
            VE.reciprocal(srow[0:1, 7:8], srow[0:1, 6:7])
            cx.vfence()
            VE.tensor_tensor(srow[0:1, 10:11], srow[0:1, 2:3],
                             srow[0:1, 7:8], ALU.mult)
            VE.tensor_scalar_mul(srow[0:1, 10:11], srow[0:1, 10:11],
                                 1.0 / 65536.0)
        if stage >= 5:
            VE.reduce_max(srow[0:1, 40:41], Ms[0:1, 32:40], axis=AX.X)
            VE.tensor_tensor(srow[0:1, 41:42], srow[0:1, 40:41],
                             srow[0:1, 6:7], ALU.mult)
            VE.tensor_tensor(srow[0:1, 41:42], srow[0:1, 41:42],
                             srow[0:1, 4:5], ALU.mult)
            VE.tensor_scalar(srow[0:1, 8:9], srow[0:1, 41:42],
                             1.0 / QA, 1e-12, ALU.mult, ALU.max)
            VE.reciprocal(srow[0:1, 9:10], srow[0:1, 8:9])
            cx.vfence()
            VE.tensor_tensor(srow[0:1, 11:12], srow[0:1, 6:7],
                             srow[0:1, 4:5], ALU.mult)
            VE.tensor_tensor(srow[0:1, 11:12], srow[0:1, 11:12],
                             srow[0:1, 9:10], ALU.mult)
            VE.tensor_scalar(srow[0:1, 12:13], srow[0:1, 8:9], -32768.0,
                             None, ALU.mult)
        VE.engine_nop().then_inc(cx.sVE.h, 1); cx.sVE.bump()
        emit_bcast_cols(cx, P, srow[0:1, 0:15], scol[:, 0:15], 15)

        def act_round(src_ap, scale_col, dst_ap):
            cx.w(SC, cx.sVE)
            cx.w(SC, cx.sSC)
            SC.activation(fscr[:, :], src_ap, AF.Identity,
                          scale=scale_col,
                          bias=c32k[:, 0:1]).then_inc(cx.sSC.h, 1)
            cx.sSC.bump()
            cx.w(SC, cx.sSC)
            SC.activation(iscr[:, :], fscr[:, :],
                          AF.Identity).then_inc(cx.sSC.h, 1)
            cx.sSC.bump()
            cx.w(VE, cx.sSC)
            VE.tensor_tensor(fscr2[:, :], fscr[:, :], iscr[:, :],
                             ALU.subtract)
            VE.scalar_tensor_tensor(dst_ap, fscr2[:, :], 0.5, iscr[:, :],
                                    ALU.is_ge, ALU.add)

        if stage == 2:
            with ExitStack() as ph:
                Ep = ph.enter_context
                P["xT"] = Ep(st("xT", [128, ND, T], BF16))
                wstg = Ep(st("wstg", [128, 2, 2, C], F32))
                wqd = Ep(st("wqd", [128, 2, 2, C], BF16))
                wTr = Ep(st("wTr", [128, 2, 2, ND, 128], BF16))
                sblk = Ep(st("sblk", [128, 32], F32))
                rblk = Ep(st("rblk", [128, 32], F32))
                gsp = Ep(st("gsp", [128, 2, T], F32))
                usp = Ep(st("usp", [128, 2, T], F32))

                for tt in range(NT):
                    bb = tt % 2
                    cx.w(SY, cx.sVE)
                    SY.dma_start(out=P["xres"][:, bb, :],
                                 in_=tiled(x1)[:, tt, :]).then_inc(
                        cx.sRS.h, 16)
                    cx.sRS.bump(16)
                    cx.w(SC, cx.sRS)
                    cx.w(SC, cx.sVE)
                    SC.activation(fscr[:, 0:C], P["xres"][:, bb, :],
                                  AF.Identity, scale=scol[:, 1:2],
                                  bias=c32k[:, 0:1]).then_inc(cx.sSC.h, 1)
                    cx.sSC.bump()
                    cx.w(SC, cx.sSC)
                    SC.activation(iscr[:, 0:C], fscr[:, 0:C],
                                  AF.Identity).then_inc(cx.sSC.h, 1)
                    cx.sSC.bump()
                    cx.w(VE, cx.sSC)
                    VE.tensor_tensor(fscr2[:, 0:C], fscr[:, 0:C],
                                     iscr[:, 0:C], ALU.subtract)
                    VE.scalar_tensor_tensor(fscr[:, 0:C], fscr2[:, 0:C], 0.5,
                                            iscr[:, 0:C], ALU.is_ge, ALU.add)
                    VE.tensor_scalar(P["xln"][:, bb, :], fscr[:, 0:C],
                                     scol[:, 0:1], scol[:, 14:15],
                                     ALU.mult, ALU.add)
                    VE.engine_nop().then_inc(cx.sVE.h, 1); cx.sVE.bump()
                    emit_xT_tile(cx, P, P["xln"][:, bb, :], tt)
                xq_xp = cx.sXP.v

                ve_chunk, te_mark, xpm, sp_mark, sc_mark = {}, {}, {}, {}, {}
                spilled = set()

                def emit_spill(j, force=False):
                    j = j if force else j - 1
                    if j < 0 or j in spilled or j not in sc_mark:
                        return
                    spilled.add(j)
                    jb = j % 2
                    cx.w(SY, cx.sSC, sc_mark[j])
                    SY.wait_ge(cx.sST.h, cx.sST.v)
                    SY.dma_start(out=gate_sp[:, T * j:T * (j + 1)],
                                 in_=gsp[:, jb, :]).then_inc(cx.sST.h, 16)
                    cx.sST.bump(16)
                    SY.dma_start(out=up_sp[:, T * j:T * (j + 1)],
                                 in_=usp[:, jb, :]).then_inc(cx.sST.h, 16)
                    cx.sST.bump(16)
                    sp_mark[j] = cx.sST.v

                for hh in range(NH):
                    bb = hh % 2
                    for wi in range(2):
                        wsrc_d = wgate if wi == 0 else wup
                        if hh >= 2 and wi == 0:
                            GP.wait_ge(cx.sVE.h, ve_chunk[hh - 2])
                        GP.wait_ge(cx.sGD.h, cx.sGD.v)
                        GP.dma_start(
                            out=wstg[:, bb, wi, :],
                            in_=wsrc_d[128 * hh:128 * (hh + 1),
                                       :]).then_inc(cx.sGD.h, 16)
                        cx.sGD.bump(16)
                        gd_here = cx.sGD.v
                        cx.w(VE, cx.sGD, gd_here)
                        if (hh - 2, wi) in xpm:
                            VE.wait_ge(cx.sXP.h, xpm[(hh - 2, wi)])
                        w3 = wstg[:, bb, wi, :].rearrange(
                            "p (b i) -> p b i", i=32)
                        fv = fscr[:, 0:C].rearrange("p (b i) -> p b i", i=32)
                        VE.reduce_max(sblk[:, :], w3, axis=AX.X,
                                      apply_absolute_value=True)
                        VE.tensor_scalar(sblk[:, :], sblk[:, :], 1.0 / 7.0,
                                         1e-12, ALU.mult, ALU.max)
                        VE.reciprocal(rblk[:, :], sblk[:, :])
                        cx.vfence()
                        VE.tensor_tensor(
                            fv, w3,
                            rblk[:, :].unsqueeze(2).broadcast_to(
                                [128, 32, 32]), ALU.mult)
                        VE.tensor_scalar(fscr[:, 0:C], fscr[:, 0:C], 16.0,
                                         None, ALU.add)
                        VE.engine_nop().then_inc(cx.sVE.h, 1); cx.sVE.bump()
                        cx.w(SC, cx.sVE)
                        SC.activation(iscr[:, 0:C], fscr[:, 0:C],
                                      AF.Identity).then_inc(cx.sSC.h, 1)
                        cx.sSC.bump()
                        cx.w(VE, cx.sSC)
                        VE.tensor_tensor(fscr2[:, 0:C], fscr[:, 0:C],
                                         iscr[:, 0:C], ALU.subtract)
                        VE.scalar_tensor_tensor(fscr[:, 0:C], fscr2[:, 0:C],
                                                0.5, iscr[:, 0:C],
                                                ALU.is_ge, ALU.add)
                        VE.tensor_scalar(fscr[:, 0:C], fscr[:, 0:C], -16.0,
                                         None, ALU.add)
                        VE.tensor_tensor(
                            wqd[:, bb, wi, :].rearrange(
                                "p (b i) -> p b i", i=32),
                            fv,
                            sblk[:, :].unsqueeze(2).broadcast_to(
                                [128, 32, 32]), ALU.mult)
                        VE.engine_nop().then_inc(cx.sVE.h, 1); cx.sVE.bump()
                        cx.w(SY, cx.sVE)
                        cx.w(SY, cx.sGD)
                        if hh >= 2:
                            SY.wait_ge(cx.sTE.h, te_mark[hh - 2])
                        SY.wait_ge(cx.sXP.h, cx.sXP.v)
                        for d in range(ND):
                            SY.dma_start_transpose(
                                wTr[:, bb, wi, d, :],
                                wqd[:, bb, wi,
                                    128 * d:128 * (d + 1)]).then_inc(
                                cx.sXP.h, 16)
                            cx.sXP.bump(16)
                        xpm[(hh, wi)] = cx.sXP.v
                    emit_spill(hh)
                    cx.gTE()
                    cx.w(TE, cx.sXP, xpm[(hh, 1)])
                    if hh == 0:
                        cx.w(TE, cx.sXP, xq_xp)
                    for wi, p in ((0, pA), (1, pB)):
                        for b in range(4):
                            for d in range(ND):
                                t = TE.matmul(
                                    p[:, 512 * b:512 * (b + 1)],
                                    wTr[:, bb, wi, d, :],
                                    P["xT"][:, d, 512 * b:512 * (b + 1)],
                                    start=(d == 0), stop=(d == ND - 1))
                        t.then_inc(cx.sTE.h, 1); cx.sTE.bump()
                    te_here = cx.sTE.v
                    te_mark[hh] = te_here

                    cx.w(VE, cx.sTE, te_here)
                    VE.reduce_max(P["col"][:, 0:1], pA[:, :], axis=AX.X,
                                  apply_absolute_value=True)
                    VE.tensor_max(P["runmax"][:, 0:1], P["runmax"][:, 0:1],
                                  P["col"][:, 0:1])
                    VE.reduce_max(P["col"][:, 1:2], pB[:, :], axis=AX.X,
                                  apply_absolute_value=True)
                    VE.tensor_max(P["runmax"][:, 1:2], P["runmax"][:, 1:2],
                                  P["col"][:, 1:2]).then_inc(cx.sVE.h, 1)
                    cx.sVE.bump()
                    ve_chunk[hh] = cx.sVE.v
                    cx.w(SC, cx.sTE, te_here)
                    cx.w(SC, cx.sVE)
                    if hh >= 2:
                        SC.wait_ge(cx.sST.h, sp_mark[hh - 2])
                    SC.activation(gsp[:, bb, :], pA[:, :], AF.Identity)
                    SC.activation(usp[:, bb, :], pB[:, :],
                                  AF.Identity).then_inc(cx.sSC.h, 1)
                    cx.sSC.bump()
                    sc_mark[hh] = cx.sSC.v
                emit_spill(NH - 1, force=True)

            for j in range(2):
                emit_colmax_scalar(cx, P, bhbm, P["runmax"][:, j:j + 1],
                                   srow[0:1, 48 + j:49 + j])
            cx.w(SY, cx.sVE)
            SY.dma_start(out=mo, in_=srow[0:1, 48:50]).then_inc(cx.sLD.h, 16)
            cx.sLD.bump(16)
            cx.w(GP, cx.sST)
            cx.w(GP, cx.sLD)

        elif stage == 3:
            gbuf = E(st("gbuf", [128, 2, T], F32))
            f3buf = E(st("f3buf", [128, 2, T], F32))
            sqbuf = E(st("sqbuf", [128, 2, T], F32))
            ve_mark, sp_mark = {}, {}
            for hh in range(NH):
                bb = hh % 2
                if hh >= 2:
                    GP.wait_ge(cx.sVE.h, ve_mark[hh - 2])
                GP.wait_ge(cx.sGD.h, cx.sGD.v)
                GP.dma_start(out=gbuf[:, bb, :],
                             in_=gate_spi[:, T * hh:T * (hh + 1)]).then_inc(
                    cx.sGD.h, 16)
                cx.sGD.bump(16)
                gd_here = cx.sGD.v
                cx.w(VE, cx.sGD, gd_here)
                if hh >= 2:
                    VE.wait_ge(cx.sST.h, sp_mark[hh - 2])
                VE.engine_nop().then_inc(cx.sVE.h, 1); cx.sVE.bump()
                act_round(gbuf[:, bb, :], scol[:, 3:4], f3buf[:, bb, :])
                VE.engine_nop().then_inc(cx.sVE.h, 1); cx.sVE.bump()
                cx.w(SC, cx.sVE)
                SC.activation(fscr[:, :], f3buf[:, bb, :], AF.Sigmoid,
                              scale=scol[:, 2:3],
                              bias=scol[:, 13:14]).then_inc(cx.sSC.h, 1)
                cx.sSC.bump()
                cx.w(VE, cx.sSC)
                VE.engine_nop().then_inc(cx.sVE.h, 1); cx.sVE.bump()
                act_round(fscr[:, :], 65536.0, sqbuf[:, bb, :])
                VE.tensor_scalar(sqbuf[:, bb, :], sqbuf[:, bb, :], -32768.0,
                                 65535.0, ALU.add, ALU.min)
                VE.scalar_tensor_tensor(fscr2[:, :], f3buf[:, bb, :],
                                        -32768.0, sqbuf[:, bb, :],
                                        ALU.add, ALU.mult)
                VE.reduce_max(P["col"][:, 0:1], fscr2[:, :], axis=AX.X,
                              apply_absolute_value=True)
                VE.tensor_max(P["runmax"][:, 0:1], P["runmax"][:, 0:1],
                              P["col"][:, 0:1]).then_inc(cx.sVE.h, 1)
                cx.sVE.bump()
                ve_mark[hh] = cx.sVE.v
                cx.w(SY, cx.sVE)
                SY.wait_ge(cx.sST.h, cx.sST.v)
                SY.dma_start(out=gq_sp[:, T * hh:T * (hh + 1)],
                             in_=f3buf[:, bb, :]).then_inc(cx.sST.h, 16)
                cx.sST.bump(16)
                SY.dma_start(out=squ_sp[:, T * hh:T * (hh + 1)],
                             in_=sqbuf[:, bb, :]).then_inc(cx.sST.h, 16)
                cx.sST.bump(16)
                sp_mark[hh] = cx.sST.v
            emit_colmax_scalar(cx, P, bhbm, P["runmax"][:, 0:1],
                               srow[0:1, 48:49])
            cx.w(SY, cx.sVE)
            SY.dma_start(out=mo, in_=srow[0:1, 48:49]).then_inc(cx.sST.h, 16)
            cx.sST.bump(16)
            cx.w(GP, cx.sST)

        elif stage == 4:
            f3buf = E(st("f3buf", [128, 2, T], F32))
            sqbuf = E(st("sqbuf", [128, 2, T], F32))
            ubuf = E(st("ubuf", [128, 2, T], F32))
            obuf = E(st("obuf", [128, 2, T], F32))
            ve_mark, sp_mark = {}, {}
            for hh in range(NH):
                bb = hh % 2
                if hh >= 2:
                    GP.wait_ge(cx.sVE.h, ve_mark[hh - 2])
                GP.wait_ge(cx.sGD.h, cx.sGD.v)
                GP.dma_start(out=f3buf[:, bb, :],
                             in_=gq_spi[:, T * hh:T * (hh + 1)]).then_inc(
                    cx.sGD.h, 16)
                cx.sGD.bump(16)
                GP.dma_start(out=sqbuf[:, bb, :],
                             in_=squ_spi[:, T * hh:T * (hh + 1)]).then_inc(
                    cx.sGD.h, 16)
                cx.sGD.bump(16)
                GP.dma_start(out=ubuf[:, bb, :],
                             in_=up_spi[:, T * hh:T * (hh + 1)]).then_inc(
                    cx.sGD.h, 16)
                cx.sGD.bump(16)
                gd_here = cx.sGD.v
                cx.w(VE, cx.sGD, gd_here)
                VE.scalar_tensor_tensor(fscr2[:, :], f3buf[:, bb, :],
                                        -32768.0, sqbuf[:, bb, :],
                                        ALU.add, ALU.mult)
                VE.engine_nop().then_inc(cx.sVE.h, 1); cx.sVE.bump()
                act_round(fscr2[:, :], scol[:, 10:11], f3buf[:, bb, :])
                VE.engine_nop().then_inc(cx.sVE.h, 1); cx.sVE.bump()
                act_round(ubuf[:, bb, :], scol[:, 5:6], sqbuf[:, bb, :])
                VE.tensor_scalar(sqbuf[:, bb, :], sqbuf[:, bb, :], -32768.0,
                                 None, ALU.add)
                if hh >= 2:
                    VE.wait_ge(cx.sST.h, sp_mark[hh - 2])
                VE.scalar_tensor_tensor(obuf[:, bb, :], f3buf[:, bb, :],
                                        -32768.0, sqbuf[:, bb, :],
                                        ALU.add, ALU.mult)
                VE.reduce_max(P["col"][:, 0:1], obuf[:, bb, :], axis=AX.X,
                              apply_absolute_value=True)
                VE.tensor_max(P["runmax"][:, 0:1], P["runmax"][:, 0:1],
                              P["col"][:, 0:1]).then_inc(cx.sVE.h, 1)
                cx.sVE.bump()
                ve_mark[hh] = cx.sVE.v
                cx.w(SY, cx.sVE)
                SY.wait_ge(cx.sST.h, cx.sST.v)
                SY.dma_start(out=oraw_sp[:, T * hh:T * (hh + 1)],
                             in_=obuf[:, bb, :]).then_inc(cx.sST.h, 16)
                cx.sST.bump(16)
                sp_mark[hh] = cx.sST.v
            emit_colmax_scalar(cx, P, bhbm, P["runmax"][:, 0:1],
                               srow[0:1, 48:49])
            cx.w(SY, cx.sVE)
            SY.dma_start(out=mo, in_=srow[0:1, 48:49]).then_inc(cx.sST.h, 16)
            cx.sST.bump(16)
            cx.w(GP, cx.sST)

        else:
            oq = E(st("oq", [128, NH, T], BF16))
            with ExitStack() as p1s:
                obuf = p1s.enter_context(st("obuf", [128, 1, T], F32))
                sc_mark = {}
                for hh in range(NH):
                    if hh >= 1:
                        GP.wait_ge(cx.sSC.h, sc_mark[hh - 1])
                    GP.wait_ge(cx.sGD.h, cx.sGD.v)
                    GP.dma_start(
                        out=obuf[:, 0, :],
                        in_=oraw_spi[:, T * hh:T * (hh + 1)]).then_inc(
                        cx.sGD.h, 16)
                    cx.sGD.bump(16)
                    gd_here = cx.sGD.v
                    cx.w(SC, cx.sGD, gd_here)
                    act_round(obuf[:, 0, :], scol[:, 11:12], fscr2[:, :])
                    sc_mark[hh] = cx.sSC.v
                    VE.tensor_scalar(oq[:, hh, :], fscr2[:, :],
                                     scol[:, 8:9], scol[:, 12:13],
                                     ALU.mult, ALU.add).then_inc(cx.sVE.h, 1)
                    cx.sVE.bump()

            with ExitStack() as ph:
                Ep = ph.enter_context
                wstg2 = Ep(st("wstg2", [128, 1536], F32))
                wqd2 = Ep(st("wqd2", [128, FH], BF16))
                sblk2 = Ep(st("sblk2", [128, 48], F32))
                rblk2 = Ep(st("rblk2", [128, 48], F32))
                wdT = Ep(st("wdT", [128, NH, C], BF16))
                cx.w(GP, cx.sTE); cx.w(GP, cx.sSC); cx.w(GP, cx.sVE)
                ve_dd, xp_marks = {}, {}
                for dd in range(ND):
                    db = dd % 2
                    if dd >= 1:
                        GP.wait_ge(cx.sXP.h, xp_marks[dd - 1])
                    for hf in range(2):
                        if dd + hf >= 1:
                            GP.wait_ge(cx.sVE.h, cx.sVE.v)
                        GP.wait_ge(cx.sGD.h, cx.sGD.v)
                        GP.dma_start(
                            out=wstg2[:, :],
                            in_=wdown[128 * dd:128 * (dd + 1),
                                      1536 * hf:1536 * (hf + 1)]).then_inc(
                            cx.sGD.h, 16)
                        cx.sGD.bump(16)
                        gd2 = cx.sGD.v
                        cx.w(VE, cx.sGD, gd2)
                        if dd >= 1 and hf == 0:
                            VE.wait_ge(cx.sXP.h, xp_marks[dd - 1])
                        w3 = wstg2[:, :].rearrange("p (b i) -> p b i",
                                                   i=32)
                        c0 = 1536 * hf
                        VE.reduce_max(sblk2[:, :], w3, axis=AX.X,
                                      apply_absolute_value=True)
                        VE.tensor_scalar(sblk2[:, :], sblk2[:, :], 1.0 / 7.0,
                                         1e-12, ALU.mult, ALU.max)
                        VE.reciprocal(rblk2[:, :], sblk2[:, :])
                        cx.vfence()
                        VE.tensor_tensor(
                            fscr[:, 0:1536].rearrange(
                                "p (b i) -> p b i", i=32),
                            w3,
                            rblk2[:, :].unsqueeze(2).broadcast_to(
                                [128, 48, 32]),
                            ALU.mult)
                        VE.tensor_scalar(fscr[:, 0:1536], fscr[:, 0:1536],
                                         16.0, None, ALU.add)
                        VE.engine_nop().then_inc(cx.sVE.h, 1); cx.sVE.bump()
                        cx.w(SC, cx.sVE)
                        SC.activation(iscr[:, 0:1536], fscr[:, 0:1536],
                                      AF.Identity).then_inc(cx.sSC.h, 1)
                        cx.sSC.bump()
                        cx.w(VE, cx.sSC)
                        VE.tensor_tensor(fscr2[:, 0:1536], fscr[:, 0:1536],
                                         iscr[:, 0:1536], ALU.subtract)
                        VE.scalar_tensor_tensor(fscr[:, 0:1536],
                                                fscr2[:, 0:1536], 0.5,
                                                iscr[:, 0:1536], ALU.is_ge,
                                                ALU.add)
                        VE.tensor_scalar(fscr[:, 0:1536], fscr[:, 0:1536],
                                         -16.0, None, ALU.add)
                        VE.tensor_tensor(
                            wqd2[:, c0:c0 + 1536].rearrange(
                                "p (b i) -> p b i", i=32),
                            fscr[:, 0:1536].rearrange(
                                "p (b i) -> p b i", i=32),
                            sblk2[:, :].unsqueeze(2).broadcast_to(
                                [128, 48, 32]),
                            ALU.mult)
                        VE.engine_nop().then_inc(cx.sVE.h, 1); cx.sVE.bump()
                    ve_dd[dd] = cx.sVE.v
                    cx.w(SY, cx.sVE, ve_dd[dd])
                    cx.w(SY, cx.sGD)
                    SY.wait_ge(cx.sXP.h, cx.sXP.v)
                    for hh in range(NH):
                        SY.dma_start_transpose(
                            wdT[:, hh, 128 * dd:128 * (dd + 1)],
                            wqd2[:, 128 * hh:128 * (hh + 1)]).then_inc(
                            cx.sXP.h, 16)
                        cx.sXP.bump(16)
                    xp_marks[dd] = cx.sXP.v
                wd_xp = cx.sXP.v

                for tt in range(NT):
                    bb = tt % 2
                    cx.w(SY, cx.sVE)
                    SY.dma_start(out=P["xres"][:, bb, :],
                                 in_=tiled(x1)[:, tt, :]).then_inc(
                        cx.sRS.h, 16); cx.sRS.bump(16)
                    cx.gTE()
                    cx.w(TE, cx.sRS)
                    if tt == 0:
                        cx.w(TE, cx.sXP, wd_xp)
                    for hh in range(NH):
                        for b in range(2):
                            t = TE.matmul(
                                pB[:, 512 * b:512 * (b + 1)],
                                oq[:, hh, 128 * tt:128 * (tt + 1)],
                                wdT[:, hh, 512 * b:512 * (b + 1)],
                                start=(hh == 0), stop=(hh == NH - 1),
                                skip_group_check=True)
                    t.then_inc(cx.sTE.h, 1); cx.sTE.bump()
                    emit_ln_tile(cx, P, pB[:, 0:C], P["xres"][:, bb, :],
                                 P["xln"][:, bb, :],
                                 out_wait=[(cx.sST, cx.sST.v)])
                    cx.w(SY, cx.sSC)
                    SY.dma_start(out=tiled(y_out)[:, tt, :],
                                 in_=P["xln"][:, bb, :]).then_inc(
                        cx.sST.h, 16); cx.sST.bump(16)
            cx.w(GP, cx.sST)

        blk = perm.enter_context(nc.Block())
        cx.rec.replay(blk)
        return nc, inp


_cache = {}


def _get(key, builder):
    if key not in _cache:
        _cache[key] = builder()
    return _cache[key]


def kernel(**inputs):
    cst = host_constants()

    def f32(x):
        return np.ascontiguousarray(np.asarray(x), dtype=None)

    cattn_w = np.ascontiguousarray(inputs["L0_attn_c_attn_w"])
    cattn_bp = np.ascontiguousarray(
        np.asarray(inputs["L0_attn_c_attn_b"]).reshape(24, 128).T)
    cproj_w = np.ascontiguousarray(inputs["L0_attn_c_proj_w"])
    cproj_b = np.ascontiguousarray(
        np.asarray(inputs["L0_attn_c_proj_b"]).reshape(1, C))
    wk_c = np.ascontiguousarray(inputs["L0_attn_wk_c"])
    wk_eT = np.ascontiguousarray(np.asarray(inputs["L0_attn_wk_e"]).T)
    wv_c = np.ascontiguousarray(inputs["L0_attn_wv_c"])
    wv_eT = np.ascontiguousarray(np.asarray(inputs["L0_attn_wv_e"]).T)
    wg0T = np.ascontiguousarray(np.asarray(inputs["L0_ffn_w_gate"]).T)
    wu0T = np.ascontiguousarray(np.asarray(inputs["L0_ffn_w_up"]).T)
    wd0T = np.ascontiguousarray(np.asarray(inputs["L0_ffn_w_down"]).T)
    wqT = np.ascontiguousarray(np.asarray(inputs["L1_attn_Wq"]).T)
    wkT = np.ascontiguousarray(np.asarray(inputs["L1_attn_Wk"]).T)
    wvT = np.ascontiguousarray(np.asarray(inputs["L1_attn_Wv"]).T)
    woT = np.ascontiguousarray(np.asarray(inputs["L1_attn_Wo"]).T)
    bq = np.ascontiguousarray(np.asarray(inputs["L1_attn_bq"]).reshape(1, C))
    bk = np.ascontiguousarray(np.asarray(inputs["L1_attn_bk"]).reshape(1, C))
    bv = np.ascontiguousarray(np.asarray(inputs["L1_attn_bv"]).reshape(1, C))
    bo = np.ascontiguousarray(np.asarray(inputs["L1_attn_bo"]).reshape(1, C))
    wgate1 = np.ascontiguousarray(inputs["L1_ffn_w_gate"])
    wup1 = np.ascontiguousarray(inputs["L1_ffn_w_up"])
    wdown1 = np.ascontiguousarray(inputs["L1_ffn_w_down"])
    x_all = np.ascontiguousarray(inputs["L0_attn_hidden_states"])

    nc1, _ = _get("p1", lambda: build_p1())
    base1 = dict(cattn_w=cattn_w, cattn_bp=cattn_bp, cproj_w=cproj_w,
                 cproj_b=cproj_b, wk_c=wk_c, wk_eT=wk_eT, wv_c=wv_c,
                 wv_eT=wv_eT, wg0T=wg0T, wu0T=wu0T, wd0T=wd0T,
                 wqT=wqT, wkT=wkT, wvT=wvT, woT=woT,
                 bq=bq, bk=bk, bv=bv, bo=bo,
                 ones_row=cst["ones_row"], ones_f32=cst["ones_f32"],
                 diagmask=cst["diagmask"], idn=cst["idn"],
                 madd=cst["madd"], ropecos=cst["ropecos"],
                 ropesin=cst["ropesin"])
    in_maps = [dict(base1, x0=np.ascontiguousarray(x_all[c]))
               for c in range(B)]
    r1 = run_bass_kernel_spmd(nc1, in_maps, core_ids=list(range(B)))
    x1 = [r1.results[c]["x1_out"] for c in range(B)]
    M1 = np.ascontiguousarray(
        np.stack([r1.results[c]["m1_out"][0, 0] for c in range(B)])[None, :])

    def run_stage(stage, extra, percore):
        nc, _ = _get(f"p{stage}", lambda: build_pn(stage))
        base = dict(ones_row=cst["ones_row"], ones_f32=cst["ones_f32"])
        if stage == 2:
            base.update(wgate=wgate1, wup=wup1, M1=M1)
        if stage == 5:
            base["wdown"] = wdown1
        base.update(extra)
        maps = [dict(base, **{k: v[c] for k, v in percore.items()})
                for c in range(B)]
        return run_bass_kernel_spmd(nc, maps, core_ids=list(range(B)))

    r2 = run_stage(2, {}, {"x1": x1})
    M2 = np.ascontiguousarray(
        np.stack([r2.results[c]["m_out"][0] for c in range(B)], axis=1))
    M2 = M2[[0, 1], :]
    gate_sp = [r2.results[c]["gate_sp"] for c in range(B)]
    up_sp = [r2.results[c]["up_sp"] for c in range(B)]
    r3 = run_stage(3, {"M2": M2}, {"gate_sp": gate_sp})
    M3 = np.ascontiguousarray(
        np.stack([r3.results[c]["m_out"][0, 0] for c in range(B)])[None, :])
    gq_sp = [r3.results[c]["gq_sp"] for c in range(B)]
    squ_sp = [r3.results[c]["squ_sp"] for c in range(B)]
    r4 = run_stage(4, {"M2": M2, "M3": M3},
                   {"up_sp": up_sp, "gq_sp": gq_sp, "squ_sp": squ_sp})
    M4 = np.ascontiguousarray(
        np.stack([r4.results[c]["m_out"][0, 0] for c in range(B)])[None, :])
    oraw_sp = [r4.results[c]["oraw_sp"] for c in range(B)]
    r5 = run_stage(5, {"M2": M2, "M3": M3, "M4": M4},
                   {"x1": x1, "oraw_sp": oraw_sp})
    y = np.stack([r5.results[c]["y_out"] for c in range(B)], axis=0)
    return y.astype(np.float32)


# version 11 (final)
# speedup vs baseline: 1.0256x; 1.0065x over previous
import numpy as np
import ml_dtypes
from contextlib import ExitStack

import concourse.bass as bass
import concourse.mybir as mybir
from concourse.bass_utils import run_bass_kernel_spmd

F32 = mybir.dt.float32
BF16 = mybir.dt.bfloat16
I32 = mybir.dt.int32
AF = mybir.ActivationFunctionType
ALU = mybir.AluOpType
AX = mybir.AxisListType

B, T, C = 8, 2048, 1024
H, HD, R = 16, 64, 32
FH = 3072
NT = T // 128
ND = C // 128
NH = FH // 128
S8 = 1.0 / 8.0
FSC = 1.0 / 127.0
QA = 32767.0
EPS = 1e-5
DEV = {}


class Rec:

    class Tok:
        __slots__ = ("incs",)

        def __init__(self):
            self.incs = []

        def then_inc(self, sem, n=1):
            self.incs.append((sem, n))
            return self

    def __init__(self):
        self.q = {k: [] for k in ("sync", "gpsimd", "vector", "scalar", "tensor")}

    def eng(self, name):
        rec = self

        class E:
            def __getattr__(self, op):
                def call(*a, **k):
                    tok = Rec.Tok()
                    rec.q[name].append((op, a, k, tok))
                    return tok
                return call
        return E()

    def replay(self, block):
        def mk(name):
            def run(e):
                for op, a, k, tok in self.q[name]:
                    inst = getattr(e, op)(*a, **k)
                    for sem, n in tok.incs:
                        inst.then_inc(sem, n)
            return run
        block.sync(mk("sync"))
        block.gpsimd(mk("gpsimd"))
        block.vector(mk("vector"))
        block.scalar(mk("scalar"))
        block.tensor(mk("tensor"))


class Sem:
    def __init__(self, E, nc, name):
        self.h = E(nc.semaphore(name))
        self.v = 0

    def bump(self, n=1):
        self.v += n
        return self.v


def tiled(x):
    return x.rearrange("(tt p) d -> p tt d", p=128)


def chunked(w):
    return w.rearrange("(c p) e -> p c e", p=128)


def host_constants():
    c = {}
    c["ones_row"] = np.ones((1, 128), dtype=ml_dtypes.bfloat16)
    c["ones_f32"] = np.ones((1, 128), dtype=np.float32)
    kk = np.arange(128)[:, None]
    qq = np.arange(128)[None, :]
    c["diagmask"] = (kk <= qq).astype(ml_dtypes.bfloat16)
    c["idn"] = np.eye(128, dtype=ml_dtypes.bfloat16)
    c["madd"] = np.where(kk > qq, -1e30, 0.0).astype(ml_dtypes.bfloat16)
    half = HD // 2
    theta = 1.0 / (10000.0 ** (np.arange(0, HD, 2, dtype=np.float64) / HD))
    pos = np.arange(1, T + 1, dtype=np.float64)
    ang = pos[:, None] * theta[None, :]
    ang = np.concatenate([ang, ang], axis=1)
    sgn = np.where(np.arange(HD) < half, -1.0, 1.0)
    c["ropecos"] = np.tile(np.cos(ang), (1, H)).astype(ml_dtypes.bfloat16)
    c["ropesin"] = np.tile(np.sin(ang) * sgn[None, :], (1, H)).astype(
        ml_dtypes.bfloat16)
    return c


class _VProxy:

    def __init__(self, cx, raw):
        self._cx, self._raw = cx, raw
        self._written = set()
        self._read = set()

    @staticmethod
    def _names(objs):
        out = set()
        for o in objs:
            t = getattr(o, "tensor", None)
            n = getattr(t, "name", None)
            if n is not None:
                out.add(n)
        return out

    def __getattr__(self, op):
        raw = self._raw

        def call(*a, **k):
            if op not in ("wait_ge", "engine_nop"):
                names = self._names(list(a) + list(k.values()))
                outs = []
                if a and getattr(a[0], "tensor", None) is not None:
                    outs.append(a[0])
                if "accum_out" in k and k["accum_out"] is not None:
                    outs.append(k["accum_out"])
                onames = self._names(outs)
                if (names & self._written) or (onames & self._read):
                    cx = self._cx
                    raw.engine_nop().then_inc(cx.sVE.h, 1)
                    cx.sVE.bump()
                    raw.wait_ge(cx.sVE.h, cx.sVE.v)
                    self._written.clear()
                    self._read.clear()
                self._written |= onames
                self._read |= names - onames
            return getattr(raw, op)(*a, **k)
        return call


class Cx:
    def __init__(self, nc, E):
        self.nc = nc
        self.rec = Rec()
        r = self.rec
        self.SY, self.GP, self.SC, self.TE = (
            r.eng("sync"), r.eng("gpsimd"), r.eng("scalar"), r.eng("tensor"))
        for nm in ("sLD", "sGD", "sXP", "sTE", "sVE", "sSC", "sST", "sRS"):
            setattr(self, nm, Sem(E, nc, nm))
        self.VE = _VProxy(self, r.eng("vector"))

    def w(self, eng, sem, val=None):
        eng.wait_ge(sem.h, sem.v if val is None else val)

    def gTE(self):
        self.w(self.TE, self.sSC)
        self.w(self.TE, self.sVE)

    def vfence(self):
        self.VE.engine_nop().then_inc(self.sVE.h, 1)
        self.sVE.bump()
        self.VE.wait_ge(self.sVE.h, self.sVE.v)


def emit_ln_tile(cx, P, pin, resid_ap, out_ap, resid_scale=None,
                 out_wait=None):
    VE, SC = cx.VE, cx.SC
    col, lnA, lnsq = P["col"], P["lnA"], P["lnsq"]
    cx.w(VE, cx.sTE)
    if resid_scale is None:
        VE.tensor_add(lnA[:, 0:C], pin, resid_ap).then_inc(cx.sVE.h, 1)
    else:
        VE.scalar_tensor_tensor(lnA[:, 0:C], pin, resid_scale, resid_ap,
                                ALU.mult, ALU.add).then_inc(cx.sVE.h, 1)
    cx.sVE.bump()
    pin_mark = cx.sVE.v
    VE.reduce_sum(col[:, 0:1], lnA[:, 0:C], axis=AX.X)
    VE.tensor_scalar_mul(col[:, 0:1], col[:, 0:1], 1.0 / C)
    VE.engine_nop().then_inc(cx.sVE.h, 1); cx.sVE.bump()
    cx.w(SC, cx.sVE)
    SC.activation(lnsq[:, 0:C], lnA[:, 0:C], AF.Square,
                  accum_out=col[:, 1:2]).then_inc(cx.sSC.h, 1); cx.sSC.bump()
    cx.w(VE, cx.sSC)
    VE.tensor_scalar(col[:, 2:3], col[:, 1:2], 1.0 / C, None, ALU.mult)
    VE.tensor_tensor(col[:, 3:4], col[:, 0:1], col[:, 0:1], ALU.mult)
    VE.tensor_sub(col[:, 2:3], col[:, 2:3], col[:, 3:4])
    VE.tensor_scalar_add(col[:, 2:3], col[:, 2:3], EPS)
    VE.reciprocal(col[:, 4:5], col[:, 2:3]).then_inc(cx.sVE.h, 1); cx.sVE.bump()
    cx.w(SC, cx.sVE)
    SC.activation(col[:, 5:6], col[:, 4:5], AF.Sqrt).then_inc(cx.sSC.h, 1)
    cx.sSC.bump()
    cx.w(VE, cx.sSC)
    VE.tensor_tensor(col[:, 6:7], col[:, 5:6], col[:, 5:6], ALU.mult)
    VE.tensor_tensor(col[:, 6:7], col[:, 6:7], col[:, 2:3], ALU.mult)
    VE.tensor_scalar(col[:, 6:7], col[:, 6:7], -0.5, 1.5, ALU.mult, ALU.add)
    VE.tensor_tensor(col[:, 5:6], col[:, 5:6], col[:, 6:7], ALU.mult)
    VE.tensor_scalar(col[:, 7:8], col[:, 0:1], -1.0, None, ALU.mult)
    VE.tensor_tensor(col[:, 7:8], col[:, 7:8], col[:, 5:6], ALU.mult)
    VE.engine_nop().then_inc(cx.sVE.h, 1); cx.sVE.bump()
    cx.w(SC, cx.sVE)
    for (sm, vv) in (out_wait or ()):
        cx.w(SC, sm, vv)
    SC.activation(out_ap, lnA[:, 0:C], AF.Identity,
                  bias=col[:, 7:8], scale=col[:, 5:6]).then_inc(cx.sSC.h, 1)
    cx.sSC.bump()
    return pin_mark


def emit_xT_tile(cx, P, src_ap, tt):
    buf = P["xbfr"][:, 0, :]
    cx.w(cx.VE, cx.sXP)
    cx.VE.tensor_copy(buf, src_ap).then_inc(cx.sVE.h, 1); cx.sVE.bump()
    cx.w(cx.SY, cx.sVE)
    cx.w(cx.SY, cx.sGD)
    for d in range(ND):
        cx.SY.dma_start_transpose(
            P["xT"][:, d, 128 * tt:128 * (tt + 1)],
            buf[:, 128 * d:128 * (d + 1)]).then_inc(cx.sXP.h, 16)
        cx.sXP.bump(16)


def emit_colmax_scalar(cx, P, bhbm, col_ap, out11):
    SY, VE = cx.SY, cx.VE
    cx.w(SY, cx.sVE)
    SY.dma_start(out=bhbm, in_=col_ap).then_inc(cx.sXP.h, 16)
    cx.sXP.bump(16)
    cx.w(SY, cx.sXP)
    SY.dma_start(out=P["grow"][0:1, :],
                 in_=bhbm.rearrange("a b -> b a")).then_inc(cx.sXP.h, 16)
    cx.sXP.bump(16)
    cx.w(VE, cx.sXP)
    VE.reduce_max(out11, P["grow"][0:1, :], axis=AX.X).then_inc(cx.sVE.h, 1)
    cx.sVE.bump()


def emit_bcast_cols(cx, P, src_row_ap, dst_cols_ap, n):
    cx.gTE()
    t = cx.TE.matmul(P["pA"][:, 0:n], P["onesf"][0:1, :], src_row_ap,
                     start=True, stop=True)
    t.then_inc(cx.sTE.h, 1); cx.sTE.bump()
    cx.w(cx.VE, cx.sTE)
    cx.VE.tensor_copy(dst_cols_ap, P["pA"][:, 0:n]).then_inc(cx.sVE.h, 1)
    cx.sVE.bump()


def alloc_perm(cx, E, with_xT=True):
    st, pt = cx.nc.sbuf_tensor, cx.nc.psum_tensor
    P = {}
    if with_xT:
        P["xT"] = E(st("xT", [128, ND, T], BF16))
    P["xbfr"] = E(st("xbfr", [128, 1, C], BF16))
    P["xres"] = E(st("xres", [128, 2, C], F32))
    P["xln"] = E(st("xln", [128, 2, C], F32))
    P["onesr"] = E(st("onesr", [1, 128], BF16))
    P["onesf"] = E(st("onesf", [1, 128], F32))
    P["onescol"] = E(st("onescol", [128, 1], BF16))
    P["col"] = E(st("col", [128, 16], F32))
    P["lnA"] = E(st("lnA", [128, C], F32))
    P["lnsq"] = E(st("lnsq", [128, C], BF16))
    P["runmax"] = E(st("runmax", [128, 4], F32))
    P["grow"] = E(st("grow", [1, 128], F32))
    P["srow"] = E(st("srow", [1, 64], F32))
    P["pA"] = E(pt("pA", [128, T], F32))
    P["pB"] = E(pt("pB", [128, T], F32))
    return P


def load_base_consts(cx, P, inp):
    SY, VE = cx.SY, cx.VE
    SY.dma_start(out=P["onesr"][:], in_=inp["ones_row"]).then_inc(cx.sLD.h, 16)
    cx.sLD.bump(16)
    SY.dma_start(out=P["onesf"][:], in_=inp["ones_f32"]).then_inc(cx.sLD.h, 16)
    cx.sLD.bump(16)
    cx.w(VE, cx.sLD)
    VE.memset(P["onescol"][:], 1.0)
    VE.memset(P["runmax"][:], 0.0)
    VE.engine_nop().then_inc(cx.sVE.h, 1); cx.sVE.bump()


def build_p1(stop_after=None):
    nc = bass.Bass("TRN2", target_bir_lowering=False)
    dt = nc.dram_tensor
    inp = {}

    def I(name, shape, dtyp=F32):
        inp[name] = dt(name, shape, dtyp, kind="ExternalInput").ap()
        return inp[name]

    x0 = I("x0", [T, C])
    cattn_w = I("cattn_w", [C, 3 * C])
    cattn_bp = I("cattn_bp", [128, 24])
    cproj_w = I("cproj_w", [C, C])
    cproj_b = I("cproj_b", [1, C])
    wk_c = I("wk_c", [HD, R]); wk_eT = I("wk_eT", [HD, R])
    wv_c = I("wv_c", [HD, R]); wv_eT = I("wv_eT", [HD, R])
    wg0T = I("wg0T", [C, FH], I32)
    wu0T = I("wu0T", [C, FH], I32)
    wd0T = I("wd0T", [FH, C], I32)
    wqT = I("wqT", [C, C]); wkT = I("wkT", [C, C])
    wvT = I("wvT", [C, C]); woT = I("woT", [C, C])
    bqr = I("bq", [1, C]); bkr = I("bk", [1, C])
    bvr = I("bv", [1, C]); bor = I("bo", [1, C])
    I("ones_row", [1, 128], BF16)
    I("ones_f32", [1, 128])
    diagmask = I("diagmask", [128, 128], BF16)
    idn_h = I("idn", [128, 128], BF16)
    madd_h = I("madd", [128, 128], BF16)
    ropecos = I("ropecos", [T, C], BF16)
    ropesin = I("ropesin", [T, C], BF16)

    x1_out = dt("x1_out", [T, C], F32, kind="ExternalOutput").ap()
    m1_out = dt("m1_out", [1, 1], F32, kind="ExternalOutput").ap()
    xspillA = dt("xspillA", [T, C], F32).ap()
    xspillB = dt("xspillB", [T, C], F32).ap()
    bhbm = dt("bhbm", [128, 1], F32).ap()

    with ExitStack() as perm:
        E = perm.enter_context
        st = nc.sbuf_tensor
        cx = Cx(nc, E)
        SY, GP, VE, SC, TE = cx.SY, cx.GP, cx.VE, cx.SC, cx.TE
        P = alloc_perm(cx, E)
        pA, pB = P["pA"], P["pB"]
        onesr, onesf, onescol = P["onesr"], P["onesf"], P["onescol"]
        srow = P["srow"]
        dmask = E(st("dmask", [128, 128], BF16))
        idn = E(st("idn_s", [128, 128], BF16))
        madd = E(st("madd_s", [128, 128], BF16))
        negsB = E(st("negsB", [128, H], F32))

        SY.dma_start(out=dmask[:], in_=diagmask).then_inc(cx.sLD.h, 16)
        cx.sLD.bump(16)
        SY.dma_start(out=idn[:], in_=idn_h).then_inc(cx.sLD.h, 16)
        cx.sLD.bump(16)
        SY.dma_start(out=madd[:], in_=madd_h).then_inc(cx.sLD.h, 16)
        cx.sLD.bump(16)
        load_base_consts(cx, P, inp)

        for tt in range(NT):
            bb = tt % 2
            cx.w(SY, cx.sVE)
            SY.dma_start(out=P["xres"][:, bb, :],
                         in_=tiled(x0)[:, tt, :]).then_inc(cx.sLD.h, 16)
            cx.sLD.bump(16)
            cx.w(VE, cx.sLD)
            emit_xT_tile(cx, P, P["xres"][:, bb, :], tt)
        xT0_xp = cx.sXP.v

        with ExitStack() as ph:
            Ep = ph.enter_context
            WAr = Ep(st("WAr", [128, 2, 3, ND, 128], BF16))
            WP = Ep(st("WP", [128, ND, C], BF16))
            bias24 = Ep(st("bias24", [128, 24], F32))
            w64 = Ep(st("w64", [128, 4 * R], BF16))
            browp = Ep(st("browp", [1, C], BF16))
            qT2 = Ep(st("qT2", [128, T], BF16))
            kT2 = Ep(st("kT2", [128, T], BF16))
            vT2 = Ep(st("vT2", [128, T], BF16))
            qpT = Ep(st("qpT", [128, 4, T], BF16))
            klT = Ep(st("klT", [128, 4, T], BF16))
            atT = Ep(st("atT", [128, 4, T], BF16))
            atst = Ep(st("atst", [32, T], BF16))
            vones = Ep(st("vones", [128, H, NT, R + 1], BF16))
            sqs = Ep(st("sqs", [128, T], BF16))
            probsT = Ep(st("probsT", [128, T], BF16))
            Weff = Ep(st("Weff", [128, 4, C], BF16))
            reciprow = Ep(st("reciprow", [1, T], F32))
            recipbf = Ep(st("recipbf", [1, T], BF16))

            GP.dma_start(out=WP[:], in_=chunked(cproj_w)).then_inc(cx.sGD.h, 16)
            cx.sGD.bump(16)
            for half64 in range(2):
                for i, wsrc in enumerate((wk_c, wk_eT, wv_c, wv_eT)):
                    GP.dma_start(out=w64[64 * half64:64 * (half64 + 1),
                                         R * i:R * (i + 1)],
                                 in_=wsrc).then_inc(cx.sGD.h, 16)
                    cx.sGD.bump(16)
            GP.dma_start(out=browp[:], in_=cproj_b).then_inc(cx.sGD.h, 16)
            cx.sGD.bump(16)
            SY.dma_start(out=bias24[:], in_=cattn_bp).then_inc(cx.sLD.h, 16)
            cx.sLD.bump(16)
            a_ld, a_gd = cx.sLD.v, cx.sGD.v
            VE.memset(vones[:, :, :, R:R + 1], 1.0)
            VE.engine_nop().then_inc(cx.sVE.h, 1); cx.sVE.bump()

            te_qkv = {}
            for hp in range(DEV.get("n_hp", 8)):
                heads = (2 * hp, 2 * hp + 1)
                bbw = hp % 2
                if hp >= 2:
                    GP.wait_ge(cx.sTE.h, te_qkv[hp - 2])
                for which, ech in enumerate((hp, 8 + hp, 16 + hp)):
                    GP.dma_start(
                        out=WAr[:, bbw, which, :, :],
                        in_=cattn_w[:, 128 * ech:128 * (ech + 1)].rearrange(
                            "(c p) e -> p c e", p=128)).then_inc(cx.sGD.h, 16)
                    cx.sGD.bump(16)
                war_gd = cx.sGD.v
                for which, (dst, ech) in enumerate(
                        [(qT2, hp), (kT2, 8 + hp), (vT2, 16 + hp)]):
                    p = pA if which % 2 == 0 else pB
                    cx.gTE()
                    cx.w(TE, cx.sGD, war_gd)
                    if hp == 0 and which == 0:
                        cx.w(TE, cx.sXP, xT0_xp)
                        cx.w(TE, cx.sLD, a_ld)
                    for b in range(4):
                        for d in range(ND):
                            t = TE.matmul(
                                p[:, 512 * b:512 * (b + 1)],
                                WAr[:, bbw, which, d, :],
                                P["xT"][:, d, 512 * b:512 * (b + 1)],
                                start=(d == 0), stop=(d == ND - 1))
                    t.then_inc(cx.sTE.h, 1); cx.sTE.bump()
                    if which == 2:
                        te_qkv[hp] = cx.sTE.v
                    cx.w(SC, cx.sTE)
                    SC.activation(dst[:], p[:], AF.Identity,
                                  bias=bias24[:, ech:ech + 1]).then_inc(
                        cx.sSC.h, 1); cx.sSC.bump()

                for hi, h in enumerate(heads):
                    if "qk" in DEV.get("skip", ()):
                        break
                    rows = 64 * (h % 2)
                    g, cc = h % 4, h // 4
                    gb = 32 * g
                    for which, (srcT, wof, dstT) in enumerate(
                            [(qT2, R, qpT), (kT2, 0, klT)]):
                        p = pA if which == 0 else pB
                        cx.gTE()
                        for b in range(4):
                            t = TE.matmul(
                                p[gb:gb + 32, 512 * b:512 * (b + 1)],
                                w64[rows:rows + 64, wof:wof + R],
                                srcT[rows:rows + 64, 512 * b:512 * (b + 1)],
                                start=True, stop=True,
                                tile_position=(rows, gb))
                        t.then_inc(cx.sTE.h, 1); cx.sTE.bump()
                        cx.w(SC, cx.sTE)
                        SC.activation(dstT[gb:gb + 32, cc, :],
                                      p[gb:gb + 32, :],
                                      AF.Identity).then_inc(cx.sSC.h, 1)
                        cx.sSC.bump()
                        SC.activation(sqs[gb:gb + 32, :], p[gb:gb + 32, :],
                                      AF.Square).then_inc(cx.sSC.h, 1)
                        cx.sSC.bump()
                        cx.gTE()
                        for b in range(4):
                            t = TE.matmul(
                                p[0:1, 512 * b:512 * (b + 1)],
                                onescol[gb:gb + 32, 0:1],
                                sqs[gb:gb + 32, 512 * b:512 * (b + 1)],
                                start=True, stop=True,
                                tile_position=(gb, 0))
                        t.then_inc(cx.sTE.h, 1); cx.sTE.bump()
                        cx.w(VE, cx.sTE)
                        idx = 2 * hi + which
                        VE.reduce_max(srow[0:1, idx:idx + 1], p[0:1, 0:T],
                                      axis=AX.X).then_inc(cx.sVE.h, 1)
                        cx.sVE.bump()

                cx.w(VE, cx.sVE)
                VE.tensor_mul(
                    srow[0:1, 4:6],
                    srow[0:1, 0:4].rearrange("a (h k) -> a h k", k=2)[:, :, 0],
                    srow[0:1, 0:4].rearrange("a (h k) -> a h k", k=2)[:, :, 1])
                VE.engine_nop().then_inc(cx.sVE.h, 1); cx.sVE.bump()
                cx.w(SC, cx.sVE)
                SC.activation(srow[0:1, 6:8], srow[0:1, 4:6],
                              AF.Sqrt).then_inc(cx.sSC.h, 1); cx.sSC.bump()
                cx.w(VE, cx.sSC)
                VE.tensor_scalar_mul(srow[0:1, 8:10], srow[0:1, 6:8], -S8)
                VE.engine_nop().then_inc(cx.sVE.h, 1); cx.sVE.bump()
                emit_bcast_cols(cx, P, srow[0:1, 8:10],
                                negsB[:, 2 * hp:2 * hp + 2], 2)

                for hi, h in enumerate(heads):
                    if "vlat" in DEV.get("skip", ()):
                        break
                    rows = 64 * (h % 2)
                    cx.gTE()
                    for tt in range(NT):
                        t = TE.matmul(
                            pA[:, 32 * tt:32 * (tt + 1)],
                            vT2[rows:rows + 64, 128 * tt:128 * (tt + 1)],
                            w64[rows:rows + 64, 2 * R:3 * R],
                            start=True, stop=True,
                            tile_position=(rows, 0))
                    t.then_inc(cx.sTE.h, 1); cx.sTE.bump()
                    cx.w(VE, cx.sTE)
                    VE.tensor_copy(
                        vones[:, h, :, 0:R],
                        pA[:, 0:512].rearrange("p (tt r) -> p tt r", r=R),
                    ).then_inc(cx.sVE.h, 1); cx.sVE.bump()

                for hi, h in enumerate(heads):
                    if "scores" in DEV.get("skip", ()):
                        break
                    g, cc = h % 4, h // 4
                    gb = 32 * g
                    for kt in range(NT):
                        q0 = 128 * kt
                        b0 = q0 // 512
                        cx.gTE()
                        segv = {}
                        for b in range(b0, 4):
                            lo, hi2 = max(512 * b, q0), 512 * (b + 1)
                            t = TE.matmul(pA[:, lo:hi2],
                                          klT[gb:gb + 32, cc, q0:q0 + 128],
                                          qpT[gb:gb + 32, cc, lo:hi2],
                                          start=True, stop=True,
                                          tile_position=(gb, 0))
                            if b == b0:
                                t = TE.matmul(pA[:, q0:q0 + 128], idn[:, :],
                                              madd[:, :], start=False,
                                              stop=True,
                                              skip_group_check=True)
                            t.then_inc(cx.sTE.h, 1); cx.sTE.bump()
                            segv[b] = cx.sTE.v
                        expv = {}
                        for b in range(b0, 4):
                            lo, hi2 = max(512 * b, q0), 512 * (b + 1)
                            SC.wait_ge(cx.sTE.h, segv[b])
                            SC.activation(probsT[:, lo:hi2], pA[:, lo:hi2],
                                          AF.Exp, bias=negsB[:, h:h + 1],
                                          scale=S8).then_inc(cx.sSC.h, 1)
                            cx.sSC.bump()
                            expv[b] = cx.sSC.v
                        for b in range(b0, 4):
                            lo, hi2 = max(512 * b, q0), 512 * (b + 1)
                            TE.wait_ge(cx.sSC.h, expv[b])
                            t = TE.matmul(
                                pB[0:R + 1, lo:hi2], vones[:, h, kt, :],
                                probsT[:, lo:hi2],
                                start=(kt == 0),
                                stop=(kt == min(4 * b + 3, NT - 1)),
                                skip_group_check=True)
                            t.then_inc(cx.sTE.h, 1); cx.sTE.bump()
                    cx.w(VE, cx.sTE)
                    VE.reciprocal(reciprow[0:1, :], pB[R:R + 1, :])
                    cx.vfence()
                    VE.tensor_copy(recipbf[0:1, :],
                                   reciprow[0:1, :]).then_inc(cx.sVE.h, 1)
                    cx.sVE.bump()
                    cx.gTE()
                    for b in range(4):
                        t = TE.matmul(pA[0:R, 512 * b:512 * (b + 1)],
                                      onesr[:, 0:R],
                                      recipbf[0:1, 512 * b:512 * (b + 1)],
                                      start=True, stop=True)
                    t.then_inc(cx.sTE.h, 1); cx.sTE.bump()
                    cx.w(SC, cx.sTE)
                    cx.w(SC, cx.sGD)
                    cx.w(SC, cx.sVE)
                    SC.activation(atst[:, :], pB[0:R, :],
                                  AF.Identity).then_inc(cx.sSC.h, 1)
                    cx.sSC.bump()
                    cx.w(VE, cx.sSC)
                    VE.tensor_mul(atst[:, :], atst[:, :],
                                  pA[0:R, :]).then_inc(cx.sVE.h, 1)
                    cx.sVE.bump()
                    cx.w(GP, cx.sVE)
                    GP.dma_start(out=atT[gb:gb + 32, cc, :],
                                 in_=atst[:, :]).then_inc(cx.sGD.h, 16)
                    cx.sGD.bump(16)

            for h in range(0 if "weff" in DEV.get("skip", ()) else H):
                g, cc = h % 4, h // 4
                gb = 32 * g
                rc, ro = (h * HD) // 128, (h * HD) % 128
                cx.gTE()
                for b in range(2):
                    t = TE.matmul(pA[gb:gb + 32, 512 * b:512 * (b + 1)],
                                  w64[ro:ro + 64, 3 * R:4 * R],
                                  WP[ro:ro + 64, rc, 512 * b:512 * (b + 1)],
                                  start=True, stop=True,
                                  tile_position=(ro, gb))
                t.then_inc(cx.sTE.h, 1); cx.sTE.bump()
                cx.w(VE, cx.sTE)
                VE.tensor_copy(Weff[gb:gb + 32, cc, :],
                               pA[gb:gb + 32, 0:C]).then_inc(cx.sVE.h, 1)
                cx.sVE.bump()

            for tt in range(NT):
                bb = tt % 2
                cx.w(SY, cx.sVE)
                SY.dma_start(out=P["xres"][:, bb, :],
                             in_=tiled(x0)[:, tt, :]).then_inc(cx.sRS.h, 16)
                cx.sRS.bump(16)
                cx.gTE()
                cx.w(TE, cx.sGD)
                first = True
                if "proj" not in DEV.get("skip", ()):
                    for cc in range(4):
                        for b in range(2):
                            TE.matmul(pB[:, 512 * b:512 * (b + 1)],
                                      atT[:, cc, 128 * tt:128 * (tt + 1)],
                                      Weff[:, cc, 512 * b:512 * (b + 1)],
                                      start=first, stop=False,
                                      skip_group_check=True)
                        first = False
                cx.w(TE, cx.sRS)
                for b in range(2):
                    t = TE.matmul(pB[:, 512 * b:512 * (b + 1)], onesr[:, :],
                                  browp[0:1, 512 * b:512 * (b + 1)],
                                  start=(first and b == 0), stop=True,
                                  skip_group_check=True)
                t.then_inc(cx.sTE.h, 1); cx.sTE.bump()
                emit_ln_tile(cx, P, pB[:, 0:C], P["xres"][:, bb, :],
                             P["xln"][:, bb, :],
                             out_wait=[(cx.sXP, cx.sXP.v)])
                cx.w(SY, cx.sSC)
                SY.dma_start(out=tiled(xspillA)[:, tt, :],
                             in_=P["xln"][:, bb, :]).then_inc(cx.sXP.h, 16)
                cx.sXP.bump(16)
                if stop_after != "l0attn":
                    emit_xT_tile(cx, P, P["xln"][:, bb, :], tt)

        if stop_after == "l0attn":
            _emit_p1_tail(cx, P, xspillA, x1_out, m1_out, bhbm)
            blk = perm.enter_context(nc.Block())
            cx.rec.replay(blk)
            return nc, inp

        with ExitStack() as ph:
            Ep = ph.enter_context
            g2 = Ep(st("g2", [128, NH, T], BF16))
            with ExitStack() as ph2:
                Ep2 = ph2.enter_context
                wring = Ep2(st("wring", [128, 2, 2, ND, 128], BF16))
                silu = Ep2(st("silu", [128, T], F32))
                sW = [Sem(Ep2, nc, "sW0"), Sem(Ep2, nc, "sW1")]
                te_after = {}
                for hh in range(NH):
                    bb = hh % 2
                    if hh >= 2:
                        GP.wait_ge(cx.sTE.h, te_after[hh - 2])
                    GP.dma_start(
                        out=wring[:, bb, 0, :, :],
                        in_=inp["wg0T"][:, 128 * hh:128 * (hh + 1)].rearrange(
                            "(c p) h -> p c h", p=128)).then_inc(
                        sW[bb].h, 16); sW[bb].bump(16)
                    GP.dma_start(
                        out=wring[:, bb, 1, :, :],
                        in_=inp["wu0T"][:, 128 * hh:128 * (hh + 1)].rearrange(
                            "(c p) h -> p c h", p=128)).then_inc(
                        sW[bb].h, 16); sW[bb].bump(16)
                    cx.gTE()
                    cx.w(TE, sW[bb])
                    if hh == 0:
                        cx.w(TE, cx.sXP, cx.sXP.v)
                    for w_i, p in ((0, pA), (1, pB)):
                        for b in range(4):
                            for d in range(ND):
                                t = TE.matmul(
                                    p[:, 512 * b:512 * (b + 1)],
                                    wring[:, bb, w_i, d, :],
                                    P["xT"][:, d, 512 * b:512 * (b + 1)],
                                    start=(d == 0), stop=(d == ND - 1))
                        t.then_inc(cx.sTE.h, 1); cx.sTE.bump()
                    te_after[hh] = cx.sTE.v
                    cx.w(SC, cx.sTE)
                    SC.activation(silu[:, :], pA[:, :], AF.Silu,
                                  scale=FSC).then_inc(cx.sSC.h, 1)
                    cx.sSC.bump()
                    cx.w(VE, cx.sSC)
                    VE.tensor_mul(g2[:, hh, :], silu[:, :],
                                  pB[:, :]).then_inc(cx.sVE.h, 1); cx.sVE.bump()

            wdsb = Ep(st("wdsb", [128, NH, C], BF16))
            cx.w(GP, cx.sTE); cx.w(GP, cx.sSC); cx.w(GP, cx.sVE)
            GP.dma_start(out=wdsb[:], in_=chunked(inp["wd0T"])).then_inc(
                cx.sGD.h, 16); cx.sGD.bump(16)
            wd_gd = cx.sGD.v
            for tt in range(NT):
                bb = tt % 2
                cx.w(SY, cx.sVE)
                cx.w(SY, cx.sXP)
                SY.dma_start(out=P["xres"][:, bb, :],
                             in_=tiled(xspillA)[:, tt, :]).then_inc(
                    cx.sRS.h, 16); cx.sRS.bump(16)
                cx.gTE()
                cx.w(TE, cx.sRS)
                if tt == 0:
                    cx.w(TE, cx.sGD, wd_gd)
                for hh in range(NH):
                    for b in range(2):
                        t = TE.matmul(pB[:, 512 * b:512 * (b + 1)],
                                      g2[:, hh, 128 * tt:128 * (tt + 1)],
                                      wdsb[:, hh, 512 * b:512 * (b + 1)],
                                      start=(hh == 0), stop=(hh == NH - 1),
                                      skip_group_check=True)
                t.then_inc(cx.sTE.h, 1); cx.sTE.bump()
                emit_ln_tile(cx, P, pB[:, 0:C], P["xres"][:, bb, :],
                             P["xln"][:, bb, :], resid_scale=FSC * FSC,
                             out_wait=[(cx.sXP, cx.sXP.v)])
                cx.w(SY, cx.sSC)
                SY.dma_start(out=tiled(xspillB)[:, tt, :],
                             in_=P["xln"][:, bb, :]).then_inc(cx.sXP.h, 16)
                cx.sXP.bump(16)
                if stop_after != "l0ffn":
                    emit_xT_tile(cx, P, P["xln"][:, bb, :], tt)

        if stop_after == "l0ffn":
            _emit_p1_tail(cx, P, xspillB, x1_out, m1_out, bhbm)
            blk = perm.enter_context(nc.Block())
            cx.rec.replay(blk)
            return nc, inp

        with ExitStack() as ph:
            Ep = ph.enter_context
            Wbuf = Ep(st("Wbuf", [128, ND, C], BF16))
            qroT = Ep(st("qroT", [128, 4, T], BF16))
            kroT = Ep(st("kroT", [128, 4, T], BF16))
            von1 = Ep(st("von1", [128, NT, 8, HD + 1], BF16))
            avT = Ep(st("avT", [128, ND, T], BF16))
            avst = Ep(st("avst", [64, T], BF16))
            probsT = Ep(st("probsT1", [128, T], BF16))
            qtile = Ep(st("qtile", [128, 512], BF16))
            rtmp = Ep(st("rtmp", [128, 512], BF16))
            rcos = Ep(st("rcos", [128, 1, C], BF16))
            rsin = Ep(st("rsin", [128, 1, C], BF16))
            nsq = Ep(st("nsq", [128, 512], F32))
            nstat = Ep(st("nstat", [128, 16], F32))
            brow4 = Ep(st("brow4", [1, 4, C], BF16))
            growL = Ep(st("growL", [1, 2048], F32))
            reciprow = Ep(st("reciprow1", [1, T], F32))
            recipbf = Ep(st("recipbf1", [1, T], BF16))
            ghbm = nc.dram_tensor("ghbm16", [128, 16], F32).ap()

            cx.w(GP, cx.sTE); cx.w(GP, cx.sSC); cx.w(GP, cx.sVE)
            for i, bsrc in enumerate((bqr, bkr, bvr, bor)):
                GP.dma_start(out=brow4[:, i, :], in_=bsrc).then_inc(
                    cx.sGD.h, 16); cx.sGD.bump(16)
            brow_gd = cx.sGD.v

            for half in range(2):
                hh0 = 8 * half
                cx.w(VE, cx.sSC)
                VE.memset(nstat[:, :], 0.0)
                VE.memset(von1[:, :, :, HD:HD + 1], 1.0)
                VE.engine_nop().then_inc(cx.sVE.h, 1); cx.sVE.bump()
                for wi, (wsrc, brix) in enumerate(
                        [(wqT, 0), (wkT, 1), (wvT, 2)]):
                    cx.w(GP, cx.sTE)
                    GP.dma_start(out=Wbuf[:, :, :],
                                 in_=chunked(wsrc)).then_inc(cx.sGD.h, 16)
                    cx.sGD.bump(16)
                    w_gd = cx.sGD.v
                    for tt in range(NT):
                        rb = 0
                        if wi != 2:
                            cx.w(SY, cx.sVE)
                            SY.dma_start(
                                out=rcos[:, rb, :],
                                in_=tiled(ropecos)[:, tt, :]).then_inc(
                                cx.sRS.h, 16); cx.sRS.bump(16)
                            SY.dma_start(
                                out=rsin[:, rb, :],
                                in_=tiled(ropesin)[:, tt, :]).then_inc(
                                cx.sRS.h, 16); cx.sRS.bump(16)
                        rs_here = cx.sRS.v
                        cx.gTE()
                        if tt == 0:
                            cx.w(TE, cx.sGD, w_gd)
                            if wi == 0 and half == 0:
                                cx.w(TE, cx.sXP, cx.sXP.v)
                        for d in range(ND):
                            TE.matmul(
                                pA[:, 0:512],
                                P["xT"][:, d, 128 * tt:128 * (tt + 1)],
                                Wbuf[:, d, 512 * half:512 * (half + 1)],
                                start=(d == 0), stop=False,
                                skip_group_check=True)
                        t = TE.matmul(pA[:, 0:512], onesr[:, :],
                                      brow4[0:1, brix,
                                            512 * half:512 * (half + 1)],
                                      start=False, stop=True,
                                      skip_group_check=True)
                        t.then_inc(cx.sTE.h, 1); cx.sTE.bump()
                        if wi == 2:
                            cx.w(SC, cx.sTE)
                            SC.activation(
                                von1[:, tt, :, 0:HD],
                                pA[:, 0:512].rearrange(
                                    "p (h d) -> p h d", d=HD),
                                AF.Identity).then_inc(
                                cx.sSC.h, 1); cx.sSC.bump()
                            continue
                        cx.w(SC, cx.sTE)
                        cx.w(SC, cx.sXP)
                        SC.activation(qtile[:, :], pA[:, 0:512],
                                      AF.Identity).then_inc(cx.sSC.h, 1)
                        cx.sSC.bump()
                        qh = qtile[:, :].rearrange("p (h d) -> p h d", d=HD)
                        th = rtmp[:, :].rearrange("p (h d) -> p h d", d=HD)
                        cs = rcos[:, rb, 512 * half:512 * (half + 1)]
                        sn = rsin[:, rb, 512 * half:512 * (half + 1)].rearrange(
                            "p (h d) -> p h d", d=HD)
                        cx.w(VE, cx.sSC)
                        cx.w(VE, cx.sRS, rs_here)
                        VE.tensor_mul(th[:, :, 0:32], qh[:, :, 32:64],
                                      sn[:, :, 0:32])
                        VE.tensor_mul(th[:, :, 32:64], qh[:, :, 0:32],
                                      sn[:, :, 32:64])
                        VE.tensor_mul(qtile[:, :], qtile[:, :], cs)
                        VE.tensor_add(qtile[:, :], qtile[:, :], rtmp[:, :])
                        VE.tensor_mul(nsq[:, :], qtile[:, :], qtile[:, :])
                        VE.reduce_sum(
                            P["col"][:, 8:16],
                            nsq[:, :].rearrange("p (h d) -> p h d", d=HD),
                            axis=AX.X)
                        VE.tensor_max(nstat[:, 8 * wi:8 * (wi + 1)],
                                      nstat[:, 8 * wi:8 * (wi + 1)],
                                      P["col"][:, 8:16])
                        VE.engine_nop().then_inc(cx.sVE.h, 1); cx.sVE.bump()
                        dstT = qroT if wi == 0 else kroT
                        cx.w(SY, cx.sVE)
                        cx.w(SY, cx.sGD)
                        for d4 in range(4):
                            SY.dma_start_transpose(
                                dstT[:, d4, 128 * tt:128 * (tt + 1)],
                                qtile[:, 128 * d4:128 * (d4 + 1)]).then_inc(
                                cx.sXP.h, 16)
                            cx.sXP.bump(16)

                cx.w(SY, cx.sVE)
                SY.dma_start(out=ghbm, in_=nstat[:, :]).then_inc(cx.sXP.h, 16)
                cx.sXP.bump(16)
                cx.w(SY, cx.sXP)
                SY.dma_start(out=growL[0:1, :],
                             in_=ghbm.rearrange("p s -> (p s)").unsqueeze(0)).then_inc(
                    cx.sXP.h, 16); cx.sXP.bump(16)
                cx.w(VE, cx.sXP)
                VE.reduce_max(srow[0:1, 0:16],
                              growL[0:1, :].rearrange("a (p s) -> a s p", s=16),
                              axis=AX.X)
                VE.engine_nop().then_inc(cx.sVE.h, 1); cx.sVE.bump()
                cx.w(VE, cx.sVE)
                VE.tensor_mul(srow[0:1, 16:24], srow[0:1, 0:8],
                              srow[0:1, 8:16])
                VE.engine_nop().then_inc(cx.sVE.h, 1); cx.sVE.bump()
                cx.w(SC, cx.sVE)
                SC.activation(srow[0:1, 24:32], srow[0:1, 16:24],
                              AF.Sqrt).then_inc(cx.sSC.h, 1); cx.sSC.bump()
                cx.w(VE, cx.sSC)
                VE.tensor_scalar_mul(srow[0:1, 32:40], srow[0:1, 24:32], -S8)
                VE.engine_nop().then_inc(cx.sVE.h, 1); cx.sVE.bump()
                emit_bcast_cols(cx, P, srow[0:1, 32:40],
                                negsB[:, hh0:hh0 + 8], 8)

                for hl in range(8):
                    h = hh0 + hl
                    ch, rows = hl // 2, 64 * (hl % 2)
                    for kt in range(NT):
                        q0 = 128 * kt
                        b0 = q0 // 512
                        cx.gTE()
                        segv = {}
                        for b in range(b0, 4):
                            lo, hi2 = max(512 * b, q0), 512 * (b + 1)
                            t = TE.matmul(pA[:, lo:hi2],
                                          kroT[rows:rows + 64, ch, q0:q0 + 128],
                                          qroT[rows:rows + 64, ch, lo:hi2],
                                          start=True, stop=True,
                                          tile_position=(rows, 0))
                            if b == b0:
                                t = TE.matmul(pA[:, q0:q0 + 128], idn[:, :],
                                              madd[:, :], start=False,
                                              stop=True,
                                              skip_group_check=True)
                            t.then_inc(cx.sTE.h, 1); cx.sTE.bump()
                            segv[b] = cx.sTE.v
                        expv = {}
                        for b in range(b0, 4):
                            lo, hi2 = max(512 * b, q0), 512 * (b + 1)
                            SC.wait_ge(cx.sTE.h, segv[b])
                            SC.activation(probsT[:, lo:hi2], pA[:, lo:hi2],
                                          AF.Exp, bias=negsB[:, h:h + 1],
                                          scale=S8).then_inc(cx.sSC.h, 1)
                            cx.sSC.bump()
                            expv[b] = cx.sSC.v
                        for b in range(b0, 4):
                            lo, hi2 = max(512 * b, q0), 512 * (b + 1)
                            TE.wait_ge(cx.sSC.h, expv[b])
                            t = TE.matmul(
                                pB[0:HD + 1, lo:hi2],
                                von1[:, kt, hl, :],
                                probsT[:, lo:hi2],
                                start=(kt == 0),
                                stop=(kt == min(4 * b + 3, NT - 1)),
                                skip_group_check=True)
                            t.then_inc(cx.sTE.h, 1); cx.sTE.bump()
                    cx.w(VE, cx.sTE)
                    VE.reciprocal(reciprow[0:1, :], pB[HD:HD + 1, :])
                    cx.vfence()
                    VE.tensor_copy(recipbf[0:1, :],
                                   reciprow[0:1, :]).then_inc(cx.sVE.h, 1)
                    cx.sVE.bump()
                    cx.gTE()
                    for b in range(4):
                        t = TE.matmul(pA[0:HD, 512 * b:512 * (b + 1)],
                                      onesr[:, 0:HD],
                                      recipbf[0:1, 512 * b:512 * (b + 1)],
                                      start=True, stop=True)
                    t.then_inc(cx.sTE.h, 1); cx.sTE.bump()
                    cx.w(SC, cx.sTE)
                    cx.w(SC, cx.sGD)
                    cx.w(SC, cx.sVE)
                    SC.activation(avst[:, :], pB[0:HD, :],
                                  AF.Identity).then_inc(cx.sSC.h, 1)
                    cx.sSC.bump()
                    cx.w(VE, cx.sSC)
                    VE.tensor_mul(avst[:, :], avst[:, :],
                                  pA[0:HD, :]).then_inc(cx.sVE.h, 1)
                    cx.sVE.bump()
                    cx.w(GP, cx.sVE)
                    GP.dma_start(out=avT[rows:rows + 64, ch + 4 * half, :],
                                 in_=avst[:, :]).then_inc(cx.sGD.h, 16)
                    cx.sGD.bump(16)

            cx.w(GP, cx.sTE)
            GP.dma_start(out=Wbuf[:, :, :],
                         in_=chunked(woT)).then_inc(cx.sGD.h, 16)
            cx.sGD.bump(16)
            wo_gd = cx.sGD.v
            for tt in range(NT):
                bb = tt % 2
                cx.w(SY, cx.sVE)
                cx.w(SY, cx.sXP)
                SY.dma_start(out=P["xres"][:, bb, :],
                             in_=tiled(xspillB)[:, tt, :]).then_inc(
                    cx.sRS.h, 16); cx.sRS.bump(16)
                cx.gTE()
                if tt == 0:
                    cx.w(TE, cx.sGD, wo_gd)
                for b in range(2):
                    for dd in range(ND):
                        ch, rows = dd % 4, 64 * (dd // 4)
                        TE.matmul(pB[:, 512 * b:512 * (b + 1)],
                                  avT[:, dd, 128 * tt:128 * (tt + 1)],
                                  Wbuf[:, dd, 512 * b:512 * (b + 1)],
                                  start=(dd == 0), stop=False,
                                  skip_group_check=True)
                cx.w(TE, cx.sRS)
                for b in range(2):
                    t = TE.matmul(pB[:, 512 * b:512 * (b + 1)], onesr[:, :],
                                  brow4[0:1, 3, 512 * b:512 * (b + 1)],
                                  start=False, stop=True,
                                  skip_group_check=True)
                t.then_inc(cx.sTE.h, 1); cx.sTE.bump()
                emit_ln_tile(cx, P, pB[:, 0:C], P["xres"][:, bb, :],
                             P["xln"][:, bb, :],
                             out_wait=[(cx.sXP, cx.sXP.v), (cx.sST, cx.sST.v)])
                cx.w(SY, cx.sSC)
                SY.dma_start(out=tiled(x1_out)[:, tt, :],
                             in_=P["xln"][:, bb, :]).then_inc(cx.sST.h, 16)
                cx.sST.bump(16)
                cx.w(VE, cx.sSC)
                VE.reduce_max(P["col"][:, 0:1], P["xln"][:, bb, :], axis=AX.X,
                              apply_absolute_value=True)
                VE.tensor_max(P["runmax"][:, 0:1], P["runmax"][:, 0:1],
                              P["col"][:, 0:1]).then_inc(cx.sVE.h, 1)
                cx.sVE.bump()

        emit_colmax_scalar(cx, P, bhbm, P["runmax"][:, 0:1], srow[0:1, 0:1])
        cx.w(SY, cx.sVE)
        SY.dma_start(out=m1_out, in_=srow[0:1, 0:1]).then_inc(cx.sST.h, 16)
        cx.sST.bump(16)
        cx.w(GP, cx.sST)
        blk = perm.enter_context(nc.Block())
        cx.rec.replay(blk)
        return nc, inp


def _emit_p1_tail(cx, P, xsp, x1_out, m1_out, bhbm):
    SY, VE, GP = cx.SY, cx.VE, cx.GP
    for tt in range(NT):
        bb = tt % 2
        cx.w(SY, cx.sVE)
        cx.w(SY, cx.sST)
        cx.w(SY, cx.sXP)
        SY.dma_start(out=P["xres"][:, bb, :],
                     in_=tiled(xsp)[:, tt, :]).then_inc(cx.sRS.h, 16)
        cx.sRS.bump(16)
        cx.w(SY, cx.sRS)
        SY.dma_start(out=tiled(x1_out)[:, tt, :],
                     in_=P["xres"][:, bb, :]).then_inc(cx.sST.h, 16)
        cx.sST.bump(16)
        cx.w(VE, cx.sRS)
        VE.reduce_max(P["col"][:, 0:1], P["xres"][:, bb, :], axis=AX.X,
                      apply_absolute_value=True)
        VE.tensor_max(P["runmax"][:, 0:1], P["runmax"][:, 0:1],
                      P["col"][:, 0:1]).then_inc(cx.sVE.h, 1)
        cx.sVE.bump()
    emit_colmax_scalar(cx, P, bhbm, P["runmax"][:, 0:1], P["srow"][0:1, 0:1])
    cx.w(SY, cx.sVE)
    SY.dma_start(out=m1_out, in_=P["srow"][0:1, 0:1]).then_inc(cx.sST.h, 16)
    cx.sST.bump(16)
    cx.w(GP, cx.sST)


def build_pn(stage):
    nc = bass.Bass("TRN2", target_bir_lowering=False)
    dt = nc.dram_tensor
    inp = {}

    def I(name, shape, dtyp=F32):
        inp[name] = dt(name, shape, dtyp, kind="ExternalInput").ap()
        return inp[name]

    def O(name, shape, dtyp=F32):
        return dt(name, shape, dtyp, kind="ExternalOutput").ap()

    SPW = NH * T
    if stage == 2:
        x1 = I("x1", [T, C])
        wgate = I("wgate", [FH, C])
        wup = I("wup", [FH, C])
        M1 = I("M1", [1, 8])
        mo = O("m_out", [1, 2])
        gate_sp = O("gate_sp", [128, SPW])
        up_sp = O("up_sp", [128, SPW])
    elif stage == 3:
        gate_spi = I("gate_sp", [128, SPW])
        M2 = I("M2", [2, 8])
        mo = O("m_out", [1, 1])
        gq_sp = O("gq_sp", [128, SPW])
        squ_sp = O("squ_sp", [128, SPW])
    elif stage == 4:
        up_spi = I("up_sp", [128, SPW])
        gq_spi = I("gq_sp", [128, SPW])
        squ_spi = I("squ_sp", [128, SPW])
        M2 = I("M2", [2, 8])
        M3 = I("M3", [1, 8])
        mo = O("m_out", [1, 1])
        oraw_sp = O("oraw_sp", [128, SPW])
    else:
        x1 = I("x1", [T, C])
        wdown = I("wdown", [C, FH])
        oraw_spi = I("oraw_sp", [128, SPW])
        M2 = I("M2", [2, 8])
        M3 = I("M3", [1, 8])
        M4 = I("M4", [1, 8])
        y_out = O("y_out", [T, C])
    I("ones_row", [1, 128], BF16)
    I("ones_f32", [1, 128])
    bhbm = dt("bhbm", [128, 1], F32).ap()

    with ExitStack() as perm:
        E = perm.enter_context
        st = nc.sbuf_tensor
        cx = Cx(nc, E)
        SY, GP, VE, SC, TE = cx.SY, cx.GP, cx.VE, cx.SC, cx.TE
        P = alloc_perm(cx, E, with_xT=False)
        pA, pB = P["pA"], P["pB"]
        srow = P["srow"]
        scol = E(st("scol", [128, 16], F32))
        fscr = E(st("fscr", [128, T], F32))
        fscr2 = E(st("fscr2", [128, T], F32))
        iscr = E(st("iscr", [128, T], I32))
        c32k = E(st("c32k", [128, 1], F32))
        Ms = E(st("Ms", [1, 48], F32))

        if stage == 2:
            SY.dma_start(out=Ms[0:1, 0:8], in_=M1).then_inc(cx.sLD.h, 16)
            cx.sLD.bump(16)
        if stage >= 3:
            SY.dma_start(out=Ms[0:1, 8:24],
                         in_=M2.rearrange("b c -> (b c)").unsqueeze(0)).then_inc(
                cx.sLD.h, 16); cx.sLD.bump(16)
        if stage >= 4:
            SY.dma_start(out=Ms[0:1, 24:32], in_=M3).then_inc(cx.sLD.h, 16)
            cx.sLD.bump(16)
        if stage >= 5:
            SY.dma_start(out=Ms[0:1, 32:40], in_=M4).then_inc(cx.sLD.h, 16)
            cx.sLD.bump(16)
        load_base_consts(cx, P, inp)
        VE.memset(srow[0:1, :], 0.0)
        VE.memset(c32k[:], 32768.0)
        cx.w(VE, cx.sLD)

        def mk_scale(ds, dr, src_row):
            VE.reduce_max(srow[0:1, 40:41], src_row, axis=AX.X)
            VE.tensor_scalar(srow[0:1, ds:ds + 1], srow[0:1, 40:41],
                             1.0 / QA, 1e-12, ALU.mult, ALU.max)
            VE.reciprocal(srow[0:1, dr:dr + 1], srow[0:1, ds:ds + 1])
            cx.vfence()

        if stage == 2:
            mk_scale(0, 1, Ms[0:1, 0:8])
            VE.tensor_scalar(srow[0:1, 14:15], srow[0:1, 0:1], -32768.0,
                             None, ALU.mult)
        if stage >= 3:
            mk_scale(2, 3, Ms[0:1, 8:16])
            mk_scale(4, 5, Ms[0:1, 16:24])
            VE.tensor_scalar(srow[0:1, 13:14], srow[0:1, 2:3], -32768.0,
                             None, ALU.mult)
        if stage >= 4:
            VE.reduce_max(srow[0:1, 40:41], Ms[0:1, 24:32], axis=AX.X)
            VE.tensor_tensor(srow[0:1, 41:42], srow[0:1, 40:41],
                             srow[0:1, 2:3], ALU.mult)
            VE.tensor_scalar_mul(srow[0:1, 41:42], srow[0:1, 41:42],
                                 1.0 / 65536.0)
            VE.tensor_scalar(srow[0:1, 6:7], srow[0:1, 41:42],
                             1.0 / QA, 1e-12, ALU.mult, ALU.max)
            VE.reciprocal(srow[0:1, 7:8], srow[0:1, 6:7])
            cx.vfence()
            VE.tensor_tensor(srow[0:1, 10:11], srow[0:1, 2:3],
                             srow[0:1, 7:8], ALU.mult)
            VE.tensor_scalar_mul(srow[0:1, 10:11], srow[0:1, 10:11],
                                 1.0 / 65536.0)
        if stage >= 5:
            VE.reduce_max(srow[0:1, 40:41], Ms[0:1, 32:40], axis=AX.X)
            VE.tensor_tensor(srow[0:1, 41:42], srow[0:1, 40:41],
                             srow[0:1, 6:7], ALU.mult)
            VE.tensor_tensor(srow[0:1, 41:42], srow[0:1, 41:42],
                             srow[0:1, 4:5], ALU.mult)
            VE.tensor_scalar(srow[0:1, 8:9], srow[0:1, 41:42],
                             1.0 / QA, 1e-12, ALU.mult, ALU.max)
            VE.reciprocal(srow[0:1, 9:10], srow[0:1, 8:9])
            cx.vfence()
            VE.tensor_tensor(srow[0:1, 11:12], srow[0:1, 6:7],
                             srow[0:1, 4:5], ALU.mult)
            VE.tensor_tensor(srow[0:1, 11:12], srow[0:1, 11:12],
                             srow[0:1, 9:10], ALU.mult)
            VE.tensor_scalar(srow[0:1, 12:13], srow[0:1, 8:9], -32768.0,
                             None, ALU.mult)
        VE.engine_nop().then_inc(cx.sVE.h, 1); cx.sVE.bump()
        emit_bcast_cols(cx, P, srow[0:1, 0:15], scol[:, 0:15], 15)

        def act_round(src_ap, scale_col, dst_ap):
            cx.w(SC, cx.sVE)
            cx.w(SC, cx.sSC)
            SC.activation(fscr[:, :], src_ap, AF.Identity,
                          scale=scale_col,
                          bias=c32k[:, 0:1]).then_inc(cx.sSC.h, 1)
            cx.sSC.bump()
            cx.w(SC, cx.sSC)
            SC.activation(iscr[:, :], fscr[:, :],
                          AF.Identity).then_inc(cx.sSC.h, 1)
            cx.sSC.bump()
            cx.w(VE, cx.sSC)
            VE.tensor_tensor(fscr2[:, :], fscr[:, :], iscr[:, :],
                             ALU.subtract)
            VE.scalar_tensor_tensor(dst_ap, fscr2[:, :], 0.5, iscr[:, :],
                                    ALU.is_ge, ALU.add)

        if stage == 2:
            with ExitStack() as ph:
                Ep = ph.enter_context
                P["xT"] = Ep(st("xT", [128, ND, T], BF16))
                wstg = Ep(st("wstg", [128, 2, 2, C], F32))
                wqd = Ep(st("wqd", [128, 2, 2, C], BF16))
                wTr = Ep(st("wTr", [128, 2, 2, ND, 128], BF16))
                sblk = Ep(st("sblk", [128, 32], F32))
                rblk = Ep(st("rblk", [128, 32], F32))
                gsp = Ep(st("gsp", [128, 2, T], F32))
                usp = Ep(st("usp", [128, 2, T], F32))

                rs_mark, sc_z = {}, {}
                cx.w(SY, cx.sVE)
                SY.dma_start(out=P["xres"][:, 0, :],
                             in_=tiled(x1)[:, 0, :]).then_inc(cx.sRS.h, 16)
                cx.sRS.bump(16)
                rs_mark[0] = cx.sRS.v
                for tt in range(NT):
                    bb = tt % 2
                    if tt + 1 < NT:
                        if tt >= 1:
                            SY.wait_ge(cx.sSC.h, sc_z[tt - 1])
                        SY.wait_ge(cx.sRS.h, cx.sRS.v)
                        SY.dma_start(out=P["xres"][:, 1 - bb, :],
                                     in_=tiled(x1)[:, tt + 1, :]).then_inc(
                            cx.sRS.h, 16)
                        cx.sRS.bump(16)
                        rs_mark[tt + 1] = cx.sRS.v
                    cx.w(SC, cx.sRS, rs_mark[tt])
                    cx.w(SC, cx.sVE)
                    SC.activation(fscr[:, 0:C], P["xres"][:, bb, :],
                                  AF.Identity, scale=scol[:, 1:2],
                                  bias=c32k[:, 0:1]).then_inc(cx.sSC.h, 1)
                    cx.sSC.bump()
                    sc_z[tt] = cx.sSC.v
                    cx.w(SC, cx.sSC)
                    SC.activation(iscr[:, 0:C], fscr[:, 0:C],
                                  AF.Identity).then_inc(cx.sSC.h, 1)
                    cx.sSC.bump()
                    cx.w(VE, cx.sSC)
                    VE.tensor_tensor(fscr2[:, 0:C], fscr[:, 0:C],
                                     iscr[:, 0:C], ALU.subtract)
                    VE.scalar_tensor_tensor(fscr[:, 0:C], fscr2[:, 0:C], 0.5,
                                            iscr[:, 0:C], ALU.is_ge, ALU.add)
                    VE.tensor_scalar(P["xln"][:, bb, :], fscr[:, 0:C],
                                     scol[:, 0:1], scol[:, 14:15],
                                     ALU.mult, ALU.add)
                    VE.engine_nop().then_inc(cx.sVE.h, 1); cx.sVE.bump()
                    emit_xT_tile(cx, P, P["xln"][:, bb, :], tt)
                xq_xp = cx.sXP.v

                ve_chunk, te_mark, xpm, sp_mark, sc_mark = {}, {}, {}, {}, {}
                spilled = set()

                def emit_spill(j, force=False):
                    j = j if force else j - 1
                    if j < 0 or j in spilled or j not in sc_mark:
                        return
                    spilled.add(j)
                    jb = j % 2
                    cx.w(SY, cx.sSC, sc_mark[j])
                    SY.wait_ge(cx.sST.h, cx.sST.v)
                    SY.dma_start(out=gate_sp[:, T * j:T * (j + 1)],
                                 in_=gsp[:, jb, :]).then_inc(cx.sST.h, 16)
                    cx.sST.bump(16)
                    SY.dma_start(out=up_sp[:, T * j:T * (j + 1)],
                                 in_=usp[:, jb, :]).then_inc(cx.sST.h, 16)
                    cx.sST.bump(16)
                    sp_mark[j] = cx.sST.v

                for hh in range(NH):
                    bb = hh % 2
                    for wi in range(2):
                        wsrc_d = wgate if wi == 0 else wup
                        if hh >= 2 and wi == 0:
                            GP.wait_ge(cx.sVE.h, ve_chunk[hh - 2])
                        GP.wait_ge(cx.sGD.h, cx.sGD.v)
                        GP.dma_start(
                            out=wstg[:, bb, wi, :],
                            in_=wsrc_d[128 * hh:128 * (hh + 1),
                                       :]).then_inc(cx.sGD.h, 16)
                        cx.sGD.bump(16)
                        gd_here = cx.sGD.v
                        cx.w(VE, cx.sGD, gd_here)
                        if (hh - 2, wi) in xpm:
                            VE.wait_ge(cx.sXP.h, xpm[(hh - 2, wi)])
                        w3 = wstg[:, bb, wi, :].rearrange(
                            "p (b i) -> p b i", i=32)
                        fv = fscr[:, 0:C].rearrange("p (b i) -> p b i", i=32)
                        VE.reduce_max(sblk[:, :], w3, axis=AX.X,
                                      apply_absolute_value=True)
                        VE.tensor_scalar(sblk[:, :], sblk[:, :], 1.0 / 7.0,
                                         1e-12, ALU.mult, ALU.max)
                        VE.reciprocal(rblk[:, :], sblk[:, :])
                        cx.vfence()
                        VE.tensor_tensor(
                            fv, w3,
                            rblk[:, :].unsqueeze(2).broadcast_to(
                                [128, 32, 32]), ALU.mult)
                        VE.tensor_scalar(fscr[:, 0:C], fscr[:, 0:C], 16.0,
                                         None, ALU.add)
                        VE.engine_nop().then_inc(cx.sVE.h, 1); cx.sVE.bump()
                        cx.w(SC, cx.sVE)
                        SC.activation(iscr[:, 0:C], fscr[:, 0:C],
                                      AF.Identity).then_inc(cx.sSC.h, 1)
                        cx.sSC.bump()
                        cx.w(VE, cx.sSC)
                        VE.tensor_tensor(fscr2[:, 0:C], fscr[:, 0:C],
                                         iscr[:, 0:C], ALU.subtract)
                        VE.scalar_tensor_tensor(fscr[:, 0:C], fscr2[:, 0:C],
                                                0.5, iscr[:, 0:C],
                                                ALU.is_ge, ALU.add)
                        VE.tensor_scalar(fscr[:, 0:C], fscr[:, 0:C], -16.0,
                                         None, ALU.add)
                        VE.tensor_tensor(
                            wqd[:, bb, wi, :].rearrange(
                                "p (b i) -> p b i", i=32),
                            fv,
                            sblk[:, :].unsqueeze(2).broadcast_to(
                                [128, 32, 32]), ALU.mult)
                        VE.engine_nop().then_inc(cx.sVE.h, 1); cx.sVE.bump()
                        cx.w(SY, cx.sVE)
                        cx.w(SY, cx.sGD)
                        if hh >= 2:
                            SY.wait_ge(cx.sTE.h, te_mark[hh - 2])
                        SY.wait_ge(cx.sXP.h, cx.sXP.v)
                        for d in range(ND):
                            SY.dma_start_transpose(
                                wTr[:, bb, wi, d, :],
                                wqd[:, bb, wi,
                                    128 * d:128 * (d + 1)]).then_inc(
                                cx.sXP.h, 16)
                            cx.sXP.bump(16)
                        xpm[(hh, wi)] = cx.sXP.v
                    emit_spill(hh)
                    cx.gTE()
                    cx.w(TE, cx.sXP, xpm[(hh, 1)])
                    if hh == 0:
                        cx.w(TE, cx.sXP, xq_xp)
                    for wi, p in ((0, pA), (1, pB)):
                        for b in range(4):
                            for d in range(ND):
                                t = TE.matmul(
                                    p[:, 512 * b:512 * (b + 1)],
                                    wTr[:, bb, wi, d, :],
                                    P["xT"][:, d, 512 * b:512 * (b + 1)],
                                    start=(d == 0), stop=(d == ND - 1))
                        t.then_inc(cx.sTE.h, 1); cx.sTE.bump()
                    te_here = cx.sTE.v
                    te_mark[hh] = te_here

                    cx.w(VE, cx.sTE, te_here)
                    VE.reduce_max(P["col"][:, 0:1], pA[:, :], axis=AX.X,
                                  apply_absolute_value=True)
                    VE.tensor_max(P["runmax"][:, 0:1], P["runmax"][:, 0:1],
                                  P["col"][:, 0:1])
                    VE.reduce_max(P["col"][:, 1:2], pB[:, :], axis=AX.X,
                                  apply_absolute_value=True)
                    VE.tensor_max(P["runmax"][:, 1:2], P["runmax"][:, 1:2],
                                  P["col"][:, 1:2]).then_inc(cx.sVE.h, 1)
                    cx.sVE.bump()
                    ve_chunk[hh] = cx.sVE.v
                    cx.w(SC, cx.sTE, te_here)
                    cx.w(SC, cx.sVE)
                    if hh >= 2:
                        SC.wait_ge(cx.sST.h, sp_mark[hh - 2])
                    SC.activation(gsp[:, bb, :], pA[:, :], AF.Identity)
                    SC.activation(usp[:, bb, :], pB[:, :],
                                  AF.Identity).then_inc(cx.sSC.h, 1)
                    cx.sSC.bump()
                    sc_mark[hh] = cx.sSC.v
                emit_spill(NH - 1, force=True)

            for j in range(2):
                emit_colmax_scalar(cx, P, bhbm, P["runmax"][:, j:j + 1],
                                   srow[0:1, 48 + j:49 + j])
            cx.w(SY, cx.sVE)
            SY.dma_start(out=mo, in_=srow[0:1, 48:50]).then_inc(cx.sLD.h, 16)
            cx.sLD.bump(16)
            cx.w(GP, cx.sST)
            cx.w(GP, cx.sLD)

        elif stage == 3:
            gbuf = E(st("gbuf", [128, 2, T], F32))
            f3buf = E(st("f3buf", [128, 2, T], F32))
            sqbuf = E(st("sqbuf", [128, 2, T], F32))
            ve_mark, sp_mark = {}, {}
            for hh in range(NH):
                bb = hh % 2
                if hh >= 2:
                    GP.wait_ge(cx.sVE.h, ve_mark[hh - 2])
                GP.wait_ge(cx.sGD.h, cx.sGD.v)
                GP.dma_start(out=gbuf[:, bb, :],
                             in_=gate_spi[:, T * hh:T * (hh + 1)]).then_inc(
                    cx.sGD.h, 16)
                cx.sGD.bump(16)
                gd_here = cx.sGD.v
                cx.w(VE, cx.sGD, gd_here)
                if hh >= 2:
                    VE.wait_ge(cx.sST.h, sp_mark[hh - 2])
                VE.engine_nop().then_inc(cx.sVE.h, 1); cx.sVE.bump()
                act_round(gbuf[:, bb, :], scol[:, 3:4], f3buf[:, bb, :])
                VE.engine_nop().then_inc(cx.sVE.h, 1); cx.sVE.bump()
                cx.w(SC, cx.sVE)
                SC.activation(fscr[:, :], f3buf[:, bb, :], AF.Sigmoid,
                              scale=scol[:, 2:3],
                              bias=scol[:, 13:14]).then_inc(cx.sSC.h, 1)
                cx.sSC.bump()
                cx.w(VE, cx.sSC)
                VE.engine_nop().then_inc(cx.sVE.h, 1); cx.sVE.bump()
                act_round(fscr[:, :], 65536.0, sqbuf[:, bb, :])
                VE.tensor_scalar(sqbuf[:, bb, :], sqbuf[:, bb, :], -32768.0,
                                 65535.0, ALU.add, ALU.min)
                VE.scalar_tensor_tensor(fscr2[:, :], f3buf[:, bb, :],
                                        -32768.0, sqbuf[:, bb, :],
                                        ALU.add, ALU.mult)
                VE.reduce_max(P["col"][:, 0:1], fscr2[:, :], axis=AX.X,
                              apply_absolute_value=True)
                VE.tensor_max(P["runmax"][:, 0:1], P["runmax"][:, 0:1],
                              P["col"][:, 0:1]).then_inc(cx.sVE.h, 1)
                cx.sVE.bump()
                ve_mark[hh] = cx.sVE.v
                cx.w(SY, cx.sVE)
                SY.wait_ge(cx.sST.h, cx.sST.v)
                SY.dma_start(out=gq_sp[:, T * hh:T * (hh + 1)],
                             in_=f3buf[:, bb, :]).then_inc(cx.sST.h, 16)
                cx.sST.bump(16)
                SY.dma_start(out=squ_sp[:, T * hh:T * (hh + 1)],
                             in_=sqbuf[:, bb, :]).then_inc(cx.sST.h, 16)
                cx.sST.bump(16)
                sp_mark[hh] = cx.sST.v
            emit_colmax_scalar(cx, P, bhbm, P["runmax"][:, 0:1],
                               srow[0:1, 48:49])
            cx.w(SY, cx.sVE)
            SY.dma_start(out=mo, in_=srow[0:1, 48:49]).then_inc(cx.sST.h, 16)
            cx.sST.bump(16)
            cx.w(GP, cx.sST)

        elif stage == 4:
            f3buf = E(st("f3buf", [128, 2, T], F32))
            sqbuf = E(st("sqbuf", [128, 2, T], F32))
            ubuf = E(st("ubuf", [128, 2, T], F32))
            obuf = E(st("obuf", [128, 2, T], F32))
            ve_mark, sp_mark = {}, {}
            for hh in range(NH):
                bb = hh % 2
                if hh >= 2:
                    GP.wait_ge(cx.sVE.h, ve_mark[hh - 2])
                GP.wait_ge(cx.sGD.h, cx.sGD.v)
                GP.dma_start(out=f3buf[:, bb, :],
                             in_=gq_spi[:, T * hh:T * (hh + 1)]).then_inc(
                    cx.sGD.h, 16)
                cx.sGD.bump(16)
                GP.dma_start(out=sqbuf[:, bb, :],
                             in_=squ_spi[:, T * hh:T * (hh + 1)]).then_inc(
                    cx.sGD.h, 16)
                cx.sGD.bump(16)
                GP.dma_start(out=ubuf[:, bb, :],
                             in_=up_spi[:, T * hh:T * (hh + 1)]).then_inc(
                    cx.sGD.h, 16)
                cx.sGD.bump(16)
                gd_here = cx.sGD.v
                cx.w(VE, cx.sGD, gd_here)
                VE.scalar_tensor_tensor(fscr2[:, :], f3buf[:, bb, :],
                                        -32768.0, sqbuf[:, bb, :],
                                        ALU.add, ALU.mult)
                VE.engine_nop().then_inc(cx.sVE.h, 1); cx.sVE.bump()
                act_round(fscr2[:, :], scol[:, 10:11], f3buf[:, bb, :])
                VE.engine_nop().then_inc(cx.sVE.h, 1); cx.sVE.bump()
                act_round(ubuf[:, bb, :], scol[:, 5:6], sqbuf[:, bb, :])
                VE.tensor_scalar(sqbuf[:, bb, :], sqbuf[:, bb, :], -32768.0,
                                 None, ALU.add)
                if hh >= 2:
                    VE.wait_ge(cx.sST.h, sp_mark[hh - 2])
                VE.scalar_tensor_tensor(obuf[:, bb, :], f3buf[:, bb, :],
                                        -32768.0, sqbuf[:, bb, :],
                                        ALU.add, ALU.mult)
                VE.reduce_max(P["col"][:, 0:1], obuf[:, bb, :], axis=AX.X,
                              apply_absolute_value=True)
                VE.tensor_max(P["runmax"][:, 0:1], P["runmax"][:, 0:1],
                              P["col"][:, 0:1]).then_inc(cx.sVE.h, 1)
                cx.sVE.bump()
                ve_mark[hh] = cx.sVE.v
                cx.w(SY, cx.sVE)
                SY.wait_ge(cx.sST.h, cx.sST.v)
                SY.dma_start(out=oraw_sp[:, T * hh:T * (hh + 1)],
                             in_=obuf[:, bb, :]).then_inc(cx.sST.h, 16)
                cx.sST.bump(16)
                sp_mark[hh] = cx.sST.v
            emit_colmax_scalar(cx, P, bhbm, P["runmax"][:, 0:1],
                               srow[0:1, 48:49])
            cx.w(SY, cx.sVE)
            SY.dma_start(out=mo, in_=srow[0:1, 48:49]).then_inc(cx.sST.h, 16)
            cx.sST.bump(16)
            cx.w(GP, cx.sST)

        else:
            oq = E(st("oq", [128, NH, T], BF16))
            with ExitStack() as p1s:
                obuf = p1s.enter_context(st("obuf", [128, 1, T], F32))
                sc_mark = {}
                for hh in range(NH):
                    if hh >= 1:
                        GP.wait_ge(cx.sSC.h, sc_mark[hh - 1])
                    GP.wait_ge(cx.sGD.h, cx.sGD.v)
                    GP.dma_start(
                        out=obuf[:, 0, :],
                        in_=oraw_spi[:, T * hh:T * (hh + 1)]).then_inc(
                        cx.sGD.h, 16)
                    cx.sGD.bump(16)
                    gd_here = cx.sGD.v
                    cx.w(SC, cx.sGD, gd_here)
                    act_round(obuf[:, 0, :], scol[:, 11:12], fscr2[:, :])
                    sc_mark[hh] = cx.sSC.v
                    VE.tensor_scalar(oq[:, hh, :], fscr2[:, :],
                                     scol[:, 8:9], scol[:, 12:13],
                                     ALU.mult, ALU.add).then_inc(cx.sVE.h, 1)
                    cx.sVE.bump()

            with ExitStack() as ph:
                Ep = ph.enter_context
                wstg2 = Ep(st("wstg2", [128, 1536], F32))
                wqd2 = Ep(st("wqd2", [128, FH], BF16))
                sblk2 = Ep(st("sblk2", [128, 48], F32))
                rblk2 = Ep(st("rblk2", [128, 48], F32))
                wdT = Ep(st("wdT", [128, NH, C], BF16))
                cx.w(GP, cx.sTE); cx.w(GP, cx.sSC); cx.w(GP, cx.sVE)
                ve_dd, xp_marks = {}, {}
                for dd in range(ND):
                    db = dd % 2
                    if dd >= 1:
                        GP.wait_ge(cx.sXP.h, xp_marks[dd - 1])
                    for hf in range(2):
                        if dd + hf >= 1:
                            GP.wait_ge(cx.sVE.h, cx.sVE.v)
                        GP.wait_ge(cx.sGD.h, cx.sGD.v)
                        GP.dma_start(
                            out=wstg2[:, :],
                            in_=wdown[128 * dd:128 * (dd + 1),
                                      1536 * hf:1536 * (hf + 1)]).then_inc(
                            cx.sGD.h, 16)
                        cx.sGD.bump(16)
                        gd2 = cx.sGD.v
                        cx.w(VE, cx.sGD, gd2)
                        if dd >= 1 and hf == 0:
                            VE.wait_ge(cx.sXP.h, xp_marks[dd - 1])
                        w3 = wstg2[:, :].rearrange("p (b i) -> p b i",
                                                   i=32)
                        c0 = 1536 * hf
                        VE.reduce_max(sblk2[:, :], w3, axis=AX.X,
                                      apply_absolute_value=True)
                        VE.tensor_scalar(sblk2[:, :], sblk2[:, :], 1.0 / 7.0,
                                         1e-12, ALU.mult, ALU.max)
                        VE.reciprocal(rblk2[:, :], sblk2[:, :])
                        cx.vfence()
                        VE.tensor_tensor(
                            fscr[:, 0:1536].rearrange(
                                "p (b i) -> p b i", i=32),
                            w3,
                            rblk2[:, :].unsqueeze(2).broadcast_to(
                                [128, 48, 32]),
                            ALU.mult)
                        VE.tensor_scalar(fscr[:, 0:1536], fscr[:, 0:1536],
                                         16.0, None, ALU.add)
                        VE.engine_nop().then_inc(cx.sVE.h, 1); cx.sVE.bump()
                        cx.w(SC, cx.sVE)
                        SC.activation(iscr[:, 0:1536], fscr[:, 0:1536],
                                      AF.Identity).then_inc(cx.sSC.h, 1)
                        cx.sSC.bump()
                        cx.w(VE, cx.sSC)
                        VE.tensor_tensor(fscr2[:, 0:1536], fscr[:, 0:1536],
                                         iscr[:, 0:1536], ALU.subtract)
                        VE.scalar_tensor_tensor(fscr[:, 0:1536],
                                                fscr2[:, 0:1536], 0.5,
                                                iscr[:, 0:1536], ALU.is_ge,
                                                ALU.add)
                        VE.tensor_scalar(fscr[:, 0:1536], fscr[:, 0:1536],
                                         -16.0, None, ALU.add)
                        VE.tensor_tensor(
                            wqd2[:, c0:c0 + 1536].rearrange(
                                "p (b i) -> p b i", i=32),
                            fscr[:, 0:1536].rearrange(
                                "p (b i) -> p b i", i=32),
                            sblk2[:, :].unsqueeze(2).broadcast_to(
                                [128, 48, 32]),
                            ALU.mult)
                        VE.engine_nop().then_inc(cx.sVE.h, 1); cx.sVE.bump()
                    ve_dd[dd] = cx.sVE.v
                    cx.w(SY, cx.sVE, ve_dd[dd])
                    cx.w(SY, cx.sGD)
                    SY.wait_ge(cx.sXP.h, cx.sXP.v)
                    for hh in range(NH):
                        SY.dma_start_transpose(
                            wdT[:, hh, 128 * dd:128 * (dd + 1)],
                            wqd2[:, 128 * hh:128 * (hh + 1)]).then_inc(
                            cx.sXP.h, 16)
                        cx.sXP.bump(16)
                    xp_marks[dd] = cx.sXP.v
                wd_xp = cx.sXP.v

                for tt in range(NT):
                    bb = tt % 2
                    cx.w(SY, cx.sVE)
                    SY.dma_start(out=P["xres"][:, bb, :],
                                 in_=tiled(x1)[:, tt, :]).then_inc(
                        cx.sRS.h, 16); cx.sRS.bump(16)
                    cx.gTE()
                    cx.w(TE, cx.sRS)
                    if tt == 0:
                        cx.w(TE, cx.sXP, wd_xp)
                    for hh in range(NH):
                        for b in range(2):
                            t = TE.matmul(
                                pB[:, 512 * b:512 * (b + 1)],
                                oq[:, hh, 128 * tt:128 * (tt + 1)],
                                wdT[:, hh, 512 * b:512 * (b + 1)],
                                start=(hh == 0), stop=(hh == NH - 1),
                                skip_group_check=True)
                    t.then_inc(cx.sTE.h, 1); cx.sTE.bump()
                    emit_ln_tile(cx, P, pB[:, 0:C], P["xres"][:, bb, :],
                                 P["xln"][:, bb, :],
                                 out_wait=[(cx.sST, cx.sST.v)])
                    cx.w(SY, cx.sSC)
                    SY.dma_start(out=tiled(y_out)[:, tt, :],
                                 in_=P["xln"][:, bb, :]).then_inc(
                        cx.sST.h, 16); cx.sST.bump(16)
            cx.w(GP, cx.sST)

        blk = perm.enter_context(nc.Block())
        cx.rec.replay(blk)
        return nc, inp


_cache = {}


def _get(key, builder):
    if key not in _cache:
        _cache[key] = builder()
    return _cache[key]


def kernel(**inputs):
    cst = host_constants()

    def f32(x):
        return np.ascontiguousarray(np.asarray(x), dtype=None)

    cattn_w = np.ascontiguousarray(inputs["L0_attn_c_attn_w"])
    cattn_bp = np.ascontiguousarray(
        np.asarray(inputs["L0_attn_c_attn_b"]).reshape(24, 128).T)
    cproj_w = np.ascontiguousarray(inputs["L0_attn_c_proj_w"])
    cproj_b = np.ascontiguousarray(
        np.asarray(inputs["L0_attn_c_proj_b"]).reshape(1, C))
    wk_c = np.ascontiguousarray(inputs["L0_attn_wk_c"])
    wk_eT = np.ascontiguousarray(np.asarray(inputs["L0_attn_wk_e"]).T)
    wv_c = np.ascontiguousarray(inputs["L0_attn_wv_c"])
    wv_eT = np.ascontiguousarray(np.asarray(inputs["L0_attn_wv_e"]).T)
    wg0T = np.ascontiguousarray(np.asarray(inputs["L0_ffn_w_gate"]).T)
    wu0T = np.ascontiguousarray(np.asarray(inputs["L0_ffn_w_up"]).T)
    wd0T = np.ascontiguousarray(np.asarray(inputs["L0_ffn_w_down"]).T)
    wqT = np.ascontiguousarray(np.asarray(inputs["L1_attn_Wq"]).T)
    wkT = np.ascontiguousarray(np.asarray(inputs["L1_attn_Wk"]).T)
    wvT = np.ascontiguousarray(np.asarray(inputs["L1_attn_Wv"]).T)
    woT = np.ascontiguousarray(np.asarray(inputs["L1_attn_Wo"]).T)
    bq = np.ascontiguousarray(np.asarray(inputs["L1_attn_bq"]).reshape(1, C))
    bk = np.ascontiguousarray(np.asarray(inputs["L1_attn_bk"]).reshape(1, C))
    bv = np.ascontiguousarray(np.asarray(inputs["L1_attn_bv"]).reshape(1, C))
    bo = np.ascontiguousarray(np.asarray(inputs["L1_attn_bo"]).reshape(1, C))
    wgate1 = np.ascontiguousarray(inputs["L1_ffn_w_gate"])
    wup1 = np.ascontiguousarray(inputs["L1_ffn_w_up"])
    wdown1 = np.ascontiguousarray(inputs["L1_ffn_w_down"])
    x_all = np.ascontiguousarray(inputs["L0_attn_hidden_states"])

    nc1, _ = _get("p1", lambda: build_p1())
    base1 = dict(cattn_w=cattn_w, cattn_bp=cattn_bp, cproj_w=cproj_w,
                 cproj_b=cproj_b, wk_c=wk_c, wk_eT=wk_eT, wv_c=wv_c,
                 wv_eT=wv_eT, wg0T=wg0T, wu0T=wu0T, wd0T=wd0T,
                 wqT=wqT, wkT=wkT, wvT=wvT, woT=woT,
                 bq=bq, bk=bk, bv=bv, bo=bo,
                 ones_row=cst["ones_row"], ones_f32=cst["ones_f32"],
                 diagmask=cst["diagmask"], idn=cst["idn"],
                 madd=cst["madd"], ropecos=cst["ropecos"],
                 ropesin=cst["ropesin"])
    in_maps = [dict(base1, x0=np.ascontiguousarray(x_all[c]))
               for c in range(B)]
    r1 = run_bass_kernel_spmd(nc1, in_maps, core_ids=list(range(B)))
    x1 = [r1.results[c]["x1_out"] for c in range(B)]
    M1 = np.ascontiguousarray(
        np.stack([r1.results[c]["m1_out"][0, 0] for c in range(B)])[None, :])

    def run_stage(stage, extra, percore):
        nc, _ = _get(f"p{stage}", lambda: build_pn(stage))
        base = dict(ones_row=cst["ones_row"], ones_f32=cst["ones_f32"])
        if stage == 2:
            base.update(wgate=wgate1, wup=wup1, M1=M1)
        if stage == 5:
            base["wdown"] = wdown1
        base.update(extra)
        maps = [dict(base, **{k: v[c] for k, v in percore.items()})
                for c in range(B)]
        return run_bass_kernel_spmd(nc, maps, core_ids=list(range(B)))

    r2 = run_stage(2, {}, {"x1": x1})
    M2 = np.ascontiguousarray(
        np.stack([r2.results[c]["m_out"][0] for c in range(B)], axis=1))
    M2 = M2[[0, 1], :]
    gate_sp = [r2.results[c]["gate_sp"] for c in range(B)]
    up_sp = [r2.results[c]["up_sp"] for c in range(B)]
    r3 = run_stage(3, {"M2": M2}, {"gate_sp": gate_sp})
    M3 = np.ascontiguousarray(
        np.stack([r3.results[c]["m_out"][0, 0] for c in range(B)])[None, :])
    gq_sp = [r3.results[c]["gq_sp"] for c in range(B)]
    squ_sp = [r3.results[c]["squ_sp"] for c in range(B)]
    r4 = run_stage(4, {"M2": M2, "M3": M3},
                   {"up_sp": up_sp, "gq_sp": gq_sp, "squ_sp": squ_sp})
    M4 = np.ascontiguousarray(
        np.stack([r4.results[c]["m_out"][0, 0] for c in range(B)])[None, :])
    oraw_sp = [r4.results[c]["oraw_sp"] for c in range(B)]
    r5 = run_stage(5, {"M2": M2, "M3": M3, "M4": M4},
                   {"x1": x1, "oraw_sp": oraw_sp})
    y = np.stack([r5.results[c]["y_out"] for c in range(B)], axis=0)
    return y.astype(np.float32)
